# version 06 (final)
# baseline (speedup 1.0000x reference)
import contextlib
import numpy as np
import concourse.bass as bass
import concourse.mybir as mybir
from concourse.bass_utils import run_bass_kernel_spmd

F32 = mybir.dt.float32
BF16 = mybir.dt.bfloat16
AF = mybir.ActivationFunctionType
ALU = mybir.AluOpType
AX = mybir.AxisListType

T = 4096
D = 1024
NT = T // 128
NW = T // 512
NEG = -3.0e38
EM8 = float(np.exp(-8.0))

PHASES = ["l0a", "l0f0", "l0b", "l0f1", "l1a", "l1f0", "l1b", "l1f1"]
ENABLED = list(PHASES)

class Buf:
    __slots__ = ("h", "w", "rs", "dsem")
    def __init__(self, h):
        self.h = h
        self.w = None
        self.rs = []
        self.dsem = {}

SERIAL = False
NPOOL = 40

class Prog:
    ENGS = ("T", "V", "S", "Y", "G")
    def __init__(self, nc, stack):
        self.nc = nc
        self.q = {e: [] for e in self.ENGS}
        self.cnt = {e: 0 for e in self.ENGS}
        self.sem = {e: stack.enter_context(nc.semaphore("sem_" + e)) for e in self.ENGS}
        self.waited = {e: {} for e in self.ENGS}
        self.dpool = {q: [stack.enter_context(nc.semaphore(f"dp{q}{i}")) for i in range(NPOOL)]
                      for q in ("Y", "G")}
        self.dcnt = {q: [0] * NPOOL for q in ("Y", "G")}
        self.dnext = {"Y": 0, "G": 0}

    def _wait(self, e, tok):
        if tok[0] == "D":
            _, q, idx, v = tok
            key = ("D", q, idx)
            if self.waited[e].get(key, -1) >= v:
                return
            self.waited[e][key] = v
            sem = self.dpool[q][idx]
            self.q[e].append(lambda eng, sem=sem, v=v: eng.wait_ge(sem, v))
            return
        se, v = tok
        if self.waited[e].get(se, -1) >= v:
            return
        self.waited[e][se] = v
        sem = self.sem[se]
        self.q[e].append(lambda eng, sem=sem, v=v: eng.wait_ge(sem, v))

    def group(self, e):
        assert e in ("T", "V", "S", "G")
        return Group(self, e)

    def dma(self, e, dst, srcs, fns):
        assert e in ("Y", "G")
        if e not in dst.dsem:
            dst.dsem[e] = self.dnext[e]
            self.dnext[e] = (self.dnext[e] + 1) % NPOOL
        idx = dst.dsem[e]
        if dst.w is not None:
            self._wait(e, dst.w)
        for t in dst.rs:
            self._wait(e, t)
        for b in srcs:
            if b.w is not None:
                self._wait(e, b.w)
        if SERIAL:
            for se in ("T", "V", "S"):
                if self.cnt[se] > 0:
                    self._wait(e, (se, self.cnt[se]))
        sem = self.dpool[e][idx]
        if not isinstance(fns, (list, tuple)):
            fns = [fns]
        for f in fns:
            self.q[e].append(lambda eng, f=f, sem=sem: f(eng).then_inc(sem, 16))
            self.dcnt[e][idx] += 16
        tok = ("D", e, idx, self.dcnt[e][idx])
        dst.w = tok
        dst.rs = []
        for b in srcs:
            b.rs.append(tok)
        return tok

    def drain_dma(self):
        for e in ("Y", "G"):
            for q in ("Y", "G"):
                for idx in range(NPOOL):
                    if self.dcnt[q][idx] > 0:
                        self._wait(e, ("D", q, idx, self.dcnt[q][idx]))

    def barrier(self):
        for e in self.ENGS:
            for se in self.ENGS:
                if se not in ("Y", "G") and se != e and self.cnt[se] > 0:
                    self._wait(e, (se, self.cnt[se]))
            for q in ("Y", "G"):
                for idx in range(NPOOL):
                    if self.dcnt[q][idx] > 0:
                        self._wait(e, ("D", q, idx, self.dcnt[q][idx]))

    def flush(self, blk):
        for e, attr in (("T", "tensor"), ("V", "vector"), ("S", "scalar"),
                        ("Y", "sync"), ("G", "gpsimd")):
            ops = self.q[e]
            if not ops:
                continue
            def body(eng, ops=ops):
                for f in ops:
                    f(eng)
            getattr(blk, attr)(body)
            self.q[e] = []

class Group:
    def __init__(self, p, e):
        self.p = p
        self.e = e
        self.fns = []
        self.marks_r = []
        self.marks_w = []
        if SERIAL:
            for se in ("T", "V", "S", "G"):
                if se != e and p.cnt[se] > 0:
                    p._wait(e, (se, p.cnt[se]))
            for q in ("Y", "G"):
                for idx in range(NPOOL):
                    if p.dcnt[q][idx] > 0:
                        p._wait(e, ("D", q, idx, p.dcnt[q][idx]))
    def r(self, *bufs):
        for b in bufs:
            if b.w is not None:
                self.p._wait(self.e, b.w)
            self.marks_r.append(b)
        return self
    def w(self, *bufs):
        for b in bufs:
            if b.w is not None:
                self.p._wait(self.e, b.w)
            for t in b.rs:
                self.p._wait(self.e, t)
            self.marks_w.append(b)
        return self
    def add(self, fn):
        self.fns.append(fn)
        return self
    def fin(self):
        assert self.fns
        p, e = self.p, self.e
        sem = p.sem[e]
        last = self.fns[-1]
        for f in self.fns[:-1]:
            p.q[e].append(f)
        p.q[e].append(lambda eng, f=last, sem=sem: f(eng).then_inc(sem, 1))
        p.cnt[e] += 1
        tok = (e, p.cnt[e])
        for b in self.marks_w:
            b.w = tok
            b.rs = []
        for b in self.marks_r:
            b.rs.append(tok)
        return tok


_UID = [0]
def SB(nc, name, shape, dt):
    _UID[0] += 1
    return nc.sbuf_tensor(f"{name}_{_UID[0]}", shape, dt)

def PS(nc, name, shape, dt):
    _UID[0] += 1
    return nc.psum_tensor(f"{name}_{_UID[0]}", shape, dt)

def ap3(handle, off, dims):
    return bass.AP(handle, off, dims)

def g1(p, e, reads, writes, fn):
    g = p.group(e).r(*reads).w(*writes)
    g.add(fn)
    return g.fin()

def emit_ln_tile(p, ctx, xf, stats, xn_out):
    cmisc = ctx["cmisc"]
    g = p.group("V").r(xf).w(stats)
    g.add(lambda v: v.bn_stats(stats.h[:, 4:10], xf.h[:, 0:512]))
    g.add(lambda v: v.bn_stats(stats.h[:, 10:16], xf.h[:, 512:1024]))
    g.fin()
    g1(p, "V", [stats], [stats], lambda v: v.bn_aggr(stats.h[:, 0:2], stats.h[:, 4:16]))
    g1(p, "S", [stats, cmisc], [stats], lambda s: s.activation(
        stats.h[:, 2:3], stats.h[:, 1:2], AF.Sqrt, bias=cmisc.h[:, 0:1]))
    g1(p, "V", [stats], [stats], lambda v: v.reciprocal(stats.h[:, 3:4], stats.h[:, 2:3]))
    g1(p, "V", [xf, stats], [xn_out], lambda v: v.tensor_scalar(
        xn_out.h[:], xf.h[:], stats.h[:, 0:1], stats.h[:, 3:4], ALU.subtract, ALU.mult))

def transposes(p, ctx, src, psum_tr, dst, nch, width=128, src_off=0, dst_off=0,
               evac="S"):
    ident = ctx["ident"]
    g = p.group("T").r(src, ident).w(psum_tr)
    for c in range(nch):
        g.add(lambda t, c=c: t.transpose(
            psum_tr.h[0:width, c * 128:(c + 1) * 128],
            src.h[:, src_off + c * width:src_off + (c + 1) * width], ident.h[:]))
    g.fin()
    g = p.group(evac).r(psum_tr).w(dst)
    if evac == "S":
        g.add(lambda s: s.copy(dst.h[0:width, dst_off:dst_off + nch * 128],
                               psum_tr.h[0:width, 0:nch * 128]))
    else:
        g.add(lambda v: v.tensor_copy(dst.h[0:width, dst_off:dst_off + nch * 128],
                                      psum_tr.h[0:width, 0:nch * 128]))
    g.fin()

def mm_proj(p, xnT, w_sb, psum, nout, kchunks=8, wcol=None, kpart=128,
            xoff=0, woff=0, extra_r=()):
    if wcol is None:
        wcol = nout
    g = p.group("T").r(xnT, w_sb, *extra_r).w(psum)
    ns = [(o, min(512, nout - o)) for o in range(0, nout, 512)]
    for c in range(kchunks):
        kp = kpart if c < kchunks - 1 else kpart
        for (o, wd) in ns:
            g.add(lambda t, c=c, o=o, wd=wd, kp=kp: t.matmul(
                psum.h[:, o:o + wd],
                xnT.h[0:kp, xoff + c * 128:xoff + c * 128 + 128],
                w_sb.h[0:kp, woff + c * wcol + o:woff + c * wcol + o + wd],
                start=(c == 0), stop=(c == kchunks - 1)))
    return g.fin()

def load_w_bf16(nc, p, st2, ap, name, rows, cols, dst=None):
    E2 = st2.enter_context
    ch = (rows + 127) // 128
    if dst is None:
        dst = Buf(E2(SB(nc, name + "_bf", [128, ch * cols], BF16)))
    with contextlib.ExitStack() as stt:
        tmp = Buf(stt.enter_context(SB(nc, name + "_tmpf", [128, cols], F32)))
        for c in range(ch):
            rp = min(128, rows - c * 128)
            p.dma("Y", tmp, [], lambda e, c=c, rp=rp: e.dma_start(
                out=tmp.h[0:rp, :], in_=ap[name][c * 128:c * 128 + rp, :]))
            g1(p, "V", [tmp], [dst], lambda v, c=c, rp=rp: v.tensor_copy(
                dst.h[0:rp, c * cols:(c + 1) * cols], tmp.h[0:rp, :]))
        p.barrier()
    return dst

def conv_bf16(nc, p, st2, ap, name, rows, cols):
    E2 = st2.enter_context
    dst = Buf(E2(SB(nc, name + "_bf", [rows, cols], BF16)))
    tmp = Buf(E2(SB(nc, name + "_tf", [rows, cols], F32)))
    p.dma("Y", tmp, [], lambda e: e.dma_start(out=tmp.h[:], in_=ap[name]))
    g1(p, "V", [tmp], [dst], lambda v: v.tensor_copy(dst.h[:], tmp.h[:]))
    return dst


def phase_l0a(nc, p, st, ap, ctx, xin, xout):
  with contextlib.ExitStack() as stp:
    Ep = stp.enter_context
    kT = Buf(Ep(SB(nc, "l0a_kT", [128, 8 * T], BF16)))
    vb = Buf(Ep(SB(nc, "l0a_vb", [128, NT * 1032], BF16)))
    qTd = Buf(ap["qTd"])
    cmisc = ctx["cmisc"]
    g1(p, "V", [cmisc], [vb], lambda v: v.tensor_copy(
        ap3(vb.h, 1024, [[NT * 1032, 128], [1032, NT], [1, 8]]),
        ap3(cmisc.h, 3, [[16, 128], [0, NT], [0, 8]])))

    for proj in ("k", "q", "v"):
        _l0a_proj(nc, p, ap, ctx, xin, proj, kT, vb, qTd)

    _l0a_passB(nc, p, ap, ctx, xin, xout, kT, vb)


def _l0a_proj(nc, p, ap, ctx, xin, proj, kT, vb, qTd):
      cmisc = ctx["cmisc"]
      with contextlib.ExitStack() as st2:
        E2 = st2.enter_context
        wsb = load_w_bf16(nc, p, st2, ap, "l0a_w" + proj, D, D)
        xf = [Buf(E2(SB(nc, f"l0a_xf{i}", [128, D], F32))) for i in range(2)]
        stats = Buf(E2(SB(nc, "l0a_st", [128, 16], F32)))
        xn = Buf(E2(SB(nc, "l0a_xn", [128, D], BF16)))
        xnT = Buf(E2(SB(nc, "l0a_xnT", [128, D], BF16)))
        osb = Buf(E2(SB(nc, "l0a_o", [128, D], BF16)))
        ptr = Buf(E2(PS(nc, "l0a_ptr", [128, D], BF16)))
        pmm = Buf(E2(PS(nc, "l0a_pmm", [128, D], F32)))
        for t in range(NT):
            i = t % 2
            p.dma("Y", xf[i], [xin], lambda e, t=t, i=i: e.dma_start(
                out=xf[i].h[:], in_=xin.h[t * 128:(t + 1) * 128, :]))
            emit_ln_tile(p, ctx, xf[i], stats, xn)
            transposes(p, ctx, xn, ptr, xnT, 8)
            mm_proj(p, xnT, wsb, pmm, D)
            if proj == "v":
                g1(p, "S", [pmm], [vb], lambda s, t=t: s.copy(
                    vb.h[:, t * 1032:t * 1032 + 1024], pmm.h[:]))
                continue
            g1(p, "S", [pmm], [osb], lambda s: s.copy(osb.h[:], pmm.h[:]))
            g = p.group("T").r(osb, ctx["ident"]).w(ptr)
            for c in range(8):
                g.add(lambda e, c=c: e.transpose(ptr.h[:, c * 128:(c + 1) * 128],
                                                 osb.h[:, c * 128:(c + 1) * 128],
                                                 ctx["ident"].h[:]))
            g.fin()
            if proj == "k":
                g1(p, "V", [ptr], [kT], lambda v, t=t: v.tensor_copy(
                    ap3(kT.h, t * 128, [[8 * T, 128], [T, 8], [1, 128]]),
                    ap3(ptr.h, 0, [[1024, 128], [128, 8], [1, 128]])))
            else:
                g1(p, "V", [ptr], [osb], lambda v: v.tensor_copy(osb.h[:], ptr.h[:]))
                p.dma("G", qTd, [osb], [
                    (lambda e, t=t, c=c: e.dma_start(
                        out=qTd.h[c * 128:(c + 1) * 128, t * 128:(t + 1) * 128],
                        in_=osb.h[:, c * 128:(c + 1) * 128]))
                    for c in range(8)])
        p.barrier()


def _l0a_passB(nc, p, ap, ctx, xin, xout, kT, vb):
    cmisc = ctx["cmisc"]
    with contextlib.ExitStack() as st2:
        E2 = st2.enter_context
        qTg = Buf(E2(SB(nc, "l0a_qTg", [128, 8 * 256], BF16)))
        psT = [Buf(E2(PS(nc, f"l0a_psT{i}", [128, 512], F32))) for i in range(2)]
        pnum = [Buf(E2(PS(nc, f"l0a_pnum{i}", [128, 1032], F32))) for i in range(2)]
        pT = [Buf(E2(SB(nc, f"l0a_pT{i}", [128, 256], BF16))) for i in range(2)]
        yf = Buf(E2(SB(nc, "l0a_yf", [128, D], F32)))
        rden = Buf(E2(SB(nc, "l0a_rd", [128, 2], F32)))
        xf2 = [Buf(E2(SB(nc, f"l0a_x2{i}", [128, D], F32))) for i in range(2)]
        maskF = ctx["maskF"]

        for qg in range(T // 256):
            p.dma("Y", qTg, [], [
                (lambda e, qg=qg, c=c: e.dma_start(
                    out=qTg.h[:, c * 256:(c + 1) * 256],
                    in_=ap["qTd"][c * 128:(c + 1) * 128, qg * 256:(qg + 1) * 256]))
                for c in range(8)])
            for qc in range(2):
                qt = 2 * qg + qc
                p.dma("Y", xf2[qc], [xin], lambda e, qt=qt, qc=qc: e.dma_start(
                    out=xf2[qc].h[:], in_=xin.h[qt * 128:(qt + 1) * 128, :]))
            nkt = 2 * qg + 2
            def emit_av(kt, qg=qg):
                par = kt % 2
                for qc in range(2):
                    qt = 2 * qg + qc
                    if kt > qt:
                        continue
                    g = p.group("T").r(pT[par], vb).w(pnum[qc])
                    for (o, wd) in ((0, 512), (512, 512), (1024, 8)):
                        g.add(lambda t, qc=qc, kt=kt, o=o, wd=wd, qt=qt, par=par: t.matmul(
                            pnum[qc].h[:, o:o + wd],
                            pT[par].h[:, qc * 128:qc * 128 + 128],
                            vb.h[:, kt * 1032 + o:kt * 1032 + o + wd],
                            start=(kt == 0), stop=(kt == qt)))
                    g.fin()
            for kt in range(nkt):
                par = kt % 2
                g = p.group("T").r(kT, qTg).w(psT[par])
                for c in range(8):
                    g.add(lambda t, c=c, kt=kt, par=par: t.matmul(
                        psT[par].h[:, 0:256],
                        kT.h[:, c * T + kt * 128:c * T + kt * 128 + 128],
                        qTg.h[:, c * 256:(c + 1) * 256],
                        start=(c == 0), stop=(c == 7)))
                g.fin()
                g1(p, "S", [psT[par], cmisc], [pT[par]], lambda s, par=par: s.activation(
                    pT[par].h[:], psT[par].h[:, 0:256], AF.Exp,
                    bias=cmisc.h[:, 1:2], scale=1.0 / 32.0))
                j = kt - 2 * qg
                if j >= 0:
                    g1(p, "V", [maskF, pT[par]], [pT[par]], lambda v, j=j, par=par: v.tensor_tensor(
                        pT[par].h[:], pT[par].h[:], maskF.h[:, j * 512:j * 512 + 256], ALU.mult))
                if kt > 0:
                    emit_av(kt - 1)
            emit_av(nkt - 1)
            for qc in range(2):
                qt = 2 * qg + qc
                g1(p, "V", [pnum[qc]], [rden], lambda v, qc=qc: v.reciprocal(
                    rden.h[:, qc:qc + 1], pnum[qc].h[:, 1024:1025]))
                g1(p, "V", [pnum[qc], rden, xf2[qc]], [yf],
                   lambda v, qc=qc: v.scalar_tensor_tensor(
                       yf.h[:], pnum[qc].h[:, 0:1024], rden.h[:, qc:qc + 1],
                       xf2[qc].h[:], ALU.mult, ALU.add))
                p.dma("G", xout, [yf], lambda e, qt=qt: e.dma_start(
                    out=xout.h[qt * 128:(qt + 1) * 128, :], in_=yf.h[:]))
        p.barrier()


def emit_shift_load(p, ybuf, xf, t, cmisc=None):
    p.dma("Y", xf, [ybuf], lambda e, t=t: e.dma_start(
        out=xf.h[:, 0:512], in_=ybuf.h[t * 128:(t + 1) * 128, 0:512]))
    if t == 0:
        p.dma("Y", xf, [ybuf], lambda e: e.dma_start(
            out=xf.h[1:128, 512:1024], in_=ybuf.h[0:127, 512:1024]))
        g1(p, "V", [cmisc], [xf], lambda v: v.tensor_copy(
            xf.h[0:1, 512:1024], ap3(cmisc.h, 2, [[16, 1], [0, 512]])))
    else:
        p.dma("Y", xf, [ybuf], lambda e, t=t: e.dma_start(
            out=xf.h[:, 512:1024], in_=ybuf.h[t * 128 - 1:t * 128 + 127, 512:1024]))


def l0f_inner_layer(nc, p, st, ap, ctx, li, yin, ymid, yout, xin, last):
    cmisc = ctx["cmisc"]
    with contextlib.ExitStack() as sa:
        Ea = sa.enter_context
        qTl = Buf(Ea(SB(nc, "f_qT", [128, 4 * T], BF16)))
        kTl = Buf(Ea(SB(nc, "f_kT", [128, 4 * T], BF16)))
        vsb = Buf(Ea(SB(nc, "f_v", [128, NT * 576], BF16)))
        osb = Buf(Ea(SB(nc, "f_o", [128, 4 * 512], BF16)))
        obuf = Buf(ap["obuf"])
        g1(p, "V", [cmisc], [vsb], lambda v: v.tensor_copy(
            ap3(vsb.h, 64, [[NT * 576, 128], [576, NT], [72, 8], [1, 8]]),
            ap3(cmisc.h, 3, [[16, 128], [0, NT], [0, 8], [0, 8]])))
        with contextlib.ExitStack() as s2:
            E2 = s2.enter_context
            wqkv = load_w_bf16(nc, p, s2, ap, f"l0f_wqkv{li}", D, 1536)
            qsr = Buf(E2(SB(nc, "f_qsr", [128, 512], F32)))
            ksr = Buf(E2(SB(nc, "f_ksr", [128, 512], F32)))
            p.dma("Y", qsr, [], lambda e: e.dma_start(out=qsr.h[:], in_=ap[f"l0f_qs{li}"]))
            p.dma("Y", ksr, [], lambda e: e.dma_start(out=ksr.h[:], in_=ap[f"l0f_ks{li}"]))
            xf = [Buf(E2(SB(nc, "f_xf0", [128, D], F32)))] * 2
            hbf = Buf(E2(SB(nc, "f_hbf", [128, D], BF16)))
            hT = Buf(E2(SB(nc, "f_hT", [128, D], BF16)))
            qkvf = Buf(E2(SB(nc, "f_qkvf", [128, 1536], F32)))
            sqt = Buf(E2(SB(nc, "f_sq", [128, 512], F32)))
            ns = Buf(E2(SB(nc, "f_ns", [128, 32], F32)))
            rot = Buf(E2(SB(nc, "f_rot", [128, 512], F32)))
            tq1 = Buf(E2(SB(nc, "f_tq1", [128, 512], F32)))
            tq2 = Buf(E2(SB(nc, "f_tq2", [128, 512], F32)))
            qhat = Buf(E2(SB(nc, "f_qhat", [128, 512], BF16)))
            khat = Buf(E2(SB(nc, "f_khat", [128, 512], BF16)))
            tabs = [Buf(E2(SB(nc, f"f_tab{i}", [128, 512], F32))) for i in range(4)]
            pqkv = Buf(E2(PS(nc, "f_pqkv", [128, 1536], F32)))
            ptr = Buf(E2(PS(nc, "f_ptr", [128, D], BF16)))
            for t in range(NT):
                i = t % 2
                emit_shift_load(p, yin, xf[i], t, cmisc)
                g1(p, "V", [xf[i]], [hbf], lambda v, i=i: v.tensor_copy(hbf.h[:], xf[i].h[:]))
                transposes(p, ctx, hbf, ptr, hT, 8)
                mm_proj(p, hT, wqkv, pqkv, 1536, wcol=1536)
                g1(p, "V", [pqkv], [qkvf], lambda v: v.tensor_copy(qkvf.h[:], pqkv.h[:]))
                g1(p, "S", [qkvf], [sqt], lambda s: s.activation(sqt.h[:], qkvf.h[:, 0:512], AF.Square))
                g = p.group("V").r(sqt).w(ns)
                for h in range(8):
                    g.add(lambda v, h=h: v.reduce_sum(ns.h[:, h:h + 1], sqt.h[:, h * 64:(h + 1) * 64], axis=AX.X))
                g.fin()
                g1(p, "S", [qkvf], [sqt], lambda s: s.activation(sqt.h[:], qkvf.h[:, 512:1024], AF.Square))
                g = p.group("V").r(sqt).w(ns)
                for h in range(8):
                    g.add(lambda v, h=h: v.reduce_sum(ns.h[:, 8 + h:9 + h], sqt.h[:, h * 64:(h + 1) * 64], axis=AX.X))
                g.fin()
                g1(p, "S", [ns, cmisc], [ns], lambda s: s.activation(
                    ns.h[:, 16:32], ns.h[:, 0:16], AF.Sqrt, bias=cmisc.h[:, 5:6]))
                g1(p, "V", [ns], [ns], lambda v: v.reciprocal(
                    ns.h[:, 16:32], ns.h[:, 16:32]))
                g = p.group("V").r(qkvf, ns).w(qkvf)
                for h in range(8):
                    g.add(lambda v, h=h: v.tensor_scalar(
                        qkvf.h[:, h * 64:(h + 1) * 64], qkvf.h[:, h * 64:(h + 1) * 64],
                        ns.h[:, 16 + h:17 + h], None, ALU.mult))
                for h in range(8):
                    g.add(lambda v, h=h: v.tensor_scalar(
                        qkvf.h[:, 512 + h * 64:512 + (h + 1) * 64],
                        qkvf.h[:, 512 + h * 64:512 + (h + 1) * 64],
                        ns.h[:, 24 + h:25 + h], None, ALU.mult))
                g.fin()
                g1(p, "V", [qkvf, qsr], [qkvf], lambda v: v.tensor_tensor(
                    qkvf.h[:, 0:512], qkvf.h[:, 0:512], qsr.h[:], ALU.mult))
                g1(p, "V", [qkvf, ksr], [qkvf], lambda v: v.tensor_tensor(
                    qkvf.h[:, 512:1024], qkvf.h[:, 512:1024], ksr.h[:], ALU.mult))
                for ti, nm in enumerate(("c_csq", "c_snq", "c_csk", "c_snk")):
                    p.dma("Y", tabs[ti], [], lambda e, t=t, ti=ti, nm=nm: e.dma_start(
                        out=tabs[ti].h[:], in_=ap[nm][t * 128:(t + 1) * 128, :]))
                for (qoff, cs, sn, dstb) in ((0, tabs[0], tabs[1], qhat),
                                             (512, tabs[2], tabs[3], khat)):
                    g = p.group("V").r(qkvf).w(rot)
                    g.add(lambda v, qoff=qoff: v.tensor_single_scalar(
                        ap3(rot.h, 0, [[512, 128], [64, 8], [1, 32]]),
                        ap3(qkvf.h, qoff + 32, [[1536, 128], [64, 8], [1, 32]]),
                        -1.0, ALU.mult))
                    g.add(lambda v, qoff=qoff: v.tensor_copy(
                        ap3(rot.h, 32, [[512, 128], [64, 8], [1, 32]]),
                        ap3(qkvf.h, qoff, [[1536, 128], [64, 8], [1, 32]])))
                    g.fin()
                    g1(p, "V", [qkvf, cs], [tq1], lambda v, qoff=qoff, cs=cs: v.tensor_tensor(
                        tq1.h[:], qkvf.h[:, qoff:qoff + 512], cs.h[:], ALU.mult))
                    g1(p, "V", [rot, sn], [tq2], lambda v, sn=sn: v.tensor_tensor(
                        tq2.h[:], rot.h[:], sn.h[:], ALU.mult))
                    g1(p, "V", [tq1, tq2], [dstb], lambda v, dstb=dstb: v.tensor_tensor(
                        dstb.h[:], tq1.h[:], tq2.h[:], ALU.add))
                for (src, dstT) in ((qhat, qTl), (khat, kTl)):
                    g = p.group("T").r(src, ctx["ident"]).w(ptr)
                    for c in range(4):
                        g.add(lambda e, c=c, src=src: e.transpose(
                            ptr.h[:, c * 128:(c + 1) * 128],
                            src.h[:, c * 128:(c + 1) * 128], ctx["ident"].h[:]))
                    g.fin()
                    g1(p, "V", [ptr], [dstT], lambda v, t=t, dstT=dstT: v.tensor_copy(
                        ap3(dstT.h, t * 128, [[4 * T, 128], [T, 4], [1, 128]]),
                        ap3(ptr.h, 0, [[1024, 128], [128, 4], [1, 128]])))
                g1(p, "V", [qkvf], [vsb], lambda v, t=t: v.tensor_copy(
                    ap3(vsb.h, t * 576, [[NT * 576, 128], [72, 8], [1, 64]]),
                    ap3(qkvf.h, 1024, [[1536, 128], [64, 8], [1, 64]])))
            p.barrier()
        with contextlib.ExitStack() as s2:
            E2 = s2.enter_context
            maskW = ctx["maskW"]
            psT = [Buf(E2(PS(nc, f"f_psT{i}", [128, 512], F32))) for i in range(2)]
            pacc = [Buf(E2(PS(nc, f"f_pacc{i}", [128, 128], F32))) for i in range(4)]
            Pb = [Buf(E2(SB(nc, f"f_Pb{i}", [128, 512], BF16))) for i in range(2)]
            stmp = [Buf(E2(SB(nc, f"f_stmp{i}", [128, 512], F32))) for i in range(2)]
            rd = Buf(E2(SB(nc, "f_rd", [128, 4], F32)))
            for w in range(NW):
                for h in range(8):
                    kts = list(range(8)) if w > 0 else list(range(4, 8))
                    def emit_avf(kt, h=h, w=w, kts=kts):
                        par = kt % 2
                        tk = 4 * (w - 1) + kt
                        for qc in range(4):
                            g1(p, "T", [Pb[par], vsb], [pacc[qc]],
                               lambda t, qc=qc, tk=tk, h=h, kt=kt, kts=kts, par=par: t.matmul(
                                   pacc[qc].h[:, 0:72],
                                   Pb[par].h[:, qc * 128:qc * 128 + 128],
                                   vsb.h[:, tk * 576 + h * 72:tk * 576 + h * 72 + 72],
                                   start=(kt == kts[0]), stop=(kt == kts[-1])))
                    for kt in kts:
                        par = kt % 2
                        tk = 4 * (w - 1) + kt
                        g1(p, "T", [kTl, qTl], [psT[par]], lambda t, h=h, tk=tk, w=w, par=par: t.matmul(
                            psT[par].h[:],
                            kTl.h[(h % 2) * 64:(h % 2) * 64 + 64,
                                  (h // 2) * T + tk * 128:(h // 2) * T + tk * 128 + 128],
                            qTl.h[(h % 2) * 64:(h % 2) * 64 + 64,
                                  (h // 2) * T + w * 512:(h // 2) * T + w * 512 + 512],
                            start=True, stop=True))
                        g1(p, "V", [psT[par], maskW], [stmp[par]], lambda v, kt=kt, par=par: v.tensor_tensor(
                            stmp[par].h[:], psT[par].h[:], maskW.h[:, kt * 512:(kt + 1) * 512], ALU.add))
                        g1(p, "S", [stmp[par], cmisc], [Pb[par]], lambda s, par=par: s.activation(
                            Pb[par].h[:], stmp[par].h[:], AF.Exp, bias=cmisc.h[:, 1:2], scale=8.0))
                        if kt > kts[0]:
                            emit_avf(kt - 1)
                    emit_avf(kts[-1])
                    for qc in range(4):
                        g1(p, "V", [pacc[qc]], [rd], lambda v, qc=qc: v.reciprocal(
                            rd.h[:, qc:qc + 1], pacc[qc].h[:, 64:65]))
                        g1(p, "V", [pacc[qc], rd], [osb], lambda v, qc=qc, h=h: v.tensor_scalar(
                            osb.h[:, qc * 512 + h * 64:qc * 512 + h * 64 + 64],
                            pacc[qc].h[:, 0:64], rd.h[:, qc:qc + 1], None, ALU.mult))
                p.dma("G", obuf, [osb], [
                    (lambda e, w=w, qc=qc: e.dma_start(
                        out=obuf.h[(w * 4 + qc) * 128:(w * 4 + qc + 1) * 128, :],
                        in_=osb.h[:, qc * 512:(qc + 1) * 512]))
                    for qc in range(4)])
            p.barrier()
        with contextlib.ExitStack() as s2:
            E2 = s2.enter_context
            wo = load_w_bf16(nc, p, s2, ap, f"l0f_wo{li}", 512, D)
            olb = Buf(E2(SB(nc, "f_olb", [128, 512], BF16)))
            oT = Buf(E2(SB(nc, "f_oT", [128, 512], BF16)))
            ptrW = Buf(E2(PS(nc, "f_ptr2", [128, 512], BF16)))
            pwo = Buf(E2(PS(nc, "f_pwo", [128, D], F32)))
            wyf = [Buf(E2(SB(nc, f"f_yf{i}", [128, D], F32))) for i in range(2)]
            wyn = Buf(E2(SB(nc, "f_yn", [128, D], F32)))
            for t in range(NT):
                i = t % 2
                p.dma("Y", wyf[i], [yin], lambda e, t=t, i=i: e.dma_start(
                    out=wyf[i].h[:], in_=yin.h[t * 128:(t + 1) * 128, :]))
                p.dma("Y", olb, [obuf], lambda e, t=t: e.dma_start(
                    out=olb.h[:], in_=obuf.h[t * 128:(t + 1) * 128, :]))
                g = p.group("T").r(olb, ctx["ident"]).w(ptrW)
                for c in range(4):
                    g.add(lambda e, c=c: e.transpose(
                        ptrW.h[:, c * 128:(c + 1) * 128],
                        olb.h[:, c * 128:(c + 1) * 128],
                        ctx["ident"].h[:]))
                g.fin()
                g1(p, "V", [ptrW], [oT], lambda v: v.tensor_copy(oT.h[:], ptrW.h[:, 0:512]))
                mm_proj(p, oT, wo, pwo, D, kchunks=4)
                g1(p, "V", [pwo, wyf[i]], [wyn], lambda v, i=i: v.tensor_tensor(
                    wyn.h[:], pwo.h[:], wyf[i].h[:], ALU.add))
                p.dma("G", ymid, [wyn], lambda e, t=t: e.dma_start(
                    out=ymid.h[t * 128:(t + 1) * 128, :], in_=wyn.h[:]))
            p.barrier()
    pieces = [(o, min(512, 2730 - o)) for o in range(0, 2730, 512)]
    with contextlib.ExitStack() as s2:
        E2 = s2.enter_context
        w1 = load_w_bf16(nc, p, s2, ap, f"l0f_w1{li}", D, 5460)
        lgr = Buf(E2(SB(nc, "f_lgr", [128, D], F32)))
        lbr = Buf(E2(SB(nc, "f_lbr", [128, D], F32)))
        b1r = Buf(E2(SB(nc, "f_b1r", [128, 5460], F32)))
        p.dma("Y", lgr, [], lambda e: e.dma_start(out=lgr.h[:], in_=ap[f"l0f_lg{li}"]))
        p.dma("Y", lbr, [], lambda e: e.dma_start(out=lbr.h[:], in_=ap[f"l0f_lb{li}"]))
        p.dma("Y", b1r, [], lambda e: e.dma_start(out=b1r.h[:], in_=ap[f"l0f_b1{li}"]))
        mxf = [Buf(E2(SB(nc, f"f_mx{i}", [128, D], F32))) for i in range(2)]
        mstats = Buf(E2(SB(nc, "f_mst", [128, 16], F32)))
        mnf = Buf(E2(SB(nc, "f_nf", [128, D], F32)))
        n2 = Buf(E2(SB(nc, "f_n2", [128, D], BF16)))
        n2T = Buf(E2(SB(nc, "f_n2T", [128, D], BF16)))
        aa = Buf(E2(SB(nc, "f_aa", [128, 512], F32)))
        gg = Buf(E2(SB(nc, "f_gg", [128, 512], F32)))
        hm = Buf(E2(SB(nc, "f_hm", [128, 2730], BF16)))
        hmTs = Buf(E2(SB(nc, "f_hmTs", [128, 2816], BF16)))
        ptrM = Buf(E2(PS(nc, "f_ptr3", [128, D], BF16)))
        pA = Buf(E2(PS(nc, "f_pA", [128, 512], F32)))
        pG = Buf(E2(PS(nc, "f_pG", [128, 512], F32)))
        hmT = Buf(ap["hmT"])
        for t in range(NT):
            i = t % 2
            emit_shift_load(p, ymid, mxf[i], t, ctx['cmisc'])
            emit_ln_tile(p, ctx, mxf[i], mstats, mnf)
            g1(p, "V", [mnf, lgr], [mnf], lambda v: v.tensor_tensor(mnf.h[:], mnf.h[:], lgr.h[:], ALU.mult))
            g1(p, "V", [mnf, lbr], [n2], lambda v: v.tensor_tensor(n2.h[:], mnf.h[:], lbr.h[:], ALU.add))
            transposes(p, ctx, n2, ptrM, n2T, 8)
            for (off, wd) in pieces:
                mm_proj(p, n2T, w1, pA, wd, wcol=5460, woff=off)
                mm_proj(p, n2T, w1, pG, wd, wcol=5460, woff=2730 + off)
                g1(p, "V", [pA, b1r], [aa], lambda v, off=off, wd=wd: v.tensor_tensor(
                    aa.h[:, 0:wd], pA.h[:, 0:wd], b1r.h[:, off:off + wd], ALU.add))
                g1(p, "V", [pG, b1r], [gg], lambda v, off=off, wd=wd: v.tensor_tensor(
                    gg.h[:, 0:wd], pG.h[:, 0:wd], b1r.h[:, 2730 + off:2730 + off + wd], ALU.add))
                g1(p, "S", [gg], [gg], lambda s, wd=wd: s.activation(
                    gg.h[:, 0:wd], gg.h[:, 0:wd], AF.Gelu))
                g1(p, "V", [aa, gg], [hm], lambda v, off=off, wd=wd: v.tensor_tensor(
                    hm.h[:, off:off + wd], aa.h[:, 0:wd], gg.h[:, 0:wd], ALU.mult))
            for cb in range(3):
                c0 = cb * 8
                nch = min(8, 22 - c0)
                g = p.group("T").r(hm, ctx["ident"]).w(ptrM)
                for c in range(c0, c0 + nch):
                    wd = min(128, 2730 - c * 128)
                    g.add(lambda e, c=c, wd=wd, c0=c0: e.transpose(
                        ptrM.h[0:wd, (c - c0) * 128:(c - c0) * 128 + 128],
                        hm.h[:, c * 128:c * 128 + wd], ctx["ident"].h[:]))
                g.fin()
                nfull = nch - 1 if c0 + nch == 22 else nch
                if nfull > 0:
                    g1(p, "V", [ptrM], [hmTs], lambda v, c0=c0, nfull=nfull: v.tensor_copy(
                        hmTs.h[:, c0 * 128:(c0 + nfull) * 128], ptrM.h[:, 0:nfull * 128]))
                if nfull < nch:
                    g1(p, "V", [ptrM], [hmTs], lambda v, c0=c0, nch=nch: v.tensor_copy(
                        hmTs.h[0:42, (c0 + nch - 1) * 128:(c0 + nch) * 128],
                        ptrM.h[0:42, (nch - 1) * 128:nch * 128]))
            p.dma("G", hmT, [hmTs], [
                (lambda e, t=t, c=c, rp=min(128, 2730 - c * 128): e.dma_start(
                    out=hmT.h[c * 128:c * 128 + rp, t * 128:(t + 1) * 128],
                    in_=hmTs.h[0:rp, c * 128:(c + 1) * 128]))
                for c in range(22)])
        p.barrier()
    with contextlib.ExitStack() as s2:
        E2 = s2.enter_context
        w2 = load_w_bf16(nc, p, s2, ap, f"l0f_w2{li}", 2730, D)
        b2r = Buf(E2(SB(nc, "f_b2r", [128, D], F32)))
        p.dma("Y", b2r, [], lambda e: e.dma_start(out=b2r.h[:], in_=ap[f"l0f_b2{li}"]))
        hmTl = Buf(E2(SB(nc, "f_hmTl", [128, 2816], BF16)))
        yf2 = [Buf(E2(SB(nc, f"f_wyf{i}", [128, D], F32))) for i in range(2)]
        xr = Buf(E2(SB(nc, "f_wxr", [128, D], F32)))
        yn2 = Buf(E2(SB(nc, "f_wyn", [128, D], F32)))
        pm = Buf(E2(PS(nc, "f_pm", [128, D], F32)))
        hmT2 = Buf(ap["hmT"])
        for t in range(NT):
            i = t % 2
            p.dma("Y", hmTl, [hmT2], [
                (lambda e, t=t, c=c, rp=min(128, 2730 - c * 128): e.dma_start(
                    out=hmTl.h[0:rp, c * 128:(c + 1) * 128],
                    in_=hmT2.h[c * 128:c * 128 + rp, t * 128:(t + 1) * 128]))
                for c in range(22)])
            p.dma("Y", yf2[i], [ymid], lambda e, t=t, i=i: e.dma_start(
                out=yf2[i].h[:], in_=ymid.h[t * 128:(t + 1) * 128, :]))
            g = p.group("T").r(hmTl, w2).w(pm)
            for c in range(22):
                rp = min(128, 2730 - c * 128)
                for o in (0, 512):
                    g.add(lambda t_, c=c, o=o, rp=rp: t_.matmul(
                        pm.h[:, o:o + 512],
                        hmTl.h[0:rp, c * 128:(c + 1) * 128],
                        w2.h[0:rp, c * D + o:c * D + o + 512],
                        start=(c == 0), stop=(c == 21)))
            g.fin()
            g1(p, "V", [pm, b2r], [yn2], lambda v: v.tensor_tensor(
                yn2.h[:], pm.h[:], b2r.h[:], ALU.add))
            g1(p, "V", [yn2, yf2[i]], [yn2], lambda v, i=i: v.tensor_tensor(
                yn2.h[:], yn2.h[:], yf2[i].h[:], ALU.add))
            if last:
                p.dma("Y", xr, [xin], lambda e, t=t: e.dma_start(
                    out=xr.h[:], in_=xin.h[t * 128:(t + 1) * 128, :]))
                g1(p, "V", [yn2, xr], [yn2], lambda v: v.tensor_tensor(
                    yn2.h[:], yn2.h[:], xr.h[:], ALU.add))
            p.dma("G", yout, [yn2], lambda e, t=t: e.dma_start(
                out=yout.h[t * 128:(t + 1) * 128, :], in_=yn2.h[:]))
        p.barrier()


def phase_l0f(nc, p, st, ap, ctx, xin, xout):
    ybA = Buf(ap["ybA"]); ybB = Buf(ap["ybB"])
    with contextlib.ExitStack() as s2:
        E2 = s2.enter_context
        xf = [Buf(E2(SB(nc, f"f_lx{i}", [128, D], F32))) for i in range(2)]
        stats = Buf(E2(SB(nc, "f_lst", [128, 16], F32)))
        nf = Buf(E2(SB(nc, "f_lnf", [128, D], F32)))
        for t in range(NT):
            i = t % 2
            p.dma("Y", xf[i], [xin], lambda e, t=t, i=i: e.dma_start(
                out=xf[i].h[:], in_=xin.h[t * 128:(t + 1) * 128, :]))
            emit_ln_tile(p, ctx, xf[i], stats, nf)
            p.dma("G", ybA, [nf], lambda e, t=t: e.dma_start(
                out=ybA.h[t * 128:(t + 1) * 128, :], in_=nf.h[:]))
        p.barrier()
    l0f_inner_layer(nc, p, st, ap, ctx, 0, ybA, ybB, ybA, xin, last=False)
    l0f_inner_layer(nc, p, st, ap, ctx, 1, ybA, ybB, xout, xin, last=True)


def phase_l1a(nc, p, st, ap, ctx, xin, xout):
    cmisc = ctx["cmisc"]
    maskF = ctx["maskF"]
    with contextlib.ExitStack() as s2:
        E2 = s2.enter_context
        wq = load_w_bf16(nc, p, s2, ap, "l1a_wqT", D, D)
        wk = load_w_bf16(nc, p, s2, ap, "l1a_wkT", D, D)
        wv = load_w_bf16(nc, p, s2, ap, "l1a_wvT", D, D)
        wo = load_w_bf16(nc, p, s2, ap, "l1a_woT", D, D)
        brs = {}
        for nm in ("bq", "bk", "bv", "bo"):
            brs[nm] = Buf(E2(SB(nc, "a_" + nm, [128, D], F32)))
            p.dma("Y", brs[nm], [], lambda e, nm=nm: e.dma_start(
                out=brs[nm].h[:], in_=ap["l1a_" + nm]))
        Sf = Buf(E2(SB(nc, "a_Sf", [128, 16 * 72], F32)))
        Sz = Buf(E2(SB(nc, "a_Sz", [128, 16 * 72], BF16)))
        g1(p, "V", [cmisc], [Sf], lambda v: v.tensor_copy(
            Sf.h[:], ap3(cmisc.h, 2, [[16, 128], [0, 16 * 72]])))
        g1(p, "V", [cmisc], [Sz], lambda v: v.tensor_copy(
            Sz.h[:], ap3(cmisc.h, 2, [[16, 128], [0, 16 * 72]])))
        xf = [Buf(E2(SB(nc, f"a_xf{i}", [128, D], F32))) for i in range(2)]
        stats = Buf(E2(SB(nc, "a_st", [128, 16], F32)))
        xn = Buf(E2(SB(nc, "a_xn", [128, D], BF16)))
        xnT = Buf(E2(SB(nc, "a_xnT", [128, D], BF16)))
        tmpf = Buf(E2(SB(nc, "a_tmpf", [128, D], F32)))
        qe = Buf(E2(SB(nc, "a_qe", [128, D], BF16)))
        ke = Buf(E2(SB(nc, "a_ke", [128, D], BF16)))
        qeT = Buf(E2(SB(nc, "a_qeT", [128, D], BF16)))
        keT = Buf(E2(SB(nc, "a_keT", [128, D], BF16)))
        vsb = Buf(E2(SB(nc, "a_vsb", [128, 1152], BF16)))
        Abf = Buf(E2(SB(nc, "a_Abf", [128, 128], BF16)))
        ysb = Buf(E2(SB(nc, "a_ysb", [128, D], F32)))
        ybf = Buf(E2(SB(nc, "a_ybf", [128, D], BF16)))
        rd = Buf(E2(SB(nc, "a_rd", [128, 1], F32)))
        ptr = Buf(E2(PS(nc, "a_ptr", [128, D], BF16)))
        pmm = Buf(E2(PS(nc, "a_pmm", [128, D], F32)))
        pA = Buf(E2(PS(nc, "a_pA", [128, 128], F32)))
        pO = Buf(E2(PS(nc, "a_pO", [128, 128], F32)))
        pS = Buf(E2(PS(nc, "a_pS", [128, 128], F32)))
        for t in range(NT):
            i = t % 2
            p.dma("Y", xf[i], [xin], lambda e, t=t, i=i: e.dma_start(
                out=xf[i].h[:], in_=xin.h[t * 128:(t + 1) * 128, :]))
            emit_ln_tile(p, ctx, xf[i], stats, xn)
            transposes(p, ctx, xn, ptr, xnT, 8)
            for (wsb, br, dst) in ((wq, "bq", qe), (wk, "bk", ke), (wv, "bv", None)):
                mm_proj(p, xnT, wsb, pmm, D)
                g1(p, "V", [pmm, brs[br]], [tmpf], lambda v, br=br: v.tensor_tensor(
                    tmpf.h[:], pmm.h[:], brs[br].h[:], ALU.add))
                if dst is None:
                    g1(p, "V", [tmpf], [vsb], lambda v: v.tensor_copy(
                        ap3(vsb.h, 0, [[1152, 128], [72, 16], [1, 64]]),
                        ap3(tmpf.h, 0, [[1024, 128], [64, 16], [1, 64]])))
                    g1(p, "V", [cmisc], [vsb], lambda v: v.tensor_copy(
                        ap3(vsb.h, 64, [[1152, 128], [72, 16], [1, 8]]),
                        ap3(cmisc.h, 3, [[16, 128], [0, 16], [0, 8]])))
                else:
                    g1(p, "S", [tmpf], [dst], lambda s, dst=dst: s.activation(
                        dst.h[:], tmpf.h[:], AF.Exp))
            for (src, dT) in ((qe, qeT), (ke, keT)):
                g = p.group("T").r(src, ctx["ident"]).w(ptr)
                for c in range(8):
                    g.add(lambda e, c=c, src=src: e.transpose(
                        ptr.h[:, c * 128:(c + 1) * 128],
                        src.h[:, c * 128:(c + 1) * 128], ctx["ident"].h[:]))
                g.fin()
                g1(p, "V", [ptr], [dT], lambda v, dT=dT: v.tensor_copy(dT.h[:], ptr.h[:]))
            for h in range(16):
                po = (h % 2) * 64
                ch = h // 2
                g1(p, "T", [keT, qeT], [pA], lambda t_, po=po, ch=ch: t_.matmul(
                    pA.h[:], keT.h[po:po + 64, ch * 128:(ch + 1) * 128],
                    qeT.h[po:po + 64, ch * 128:(ch + 1) * 128], start=True, stop=True))
                g1(p, "V", [pA, maskF], [Abf], lambda v: v.tensor_tensor(
                    Abf.h[:], pA.h[:], maskF.h[:, 0:128], ALU.mult))
                g = p.group("T").r(Abf, vsb, qeT, Sz).w(pO)
                g.add(lambda t_, h=h: t_.matmul(
                    pO.h[:, 0:72], Abf.h[:], vsb.h[:, h * 72:(h + 1) * 72],
                    start=True, stop=False))
                g.add(lambda t_, h=h, ch=ch: t_.matmul(
                    pO.h[:, 0:72], qeT.h[:, ch * 128:(ch + 1) * 128],
                    Sz.h[:, h * 72:(h + 1) * 72], start=False, stop=True))
                g.fin()
                g1(p, "T", [ke, vsb], [pS], lambda t_, h=h, po=po: t_.matmul(
                    pS.h[po:po + 64, 0:72], ke.h[:, h * 64:(h + 1) * 64],
                    vsb.h[:, h * 72:(h + 1) * 72], start=True, stop=True))
                g1(p, "V", [pS, Sf], [Sf], lambda v, h=h, po=po: v.tensor_tensor(
                    Sf.h[po:po + 64, h * 72:(h + 1) * 72],
                    Sf.h[po:po + 64, h * 72:(h + 1) * 72],
                    pS.h[po:po + 64, 0:72], ALU.add))
                g1(p, "V", [Sf], [Sz], lambda v, h=h, po=po: v.tensor_copy(
                    Sz.h[po:po + 64, h * 72:(h + 1) * 72],
                    Sf.h[po:po + 64, h * 72:(h + 1) * 72]))
                g1(p, "V", [pO], [rd], lambda v: v.reciprocal(rd.h[:], pO.h[:, 64:65]))
                g1(p, "V", [pO, rd], [ysb], lambda v, h=h: v.tensor_scalar(
                    ysb.h[:, h * 64:(h + 1) * 64], pO.h[:, 0:64], rd.h[:], None, ALU.mult))
            g1(p, "V", [ysb], [ybf], lambda v: v.tensor_copy(ybf.h[:], ysb.h[:]))
            transposes(p, ctx, ybf, ptr, xnT, 8)
            mm_proj(p, xnT, wo, pmm, D)
            g1(p, "V", [pmm, brs["bo"]], [tmpf], lambda v: v.tensor_tensor(
                tmpf.h[:], pmm.h[:], brs["bo"].h[:], ALU.add))
            g1(p, "V", [tmpf, xf[i]], [ysb], lambda v, i=i: v.tensor_tensor(
                ysb.h[:], tmpf.h[:], xf[i].h[:], ALU.add))
            p.dma("G", xout, [ysb], lambda e, t=t: e.dma_start(
                out=xout.h[t * 128:(t + 1) * 128, :], in_=ysb.h[:]))
        p.barrier()


def phase_l0b(nc, p, st, ap, ctx, xin, xout):
  with contextlib.ExitStack() as stp:
    Ep = stp.enter_context
    kT = Buf(Ep(SB(nc, "c_kT", [128, 8 * T], BF16)))
    vb = Buf(Ep(SB(nc, "c_vb", [128, NT * 1024], BF16)))
    kbosT = Buf(Ep(SB(nc, "c_kbosT", [128, 8], BF16)))
    vbosrep = Buf(Ep(SB(nc, "c_vbosrep", [128, D], BF16)))
    xnb = Buf(ap["ybA"])
    qTd = Buf(ap["qTd"])

    with contextlib.ExitStack() as s2:
        E2 = s2.enter_context
        xf = [Buf(E2(SB(nc, "c_lx0", [128, D], F32)))] * 2
        stats = Buf(E2(SB(nc, "c_lst", [128, 16], F32)))
        nf = Buf(E2(SB(nc, "c_lnf", [128, D], F32)))
        for t in range(NT):
            i = t % 2
            p.dma("Y", xf[i], [xin], lambda e, t=t, i=i: e.dma_start(
                out=xf[i].h[:], in_=xin.h[t * 128:(t + 1) * 128, :]))
            emit_ln_tile(p, ctx, xf[i], stats, nf)
            p.dma("G", xnb, [nf], lambda e, t=t: e.dma_start(
                out=xnb.h[t * 128:(t + 1) * 128, :], in_=nf.h[:]))
        p.barrier()

    for proj in ("k", "q", "v"):
        _l0b_proj(nc, p, ap, ctx, proj, xnb, kT, vb, kbosT, vbosrep, qTd)
    _l0b_attn(nc, p, ap, ctx, kT, vb, kbosT, vbosrep, qTd)
    _l0b_out(nc, p, ap, ctx, xin, xout, vbosrep)


def _l0b_proj(nc, p, ap, ctx, proj, xnb, kT, vb, kbosT, vbosrep, qTd):
    cmisc = ctx["cmisc"]
    obuf2 = Buf(ap["obuf2"])
    with contextlib.ExitStack() as st2:
        E2 = st2.enter_context
        wsb = load_w_bf16(nc, p, st2, ap, "l0b_w" + proj, D, D)
        xff = [Buf(E2(SB(nc, "c_xff0", [128, D], F32)))] * 2
        xf = [Buf(E2(SB(nc, "c_xf0", [128, D], BF16)))] * 2
        osb = Buf(E2(SB(nc, "c_osb", [128, D], BF16)))
        ptr = Buf(E2(PS(nc, "c_ptr", [128, D], BF16)))
        pmm = Buf(E2(PS(nc, "c_pmm", [128, D], F32)))
        if proj in ("k", "v"):
            p.dma("Y", xff[0], [xnb], lambda e: e.dma_start(
                out=xff[0].h[:], in_=xnb.h[0:128, :]))
            g1(p, "V", [xff[0]], [xf[0]], lambda v: v.tensor_copy(xf[0].h[:], xff[0].h[:]))
            transposes(p, ctx, xf[0], ptr, osb, 8)
            mm_proj(p, osb, wsb, pmm, D)
            if proj == "v":
                g1(p, "S", [pmm], [osb], lambda s: s.copy(osb.h[:], pmm.h[:]))
                vbd = Buf(ap["vbos_d"])
                p.dma("G", vbd, [osb], lambda e: e.dma_start(
                    out=vbd.h[:], in_=osb.h[0:1, :]))
                p.dma("Y", vbosrep, [vbd], lambda e: e.dma_start(
                    out=vbosrep.h[:], in_=ap3(vbd.h.tensor, 0, [[0, 128], [1, D]])))
            else:
                g1(p, "S", [pmm], [osb], lambda s: s.copy(osb.h[:], pmm.h[:]))
                g = p.group("T").r(osb, ctx["ident"]).w(ptr)
                for c in range(8):
                    g.add(lambda e, c=c: e.transpose(
                        ptr.h[:, c * 128:(c + 1) * 128],
                        osb.h[:, c * 128:(c + 1) * 128], ctx["ident"].h[:]))
                g.fin()
                g1(p, "V", [ptr], [kbosT], lambda v: v.tensor_copy(
                    ap3(kbosT.h, 0, [[8, 128], [1, 8]]),
                    ap3(ptr.h, 0, [[1024, 128], [128, 8]])))
        for t in range(NT):
            i = t % 2
            if t < NT - 1:
                p.dma("Y", xff[i], [xnb], lambda e, t=t, i=i: e.dma_start(
                    out=xff[i].h[:], in_=xnb.h[t * 128 + 1:t * 128 + 129, :]))
            else:
                p.dma("Y", xff[i], [xnb], lambda e, t=t, i=i: e.dma_start(
                    out=xff[i].h[0:127, :], in_=xnb.h[t * 128 + 1:t * 128 + 128, :]))
                p.dma("Y", xff[i], [], lambda e, i=i: e.dma_start(
                    out=xff[i].h[127:128, :], in_=ap["c_zrow"]))
            g1(p, "V", [xff[i]], [xf[i]], lambda v, i=i: v.tensor_copy(xf[i].h[:], xff[i].h[:]))
            transposes(p, ctx, xf[i], ptr, osb, 8)
            mm_proj(p, osb, wsb, pmm, D)
            if proj == "v":
                g1(p, "S", [pmm], [vb], lambda s, t=t: s.copy(
                    vb.h[:, t * 1024:(t + 1) * 1024], pmm.h[:]))
                continue
            g1(p, "S", [pmm], [osb], lambda s: s.copy(osb.h[:], pmm.h[:]))
            g = p.group("T").r(osb, ctx["ident"]).w(ptr)
            for c in range(8):
                g.add(lambda e, c=c: e.transpose(ptr.h[:, c * 128:(c + 1) * 128],
                                                 osb.h[:, c * 128:(c + 1) * 128],
                                                 ctx["ident"].h[:]))
            g.fin()
            if proj == "k":
                g1(p, "V", [ptr], [kT], lambda v, t=t: v.tensor_copy(
                    ap3(kT.h, t * 128, [[8 * T, 128], [T, 8], [1, 128]]),
                    ap3(ptr.h, 0, [[1024, 128], [128, 8], [1, 128]])))
            else:
                g1(p, "V", [ptr], [osb], lambda v: v.tensor_copy(osb.h[:], ptr.h[:]))
                p.dma("G", qTd, [osb], [
                    (lambda e, t=t, c=c: e.dma_start(
                        out=qTd.h[c * 128:(c + 1) * 128, t * 128:(t + 1) * 128],
                        in_=osb.h[:, c * 128:(c + 1) * 128]))
                    for c in range(8)])
        p.barrier()


def _l0b_attn(nc, p, ap, ctx, kT, vb, kbosT, vbosrep, qTd):
    cmisc = ctx["cmisc"]
    obuf2 = Buf(ap["obuf2"])
    with contextlib.ExitStack() as st2:
        E2 = st2.enter_context
        wthr = Buf(E2(SB(nc, "c_wthr", [128, 256], F32)))
        p.dma("Y", wthr, [], lambda e: e.dma_start(out=wthr.h[:], in_=ap["l0b_wthr"]))
        mks = Buf(E2(SB(nc, "c_mks", [128, 2 * 392], F32)))
        p.dma("Y", mks, [], lambda e: e.dma_start(out=mks.h[:], in_=ap["l0b_masks"]))
        qTl = Buf(E2(SB(nc, "c_qTl", [128, 8 * 128], BF16)))
        Pall = Buf(E2(SB(nc, "c_Pall", [128, 16 * 392], BF16)))
        PgE = {e: Buf(E2(SB(nc, "c_Pg" + e, [128, 392], BF16))) for e in "VG"}
        PgcE = {e: Buf(E2(SB(nc, "c_Pgc" + e, [128, 1], F32))) for e in "VG"}
        maccE = {e: [Buf(E2(SB(nc, f"c_macc{e}{i}", [128, 392], F32))) for i in range(2)]
                 for e in "VG"}
        ysbE = {e: Buf(E2(SB(nc, "c_ysb" + e, [128, D], F32))) for e in "VG"}
        smf = Buf(E2(SB(nc, "c_smf", [128, 392], F32)))
        den = Buf(E2(SB(nc, "c_den", [128, 2], F32)))
        PTE = {e: Buf(E2(SB(nc, "c_PT" + e, [128, 384], BF16))) for e in "VG"}
        obf = Buf(E2(SB(nc, "c_obf", [128, D], BF16)))
        psc = Buf(E2(PS(nc, "c_psc", [128, 512], F32)))
        ptr2 = Buf(E2(PS(nc, "c_ptr2", [128, 512], BF16)))
        pacc = [Buf(E2(PS(nc, f"c_pacc{i}", [128, 128], F32))) for i in range(2)]
        p.dma("G", obuf2, [vbosrep], lambda e: e.dma_start(
            out=obuf2.h[0:1, :], in_=vbosrep.h[0:1, :]))
        for t in range(NT):
            wlo = max(0, 128 * (t - 1))
            whi = min(T, 128 * (t + 2))
            width = whi - wlo
            mko = 0 if t == 0 else 392
            nch = width // 128
            p.dma("Y", qTl, [], [
                (lambda e, t=t, c=c: e.dma_start(
                    out=qTl.h[:, c * 128:(c + 1) * 128],
                    in_=ap["qTd"][c * 128:(c + 1) * 128, t * 128:(t + 1) * 128]))
                for c in range(8)])
            for h in range(16):
                po = (h % 2) * 64
                ch = h // 2
                g = p.group("T").r(qTl, kbosT, kT).w(psc)
                g.add(lambda t_, po=po, ch=ch: t_.matmul(
                    psc.h[:, 0:1], qTl.h[po:po + 64, ch * 128:(ch + 1) * 128],
                    kbosT.h[po:po + 64, ch:ch + 1], start=True, stop=True))
                g.add(lambda t_, po=po, ch=ch, wlo=wlo, width=width: t_.matmul(
                    psc.h[:, 1:1 + width],
                    qTl.h[po:po + 64, ch * 128:(ch + 1) * 128],
                    kT.h[po:po + 64, ch * T + wlo:ch * T + wlo + width],
                    start=True, stop=True))
                g.fin()
                g1(p, "V", [psc, mks], [smf], lambda v, mko=mko, width=width: v.tensor_tensor(
                    smf.h[:, 0:width + 1], psc.h[:, 0:width + 1],
                    mks.h[:, mko:mko + width + 1], ALU.add))
                g1(p, "S", [smf, cmisc], [Pall, den], lambda s, h=h, width=width: s.activation(
                    Pall.h[:, h * 392:h * 392 + width + 1], smf.h[:, 0:width + 1],
                    AF.Exp, bias=cmisc.h[:, 4:5], scale=0.125,
                    accum_out=den.h[:, 0:1]))
                g1(p, "V", [den], [den], lambda v: v.reciprocal(den.h[:, 1:2], den.h[:, 0:1]))
                g1(p, "V", [Pall, den], [Pall], lambda v, h=h, width=width: v.tensor_scalar(
                    Pall.h[:, h * 392:h * 392 + width + 1],
                    Pall.h[:, h * 392:h * 392 + width + 1],
                    den.h[:, 1:2], None, ALU.mult))
            for g_ in range(16):
                eng = "V"
                par2 = "V" if g_ % 2 == 0 else "G"
                macc = maccE[par2]; Pg = PgE[par2]; Pgc = PgcE[par2]; ysbe = ysbE[par2]
                pac = pacc[g_ % 2]
                PTe = PTE[par2]
                g1(p, eng, [Pall, wthr], [macc[0]], lambda v, g_=g_, width=width, macc=macc: v.tensor_scalar(
                    macc[0].h[:, 0:width + 1], Pall.h[:, 0:width + 1],
                    wthr.h[:, g_ * 16:g_ * 16 + 1], None, ALU.mult))
                for h in range(1, 16):
                    g1(p, eng, [Pall, wthr, macc[(h - 1) % 2]], [macc[h % 2]],
                       lambda v, g_=g_, h=h, width=width, macc=macc: v.scalar_tensor_tensor(
                           macc[h % 2].h[:, 0:width + 1],
                           Pall.h[:, h * 392:h * 392 + width + 1],
                           wthr.h[:, g_ * 16 + h:g_ * 16 + h + 1],
                           macc[(h - 1) % 2].h[:, 0:width + 1], ALU.mult, ALU.add))
                g1(p, eng, [macc[1]], [Pg], lambda v, width=width, macc=macc, Pg=Pg: v.tensor_copy(
                    Pg.h[:, 0:width + 1], macc[1].h[:, 0:width + 1]))
                g1(p, eng, [macc[1]], [Pgc], lambda v, macc=macc, Pgc=Pgc: v.tensor_copy(
                    Pgc.h[:], macc[1].h[:, 0:1]))
                g = p.group("T").r(Pg, ctx["ident"]).w(ptr2)
                for c in range(nch):
                    g.add(lambda e, c=c, Pg=Pg: e.transpose(
                        ptr2.h[:, c * 128:(c + 1) * 128],
                        Pg.h[:, 1 + c * 128:1 + (c + 1) * 128], ctx["ident"].h[:]))
                g.fin()
                g1(p, "S", [ptr2], [PTe], lambda ss, nch=nch, PTe=PTe: ss.copy(
                    PTe.h[:, 0:nch * 128], ptr2.h[:, 0:nch * 128]))
                g = p.group("T").r(PTe, vb).w(pac)
                t0 = wlo // 128
                for c in range(nch):
                    g.add(lambda t_, c=c, t0=t0, g_=g_, nch=nch, PTe=PTe, pac=pac: t_.matmul(
                        pac.h[:, 0:64], PTe.h[:, c * 128:(c + 1) * 128],
                        vb.h[:, (t0 + c) * 1024 + g_ * 64:(t0 + c) * 1024 + g_ * 64 + 64],
                        start=(c == 0), stop=(c == nch - 1)))
                g.fin()
                g1(p, "V", [vbosrep, Pgc, pac], [ysbe],
                   lambda v, g_=g_, Pgc=Pgc, pac=pac, ysbe=ysbe: v.scalar_tensor_tensor(
                       ysbe.h[:, g_ * 64:(g_ + 1) * 64], vbosrep.h[:, g_ * 64:(g_ + 1) * 64],
                       Pgc.h[:], pac.h[:, 0:64], ALU.mult, ALU.add))
            g1(p, "V", [ysbE["V"]], [obf], lambda v: v.tensor_copy(
                ap3(obf.h, 0, [[1024, 128], [128, 8], [1, 64]]),
                ap3(ysbE["V"].h, 0, [[1024, 128], [128, 8], [1, 64]])))
            g1(p, "V", [ysbE["G"]], [obf], lambda v: v.tensor_copy(
                ap3(obf.h, 64, [[1024, 128], [128, 8], [1, 64]]),
                ap3(ysbE["G"].h, 64, [[1024, 128], [128, 8], [1, 64]])))
            if t < NT - 1:
                p.dma("G", obuf2, [obf], lambda e, t=t: e.dma_start(
                    out=obuf2.h[t * 128 + 1:t * 128 + 129, :], in_=obf.h[:]))
            else:
                p.dma("G", obuf2, [obf], lambda e, t=t: e.dma_start(
                    out=obuf2.h[t * 128 + 1:t * 128 + 128, :], in_=obf.h[0:127, :]))
        p.barrier()


def _l0b_out(nc, p, ap, ctx, xin, xout, vbosrep):
    obuf2 = Buf(ap["obuf2"])
    with contextlib.ExitStack() as st2:
        E2 = st2.enter_context
        wsb = load_w_bf16(nc, p, st2, ap, "l0b_wout", D, D)
        br = Buf(E2(SB(nc, "c_obr", [128, D], F32)))
        p.dma("Y", br, [], lambda e: e.dma_start(out=br.h[:], in_=ap["l0b_bout"]))
        ob = [Buf(E2(SB(nc, "c_oob", [128, D], BF16)))] * 2
        xr = Buf(E2(SB(nc, "c_oxr", [128, D], F32)))
        oT = Buf(E2(SB(nc, "c_ooT", [128, D], BF16)))
        yn = Buf(E2(SB(nc, "c_oyn", [128, D], F32)))
        ptr = Buf(E2(PS(nc, "c_optr", [128, D], BF16)))
        pmm = Buf(E2(PS(nc, "c_opmm", [128, D], F32)))
        for t in range(NT):
            i = t % 2
            p.dma("Y", ob[i], [obuf2], lambda e, t=t, i=i: e.dma_start(
                out=ob[i].h[:], in_=obuf2.h[t * 128:(t + 1) * 128, :]))
            p.dma("Y", xr, [xin], lambda e, t=t: e.dma_start(
                out=xr.h[:], in_=xin.h[t * 128:(t + 1) * 128, :]))
            transposes(p, ctx, ob[i], ptr, oT, 8)
            mm_proj(p, oT, wsb, pmm, D)
            g1(p, "V", [pmm, br], [yn], lambda v: v.tensor_tensor(
                yn.h[:], pmm.h[:], br.h[:], ALU.add))
            g1(p, "V", [yn, xr], [yn], lambda v: v.tensor_tensor(
                yn.h[:], yn.h[:], xr.h[:], ALU.add))
            p.dma("G", xout, [yn], lambda e, t=t: e.dma_start(
                out=xout.h[t * 128:(t + 1) * 128, :], in_=yn.h[:]))
        p.barrier()


def phase_l1b(nc, p, st, ap, ctx, xin, xout):
  with contextlib.ExitStack() as stp:
    Ep = stp.enter_context
    kT = Buf(Ep(SB(nc, "b_kT", [128, 8 * T], BF16)))
    vb = Buf(Ep(SB(nc, "b_vb", [128, NT * 1152], BF16)))
    Vs = Buf(Ep(SB(nc, "b_Vs", [NT, 1152], BF16)))
    cmisc = ctx["cmisc"]
    qTd = Buf(ap["qTd"])
    g1(p, "V", [cmisc], [vb], lambda v: v.tensor_copy(
        ap3(vb.h, 64, [[NT * 1152, 128], [1152, NT], [72, 16], [1, 8]]),
        ap3(cmisc.h, 3, [[16, 128], [0, NT], [0, 16], [0, 8]])))

    for proj in ("k", "q", "v"):
        _l1b_proj(nc, p, ap, ctx, xin, proj, kT, vb, Vs, qTd)
    _l1b_attn(nc, p, ap, ctx, xin, kT, vb, Vs)
    _l1b_out(nc, p, ap, ctx, xin, xout)


def _l1b_proj(nc, p, ap, ctx, xin, proj, kT, vb, Vs, qTd):
    cmisc = ctx["cmisc"]
    with contextlib.ExitStack() as st2:
        E2 = st2.enter_context
        wsb = load_w_bf16(nc, p, st2, ap, "l1b_w" + proj, D, D)
        br = Buf(E2(SB(nc, "b_br", [128, D], F32)))
        p.dma("Y", br, [], lambda e: e.dma_start(out=br.h[:], in_=ap["l1b_b" + proj]))
        onec = Buf(E2(SB(nc, "b_onec", [128, 8], BF16)))
        g1(p, "V", [cmisc], [onec], lambda v: v.tensor_copy(
            onec.h[:], ap3(cmisc.h, 3, [[16, 128], [0, 8]])))
        xf = [Buf(E2(SB(nc, "b_xf0", [128, D], F32)))] * 2
        stats = Buf(E2(SB(nc, "b_st", [128, 16], F32)))
        xn = Buf(E2(SB(nc, "b_xn", [128, D], BF16)))
        xnT = Buf(E2(SB(nc, "b_xnT", [128, D], BF16)))
        tmpf = Buf(E2(SB(nc, "b_tmpf", [128, D], F32)))
        osb = Buf(E2(SB(nc, "b_osb", [128, D], BF16)))
        vrow = Buf(E2(SB(nc, "b_vrow", [1, 1152], BF16)))
        ptr = Buf(E2(PS(nc, "b_ptr", [128, D], BF16)))
        pmm = Buf(E2(PS(nc, "b_pmm", [128, D], F32)))
        pvs = Buf(E2(PS(nc, "b_pvs", [1, 1152], F32)))
        for t in range(NT):
            i = t % 2
            p.dma("Y", xf[i], [xin], lambda e, t=t, i=i: e.dma_start(
                out=xf[i].h[:], in_=xin.h[t * 128:(t + 1) * 128, :]))
            emit_ln_tile(p, ctx, xf[i], stats, xn)
            transposes(p, ctx, xn, ptr, xnT, 8)
            mm_proj(p, xnT, wsb, pmm, D)
            g1(p, "V", [pmm, br], [tmpf], lambda v: v.tensor_tensor(
                tmpf.h[:], pmm.h[:], br.h[:], ALU.add))
            if proj == "v":
                g1(p, "V", [tmpf], [vb], lambda v, t=t: v.tensor_copy(
                    ap3(vb.h, t * 1152, [[NT * 1152, 128], [72, 16], [1, 64]]),
                    ap3(tmpf.h, 0, [[1024, 128], [64, 16], [1, 64]])))
                g = p.group("T").r(vb, onec).w(pvs)
                for (o, wd) in ((0, 512), (512, 512), (1024, 128)):
                    g.add(lambda t_, t=t, o=o, wd=wd: t_.matmul(
                        pvs.h[:, o:o + wd], onec.h[:, 0:1],
                        vb.h[:, t * 1152 + o:t * 1152 + o + wd],
                        start=True, stop=True))
                g.fin()
                g1(p, "S", [pvs], [vrow], lambda s: s.copy(vrow.h[:], pvs.h[0:1, :]))
                p.dma("G", Vs, [vrow], lambda e, t=t: e.dma_start(
                    out=Vs.h[t:t + 1, :], in_=vrow.h[:]))
                continue
            g1(p, "V", [tmpf], [osb], lambda v: v.tensor_copy(osb.h[:], tmpf.h[:]))
            g = p.group("T").r(osb, ctx["ident"]).w(ptr)
            for c in range(8):
                g.add(lambda e, c=c: e.transpose(ptr.h[:, c * 128:(c + 1) * 128],
                                                 osb.h[:, c * 128:(c + 1) * 128],
                                                 ctx["ident"].h[:]))
            g.fin()
            if proj == "k":
                g1(p, "V", [ptr], [kT], lambda v, t=t: v.tensor_copy(
                    ap3(kT.h, t * 128, [[8 * T, 128], [T, 8], [1, 128]]),
                    ap3(ptr.h, 0, [[1024, 128], [128, 8], [1, 128]])))
            else:
                g1(p, "V", [ptr], [osb], lambda v: v.tensor_copy(osb.h[:], ptr.h[:]))
                p.dma("G", qTd, [osb], [
                    (lambda e, t=t, c=c: e.dma_start(
                        out=qTd.h[c * 128:(c + 1) * 128, t * 128:(t + 1) * 128],
                        in_=osb.h[:, c * 128:(c + 1) * 128]))
                    for c in range(8)])
        p.barrier()


def _l1b_attn(nc, p, ap, ctx, xin, kT, vb, Vs):
    cmisc = ctx["cmisc"]
    maskF = ctx["maskF"]
    utc = ctx["utc"]
    u32 = ctx["u32"]
    ydr = Buf(ap["ybB"])
    with contextlib.ExitStack() as st2:
        E2 = st2.enter_context
        qTg = Buf(E2(SB(nc, "b_qTg", [128, 8 * 512], BF16)))
        futv = Buf(E2(SB(nc, "b_futv", [128, 4 * 1152], F32)))
        pT = [Buf(E2(SB(nc, f"b_pT{i}", [128, 512], BF16))) for i in range(2)]
        nf = Buf(E2(SB(nc, "b_nf", [128, 64], F32)))
        dn = Buf(E2(SB(nc, "b_dn", [128, 2], F32)))
        ysb = [Buf(E2(SB(nc, f"b_ysb{i}", [128, D], F32))) for i in range(4)]
        psT = [Buf(E2(PS(nc, f"b_psT{i}", [128, 512], F32))) for i in range(2)]
        pacc = [Buf(E2(PS(nc, f"b_pacc{i}", [128, 128], F32))) for i in range(4)]
        pfut = Buf(E2(PS(nc, "b_pfut", [128, 576], F32)))
        for qg in range(T // 512):
            p.dma("Y", qTg, [], [
                (lambda e, qg=qg, c=c: e.dma_start(
                    out=qTg.h[:, c * 512:(c + 1) * 512],
                    in_=ap["qTd"][c * 128:(c + 1) * 128, qg * 512:(qg + 1) * 512]))
                for c in range(8)])
            for qc in range(4):
                qt = 4 * qg + qc
                for half in range(2):
                    hb = half * 576
                    g = p.group("T").r(utc, vb, u32, Vs).w(pfut)
                    for (o, wd) in ((0, 512), (512, 64)):
                        g.add(lambda t_, qt=qt, o=o, wd=wd, hb=hb: t_.matmul(
                            pfut.h[:, o:o + wd], utc.h[:],
                            vb.h[:, qt * 1152 + hb + o:qt * 1152 + hb + o + wd],
                            start=True, stop=False))
                        g.add(lambda t_, qt=qt, o=o, wd=wd, hb=hb: t_.matmul(
                            pfut.h[:, o:o + wd],
                            ap3(u32.h, qt, [[NT, NT], [0, 128]]),
                            Vs.h[:, hb + o:hb + o + wd],
                            start=False, stop=True))
                    g.fin()
                    g1(p, "S", [pfut], [futv], lambda s, qc=qc, hb=hb: s.activation(
                        futv.h[:, qc * 1152 + hb:qc * 1152 + hb + 576], pfut.h[:, 0:576],
                        AF.Copy, scale=EM8))
            for h in range(16):
                po = (h % 2) * 64
                ch = h // 2
                def emit_avb(kt, h=h, qg=qg):
                    par = kt % 2
                    j = kt - 4 * qg
                    for qc in range(max(0, j), 4):
                        qt = 4 * qg + qc
                        g1(p, "T", [pT[par], vb], [pacc[qc]],
                           lambda t_, qc=qc, kt=kt, h=h, qt=qt, par=par: t_.matmul(
                               pacc[qc].h[:, 0:72],
                               pT[par].h[:, qc * 128:qc * 128 + 128],
                               vb.h[:, kt * 1152 + h * 72:kt * 1152 + h * 72 + 72],
                               start=(kt == 0), stop=(kt == qt)))
                for kt in range(4 * qg + 4):
                    par = kt % 2
                    g1(p, "T", [kT, qTg], [psT[par]], lambda t_, po=po, ch=ch, kt=kt, par=par: t_.matmul(
                        psT[par].h[:],
                        kT.h[po:po + 64, ch * T + kt * 128:ch * T + kt * 128 + 128],
                        qTg.h[po:po + 64, ch * 512:(ch + 1) * 512],
                        start=True, stop=True))
                    g1(p, "S", [psT[par], cmisc], [pT[par]], lambda s, par=par: s.activation(
                        pT[par].h[:], psT[par].h[:], AF.Exp, bias=cmisc.h[:, 4:5], scale=0.125))
                    j = kt - 4 * qg
                    if j >= 0:
                        g1(p, "V", [maskF, pT[par]], [pT[par]], lambda v, j=j, par=par: v.tensor_tensor(
                            pT[par].h[:], pT[par].h[:], maskF.h[:, j * 512:(j + 1) * 512], ALU.mult))
                    if kt > 0:
                        emit_avb(kt - 1)
                emit_avb(4 * qg + 3)
                for qc in range(4):
                    g1(p, "V", [pacc[qc], futv], [dn], lambda v, qc=qc, h=h: v.tensor_tensor(
                        dn.h[:, 0:1], pacc[qc].h[:, 64:65],
                        futv.h[:, qc * 1152 + h * 72 + 64:qc * 1152 + h * 72 + 65], ALU.add))
                    g1(p, "V", [dn], [dn], lambda v: v.reciprocal(dn.h[:, 1:2], dn.h[:, 0:1]))
                    g1(p, "V", [pacc[qc], futv], [nf], lambda v, qc=qc, h=h: v.tensor_tensor(
                        nf.h[:], pacc[qc].h[:, 0:64],
                        futv.h[:, qc * 1152 + h * 72:qc * 1152 + h * 72 + 64], ALU.add))
                    g1(p, "V", [nf, dn], [ysb[qc]], lambda v, qc=qc, h=h: v.tensor_scalar(
                        ysb[qc].h[:, h * 64:(h + 1) * 64], nf.h[:], dn.h[:, 1:2],
                        None, ALU.mult))
            for qc in range(4):
                qt = 4 * qg + qc
                p.dma("G", ydr, [ysb[qc]], lambda e, qt=qt, qc=qc: e.dma_start(
                    out=ydr.h[qt * 128:(qt + 1) * 128, :], in_=ysb[qc].h[:]))
        p.barrier()


def _l1b_out(nc, p, ap, ctx, xin, xout):
    ydr = Buf(ap["ybB"])
    with contextlib.ExitStack() as st2:
        E2 = st2.enter_context
        wsb = load_w_bf16(nc, p, st2, ap, "l1b_wout", D, D)
        br = Buf(E2(SB(nc, "b_bro", [128, D], F32)))
        p.dma("Y", br, [], lambda e: e.dma_start(out=br.h[:], in_=ap["l1b_bout"]))
        yf = [Buf(E2(SB(nc, "b_oyf", [128, D], F32)))] * 2
        xr = Buf(E2(SB(nc, "b_oxr", [128, D], F32)))
        ybf = Buf(E2(SB(nc, "b_oybf", [128, D], BF16)))
        yT = Buf(E2(SB(nc, "b_oyT", [128, D], BF16)))
        yn = Buf(E2(SB(nc, "b_oyn", [128, D], F32)))
        ptr = Buf(E2(PS(nc, "b_optr", [128, D], BF16)))
        pmm = Buf(E2(PS(nc, "b_opmm", [128, D], F32)))
        for t in range(NT):
            i = t % 2
            p.dma("Y", yf[i], [ydr], lambda e, t=t, i=i: e.dma_start(
                out=yf[i].h[:], in_=ydr.h[t * 128:(t + 1) * 128, :]))
            p.dma("Y", xr, [xin], lambda e, t=t: e.dma_start(
                out=xr.h[:], in_=xin.h[t * 128:(t + 1) * 128, :]))
            g1(p, "V", [yf[i]], [ybf], lambda v, i=i: v.tensor_copy(ybf.h[:], yf[i].h[:]))
            transposes(p, ctx, ybf, ptr, yT, 8)
            mm_proj(p, yT, wsb, pmm, D)
            g1(p, "V", [pmm, br], [yn], lambda v: v.tensor_tensor(
                yn.h[:], pmm.h[:], br.h[:], ALU.add))
            g1(p, "V", [yn, xr], [yn], lambda v: v.tensor_tensor(
                yn.h[:], yn.h[:], xr.h[:], ALU.add))
            p.dma("G", xout, [yn], lambda e, t=t: e.dma_start(
                out=xout.h[t * 128:(t + 1) * 128, :], in_=yn.h[:]))
        p.barrier()


def phase_l1f(nc, p, st, ap, ctx, xin, xout):
    _l1f_half(nc, p, st, ap, ctx, xin, Buf(ap["ybB"]), range(0, 4), first=True)
    _l1f_half(nc, p, st, ap, ctx, xin, xout, range(4, 8), first=False, partial=Buf(ap["ybB"]))


def _l1f_half(nc, p, st, ap, ctx, xin, xout, erange, first, partial=None):
    cmisc = ctx["cmisc"]
    gsave = Buf(ap["l1f_gsave"])
    with contextlib.ExitStack() as s2:
        E2 = s2.enter_context
        ew = [load_w_bf16(nc, p, s2, ap, f"l1f_ew{e}", D, D) for e in erange]
        if first:
            gwr = Buf(E2(SB(nc, "m_gwr", [128, 8192], F32)))
            p.dma("Y", gwr, [], lambda e: e.dma_start(out=gwr.h[:], in_=ap["l1f_gwr"]))
            gbr = Buf(E2(SB(nc, "m_gbr", [128, 8], F32)))
            p.dma("Y", gbr, [], lambda e: e.dma_start(out=gbr.h[:], in_=ap["l1f_gb"]))
        ebr = Buf(E2(SB(nc, "m_ebr", [128, 4096], F32)))
        p.dma("Y", ebr, [], lambda e, e0=erange[0]: e.dma_start(
            out=ebr.h[:], in_=ap["l1f_eb"][:, e0 * D:(e0 + 4) * D]))
        xf = [Buf(E2(SB(nc, f"m_xf{i}", [128, D], F32))) for i in range(2)]
        pf = Buf(E2(SB(nc, "m_pf", [128, D], F32)))
        stats = Buf(E2(SB(nc, "m_st", [128, 16], F32)))
        xnf = Buf(E2(SB(nc, "m_xnf", [128, D], F32)))
        gt = Buf(E2(SB(nc, "m_gt", [128, D], F32)))
        xn = Buf(E2(SB(nc, "m_xn", [128, D], BF16)))
        xnT = Buf(E2(SB(nc, "m_xnT", [128, D], BF16)))
        gf = Buf(E2(SB(nc, "m_gf", [128, 32], F32)))
        acc = [Buf(E2(SB(nc, f"m_acc{i}", [128, D], F32))) for i in range(2)]
        tmp = Buf(E2(SB(nc, "m_tmp", [128, D], F32)))
        ptr = Buf(E2(PS(nc, "m_ptr", [128, D], BF16)))
        pe = [Buf(E2(PS(nc, f"m_pe{i}", [128, D], F32))) for i in range(2)]
        for t in range(NT):
            i = t % 2
            p.dma("Y", xf[i], [xin], lambda e, t=t, i=i: e.dma_start(
                out=xf[i].h[:], in_=xin.h[t * 128:(t + 1) * 128, :]))
            emit_ln_tile(p, ctx, xf[i], stats, xnf)
            g1(p, "V", [xnf], [xn], lambda v: v.tensor_copy(xn.h[:], xnf.h[:]))
            transposes(p, ctx, xn, ptr, xnT, 8)
            if first:
                for e in range(8):
                    g1(p, "V", [xnf, gwr], [gt], lambda v, e=e: v.tensor_tensor(
                        gt.h[:], xnf.h[:], gwr.h[:, e * D:(e + 1) * D], ALU.mult))
                    g1(p, "V", [gt], [gf], lambda v, e=e: v.reduce_sum(
                        gf.h[:, 24 + e:25 + e], gt.h[:], axis=AX.X))
                g1(p, "V", [gf, gbr], [gf], lambda v: v.tensor_tensor(
                    gf.h[:, 0:8], gf.h[:, 24:32], gbr.h[:], ALU.add))
                g1(p, "V", [gf], [gf], lambda v: v.reduce_max(gf.h[:, 16:17], gf.h[:, 0:8], axis=AX.X))
                g1(p, "V", [gf], [gf], lambda v: v.tensor_single_scalar(
                    gf.h[:, 17:18], gf.h[:, 16:17], -1.0, ALU.mult))
                g1(p, "S", [gf], [gf], lambda s: s.activation(
                    gf.h[:, 0:8], gf.h[:, 0:8], AF.Exp, bias=gf.h[:, 17:18],
                    accum_out=gf.h[:, 18:19]))
                g1(p, "V", [gf], [gf], lambda v: v.reciprocal(gf.h[:, 19:20], gf.h[:, 18:19]))
                g1(p, "V", [gf], [gf], lambda v: v.tensor_scalar(
                    gf.h[:, 0:8], gf.h[:, 0:8], gf.h[:, 19:20], None, ALU.mult))
                g1(p, "V", [gf], [gf], lambda v: v.tensor_single_scalar(
                    gf.h[:, 8:16], gf.h[:, 0:8], 0.1, ALU.is_ge))
                g1(p, "V", [gf], [gf], lambda v: v.tensor_tensor(
                    gf.h[:, 8:16], gf.h[:, 8:16], gf.h[:, 0:8], ALU.mult))
                g1(p, "V", [gf], [gf], lambda v: v.reduce_sum(gf.h[:, 20:21], gf.h[:, 8:16], axis=AX.X))
                g1(p, "V", [gf], [gf], lambda v: v.tensor_single_scalar(
                    gf.h[:, 21:22], gf.h[:, 20:21], 0.0, ALU.is_equal))
                g1(p, "V", [gf], [gf], lambda v: v.tensor_tensor(
                    gf.h[:, 20:21], gf.h[:, 20:21], gf.h[:, 21:22], ALU.add))
                g1(p, "V", [gf], [gf], lambda v: v.reciprocal(gf.h[:, 22:23], gf.h[:, 20:21]))
                g1(p, "V", [gf], [gf], lambda v: v.tensor_scalar(
                    gf.h[:, 8:16], gf.h[:, 8:16], gf.h[:, 22:23], None, ALU.mult))
                p.dma("G", gsave, [gf], lambda e, t=t: e.dma_start(
                    out=gsave.h[t * 128:(t + 1) * 128, :], in_=gf.h[:, 8:16]))
            else:
                p.dma("Y", gf, [gsave], lambda e, t=t: e.dma_start(
                    out=gf.h[:, 8:16], in_=gsave.h[t * 128:(t + 1) * 128, :]))
                p.dma("Y", pf, [partial], lambda e, t=t: e.dma_start(
                    out=pf.h[:], in_=partial.h[t * 128:(t + 1) * 128, :]))
            for j, e in enumerate(erange):
                jj = j % 2
                mm_proj(p, xnT, ew[j], pe[jj], D)
                g1(p, "V", [pe[jj], ebr], [tmp], lambda v, jj=jj, j=j: v.tensor_tensor(
                    tmp.h[:], pe[jj].h[:], ebr.h[:, j * D:(j + 1) * D], ALU.add))
                if j == 0:
                    base = xf[i] if first else pf
                    g1(p, "V", [tmp, gf, base], [acc[0]],
                       lambda v, base=base, e=e: v.scalar_tensor_tensor(
                           acc[0].h[:], tmp.h[:], gf.h[:, 8 + e:9 + e], base.h[:],
                           ALU.mult, ALU.add))
                else:
                    g1(p, "V", [tmp, gf, acc[(j - 1) % 2]], [acc[j % 2]],
                       lambda v, j=j, e=e: v.scalar_tensor_tensor(
                           acc[j % 2].h[:], tmp.h[:], gf.h[:, 8 + e:9 + e],
                           acc[(j - 1) % 2].h[:], ALU.mult, ALU.add))
            p.dma("G", xout, [acc[1]], lambda e_, t=t: e_.dma_start(
                out=xout.h[t * 128:(t + 1) * 128, :], in_=acc[1].h[:]))
        p.barrier()


def finalize_copy(nc, p, st, src, dst):
    with contextlib.ExitStack() as st2:
        tb = [Buf(st2.enter_context(SB(nc, f"fc{i}", [128, D], F32))) for i in range(2)]
        for t in range(NT):
            b = tb[t % 2]
            p.dma("Y", b, [src], lambda e, t=t, b=b: e.dma_start(
                out=b.h[:], in_=src.h[t * 128:(t + 1) * 128, :]))
            p.dma("G", dst, [b], lambda e, t=t, b=b: e.dma_start(
                out=dst.h[t * 128:(t + 1) * 128, :], in_=b.h[:]))
        p.barrier()


def build_nc():
    nc = bass.Bass("TRN2", target_bir_lowering=False)
    ap = {}
    def inp(name, shape, dt=F32):
        ap[name] = nc.dram_tensor(name, list(shape), dt, kind="ExternalInput").ap()
    def scr(name, shape, dt=F32):
        ap[name] = nc.dram_tensor(name, list(shape), dt).ap()

    inp("x", (T, D))
    inp("l0a_wq", (D, D)); inp("l0a_wk", (D, D)); inp("l0a_wv", (D, D))
    for li in range(2):
        inp(f"l0f_wqkv{li}", (D, 1536)); inp(f"l0f_wo{li}", (512, D))
        inp(f"l0f_w1{li}", (D, 5460)); inp(f"l0f_w2{li}", (2730, D))
        inp(f"l0f_qs{li}", (128, 512)); inp(f"l0f_ks{li}", (128, 512))
        inp(f"l0f_lg{li}", (128, D)); inp(f"l0f_lb{li}", (128, D))
        inp(f"l0f_b1{li}", (128, 5460)); inp(f"l0f_b2{li}", (128, D))
    for nm in ("wk", "wq", "wv", "wout"):
        inp("l0b_" + nm, (D, D))
    inp("l0b_bout", (128, D)); inp("l0b_wthr", (128, 256)); inp("l0b_masks", (128, 784))
    for nm in ("wk", "wq", "wv", "wout"):
        inp("l1b_" + nm, (D, D))
    for nm in ("bk", "bq", "bv", "bout"):
        inp("l1b_" + nm, (128, D))
    inp("c_utc", (128, 128))
    inp("c_zrow", (1, D))
    inp("c_u32", (NT, NT))
    for nm in ("wqT", "wkT", "wvT", "woT"):
        inp("l1a_" + nm, (D, D))
    for nm in ("bq", "bk", "bv", "bo"):
        inp("l1a_" + nm, (128, D))
    inp("l1f_gwr", (128, 8192)); inp("l1f_gb", (128, 8)); inp("l1f_eb", (128, 8192))
    for e in range(8):
        inp(f"l1f_ew{e}", (D, D))
    inp("c_ident", (128, 128))
    inp("c_misc", (128, 16))
    inp("c_maskF", (128, 4 * 512))
    inp("c_maskW", (128, 8 * 512))
    for nm in ("c_csq", "c_snq", "c_csk", "c_snk"):
        inp(nm, (T, 512))
    out_h = nc.dram_tensor("out", [T, D], F32, kind="ExternalOutput").ap()
    scr("sA", (T, D)); scr("sB", (T, D))
    scr("ybA", (T, D)); scr("ybB", (T, D))
    scr("qTd", (D, T), BF16)
    scr("l1f_gsave", (T, 8))
    scr("obuf", (T, 512), BF16)
    scr("obuf2", (T, D), BF16)
    scr("vbos_d", (1, D), BF16)
    scr("hmT", (2816, T), BF16)

    with contextlib.ExitStack() as st:
        p = Prog(nc, st)
        E = st.enter_context
        ident = Buf(E(SB(nc, "ident", [128, 128], BF16)))
        cmisc = Buf(E(SB(nc, "cmisc", [128, 16], F32)))
        maskF = Buf(E(SB(nc, "maskF", [128, 2048], BF16)))
        maskW = Buf(E(SB(nc, "maskW", [128, 4096], F32)))
        identf = Buf(E(SB(nc, "identf", [128, 128], F32)))
        utc = Buf(E(SB(nc, "utc", [128, 128], BF16)))
        u32 = Buf(E(SB(nc, "u32", [NT, NT], BF16)))
        p.dma("G", cmisc, [], lambda e: e.dma_start(out=cmisc.h[:], in_=ap["c_misc"]))
        p.dma("G", identf, [], lambda e: e.dma_start(out=identf.h[:], in_=ap["c_ident"]))
        g1(p, "V", [identf], [ident], lambda v: v.tensor_copy(ident.h[:], identf.h[:]))
        with contextlib.ExitStack() as s2:
            mtmp = Buf(s2.enter_context(SB(nc, "mtmp", [128, 4096], F32)))
            p.dma("G", mtmp, [], lambda e: e.dma_start(out=mtmp.h[:, 0:2048], in_=ap["c_maskF"]))
            g1(p, "V", [mtmp], [maskF], lambda v: v.tensor_copy(maskF.h[:], mtmp.h[:, 0:2048]))
            p.dma("G", mtmp, [maskF], lambda e: e.dma_start(out=mtmp.h[:, 0:128], in_=ap["c_utc"]))
            g1(p, "V", [mtmp], [utc], lambda v: v.tensor_copy(utc.h[:], mtmp.h[:, 0:128]))
            p.dma("G", mtmp, [utc], lambda e: e.dma_start(out=mtmp.h[0:NT, 0:NT], in_=ap["c_u32"]))
            g1(p, "V", [mtmp], [u32], lambda v: v.tensor_copy(u32.h[:], mtmp.h[0:NT, 0:NT]))
            p.barrier()
        p.dma("G", maskW, [], lambda e: e.dma_start(out=maskW.h[:], in_=ap["c_maskW"]))

        ctx = dict(ident=ident, cmisc=cmisc, maskF=maskF, maskW=maskW, utc=utc, u32=u32)

        bufs = {"x": Buf(ap["x"]), "sA": Buf(ap["sA"]), "sB": Buf(ap["sB"]),
                "out": Buf(out_h)}
        chain = [ph for ph in PHASES if ph in ENABLED]
        cur = "x"
        nxt_cycle = ["sA", "sB"]
        for i, ph in enumerate(chain):
            tgt = "out" if i == len(chain) - 1 else nxt_cycle[i % 2]
            if ph == "l0a":
                phase_l0a(nc, p, st, ap, ctx, bufs[cur], bufs[tgt])
            elif ph in ("l0f0", "l0f1"):
                phase_l0f(nc, p, st, ap, ctx, bufs[cur], bufs[tgt])
            elif ph == "l0b":
                phase_l0b(nc, p, st, ap, ctx, bufs[cur], bufs[tgt])
            elif ph == "l1b":
                phase_l1b(nc, p, st, ap, ctx, bufs[cur], bufs[tgt])
            elif ph == "l1a":
                phase_l1a(nc, p, st, ap, ctx, bufs[cur], bufs[tgt])
            elif ph in ("l1f0", "l1f1"):
                phase_l1f(nc, p, st, ap, ctx, bufs[cur], bufs[tgt])
            else:
                raise NotImplementedError(ph)
            cur = tgt
        if cur != "out":
            finalize_copy(nc, p, st, bufs[cur], bufs["out"])
        p.barrier()
        blk = st.enter_context(nc.Block())
        p.flush(blk)
    return nc


def host_consts():
    c = {}
    c["c_ident"] = np.eye(128, dtype=np.float32)
    mc = np.zeros((128, 16), dtype=np.float32)
    mc[:, 0] = 1e-5; mc[:, 1] = -12.0; mc[:, 2] = 0.0; mc[:, 3] = 1.0
    mc[:, 4] = -8.0; mc[:, 5] = 1e-24
    c["c_misc"] = mc
    c["c_zrow"] = np.zeros((1, D), dtype=np.float32)
    mf = np.zeros((128, 4 * 512), dtype=np.float32)
    kl = np.arange(128)[:, None]
    qf = np.arange(512)[None, :]
    for j in range(4):
        mf[:, j * 512:(j + 1) * 512] = (kl + 128 * j <= qf).astype(np.float32)
    c["c_maskF"] = mf
    mw = np.zeros((128, 8 * 512), dtype=np.float32)
    il = np.arange(512)[None, :]
    for kt in range(8):
        jl = kt * 128 + np.arange(128)[:, None]
        mw[:, kt * 512:(kt + 1) * 512] = np.where((jl - 512 <= il) & (il <= jl), 0.0, -1e30).astype(np.float32)
    c["c_maskW"] = mw
    mk = np.full((128, 784), -1e30, dtype=np.float32)
    ql = np.arange(128)[:, None]
    j = np.arange(384)[None, :]
    a0 = (j[:, :256] <= ql) & (np.abs(j[:, :256] // 16 - ql // 16) <= 2)
    mk[:, 0:1] = 0.0
    mk[:, 1:257] = np.where(a0, 0.0, -1e30)
    a1 = (j <= 128 + ql) & (6 <= j // 16 - ql // 16) & (j // 16 - ql // 16 <= 10)
    mk[:, 392:393] = 0.0
    mk[:, 393:777] = np.where(a1, 0.0, -1e30)
    c["l0b_masks_const"] = mk
    sgrid = np.arange(128)
    c["c_utc"] = (sgrid[:, None] > sgrid[None, :]).astype(np.float32)
    tg = np.arange(NT)
    c["c_u32"] = (tg[:, None] > tg[None, :]).astype(np.float32)
    half = np.arange(0, 64, 2, dtype=np.float64)
    inv_freq = 1.0 / 10000.0 ** (half / 64.0)
    tt = np.arange(T, dtype=np.float64)
    freqs = tt[:, None] * inv_freq[None, :]
    freqs = np.concatenate([freqs, freqs], axis=-1)
    base = (half + 0.4 * 64.0) / (1.4 * 64.0)
    power = (tt - T // 2) / 256.0
    scale = base[None, :] ** power[:, None]
    scale = np.concatenate([scale, scale], axis=-1)
    cos, sin = np.cos(freqs), np.sin(freqs)
    c["c_csq"] = np.tile((cos * scale).astype(np.float32), (1, 8))
    c["c_snq"] = np.tile((sin * scale).astype(np.float32), (1, 8))
    c["c_csk"] = np.tile((cos / scale).astype(np.float32), (1, 8))
    c["c_snk"] = np.tile((sin / scale).astype(np.float32), (1, 8))
    return c


def stage_l0f(m, inputs):
    for li in range(2):
        m[f"l0f_wqkv{li}"] = np.ascontiguousarray(inputs["L0_ffn_wqkv"][li], np.float32)
        m[f"l0f_wo{li}"] = np.ascontiguousarray(inputs["L0_ffn_wo"][li], np.float32)
        m[f"l0f_w1{li}"] = np.ascontiguousarray(inputs["L0_ffn_w1"][li], np.float32)
        m[f"l0f_w2{li}"] = np.ascontiguousarray(inputs["L0_ffn_w2"][li], np.float32)
        m[f"l0f_qs{li}"] = np.ascontiguousarray(np.broadcast_to(
            np.tile(np.asarray(inputs["L0_ffn_q_scale"][li], np.float32), 8)[None, :], (128, 512)))
        m[f"l0f_ks{li}"] = np.ascontiguousarray(np.broadcast_to(
            np.tile(np.asarray(inputs["L0_ffn_k_scale"][li], np.float32), 8)[None, :], (128, 512)))
        m[f"l0f_lg{li}"] = np.ascontiguousarray(np.broadcast_to(
            np.asarray(inputs["L0_ffn_ln_g"][li], np.float32)[None, :], (128, D)))
        m[f"l0f_lb{li}"] = np.ascontiguousarray(np.broadcast_to(
            np.asarray(inputs["L0_ffn_ln_b"][li], np.float32)[None, :], (128, D)))
        m[f"l0f_b1{li}"] = np.ascontiguousarray(np.broadcast_to(
            np.asarray(inputs["L0_ffn_b1"][li], np.float32)[None, :], (128, 5460)))
        m[f"l0f_b2{li}"] = np.ascontiguousarray(np.broadcast_to(
            np.asarray(inputs["L0_ffn_b2"][li], np.float32)[None, :], (128, D)))


def kernel(**inputs):
    nc = build_nc()
    consts = host_consts()
    masks_const = consts.pop("l0b_masks_const")
    x_full = np.ascontiguousarray(inputs["L0_attn_a_x"], dtype=np.float32)
    in_maps = []
    for core in range(8):
        m = dict(consts)
        m["x"] = np.ascontiguousarray(x_full[core])
        m["l0a_wq"] = np.ascontiguousarray(inputs["L0_attn_a_w_q"], np.float32)
        m["l0a_wk"] = np.ascontiguousarray(inputs["L0_attn_a_w_k"], np.float32)
        m["l0a_wv"] = np.ascontiguousarray(inputs["L0_attn_a_w_v"], np.float32)
        stage_l0f(m, inputs)
        wqkv_b = np.asarray(inputs["L0_attn_b_w_qkv"], np.float32)
        m["l0b_wq"] = np.ascontiguousarray(wqkv_b[:, 0:1024])
        m["l0b_wk"] = np.ascontiguousarray(wqkv_b[:, 1024:2048])
        m["l0b_wv"] = np.ascontiguousarray(wqkv_b[:, 2048:3072])
        m["l0b_wout"] = np.ascontiguousarray(inputs["L0_attn_b_w_out"], np.float32)
        m["l0b_bout"] = np.ascontiguousarray(np.broadcast_to(
            np.asarray(inputs["L0_attn_b_b_out"], np.float32)[None, :], (128, D)))
        m["l0b_wthr"] = np.ascontiguousarray(np.broadcast_to(
            np.asarray(inputs["L0_attn_b_w_th"], np.float32).reshape(1, 256), (128, 256)))
        m["l0b_masks"] = masks_const
        m["l1b_wk"] = np.ascontiguousarray(np.asarray(
            inputs["L1_attn_b_W_key"], np.float32).transpose(1, 0, 2).reshape(D, D))
        m["l1b_wq"] = np.ascontiguousarray(np.asarray(
            inputs["L1_attn_b_W_query"], np.float32).transpose(1, 0, 2).reshape(D, D))
        m["l1b_wv"] = np.ascontiguousarray(np.asarray(
            inputs["L1_attn_b_W_values"], np.float32).transpose(1, 0, 2).reshape(D, D))
        m["l1b_wout"] = np.ascontiguousarray(np.asarray(
            inputs["L1_attn_b_W_output"], np.float32).reshape(D, D))
        for nm, key in (("bk", "L1_attn_b_Bias_key"), ("bq", "L1_attn_b_Bias_query"),
                        ("bv", "L1_attn_b_Bias_values")):
            m["l1b_" + nm] = np.ascontiguousarray(np.broadcast_to(
                np.asarray(inputs[key], np.float32).reshape(1, D), (128, D)))
        m["l1b_bout"] = np.ascontiguousarray(np.broadcast_to(
            np.asarray(inputs["L1_attn_b_Bias_output"], np.float32)[None, :], (128, D)))
        for nm, key in (("wqT", "L1_attn_a_Wq"), ("wkT", "L1_attn_a_Wk"),
                        ("wvT", "L1_attn_a_Wv"), ("woT", "L1_attn_a_Wo")):
            m["l1a_" + nm] = np.ascontiguousarray(np.asarray(inputs[key], np.float32).T)
        for nm, key in (("bq", "L1_attn_a_bq"), ("bk", "L1_attn_a_bk"),
                        ("bv", "L1_attn_a_bv"), ("bo", "L1_attn_a_bo")):
            m["l1a_" + nm] = np.ascontiguousarray(np.broadcast_to(
                np.asarray(inputs[key], np.float32)[None, :], (128, D)))
        m["l1f_gwr"] = np.ascontiguousarray(np.broadcast_to(np.asarray(
            inputs["L1_ffn_gate_w"], np.float32).T.reshape(1, 8192), (128, 8192)))
        m["l1f_gb"] = np.ascontiguousarray(np.broadcast_to(
            np.asarray(inputs["L1_ffn_gate_b"], np.float32)[None, :], (128, 8)))
        m["l1f_eb"] = np.ascontiguousarray(np.broadcast_to(
            np.asarray(inputs["L1_ffn_expert_b"], np.float32).reshape(1, 8192), (128, 8192)))
        for e in range(8):
            m[f"l1f_ew{e}"] = np.ascontiguousarray(inputs["L1_ffn_expert_w"][e], np.float32)
        in_maps.append(m)
    res = run_bass_kernel_spmd(nc, in_maps, core_ids=list(range(8)))
    out = np.stack([r["out"] for r in res.results], axis=0)
    return out.astype(np.float32)


# version 08
# speedup vs baseline: 1.1505x; 1.1505x over previous
import contextlib
import numpy as np
import concourse.bass as bass
import concourse.mybir as mybir
from concourse.bass_utils import run_bass_kernel_spmd

F32 = mybir.dt.float32
BF16 = mybir.dt.bfloat16
AF = mybir.ActivationFunctionType
ALU = mybir.AluOpType
AX = mybir.AxisListType

T = 4096
D = 1024
NT = T // 128
NW = T // 512
NEG = -3.0e38
EM8 = float(np.exp(-8.0))

PHASES = ["l0a", "l0f0", "l0b", "l0f1", "l1a", "l1f0", "l1b", "l1f1"]
ENABLED = list(PHASES)

class Buf:
    __slots__ = ("h", "w", "rs", "dsem")
    def __init__(self, h):
        self.h = h
        self.w = None
        self.rs = []
        self.dsem = {}

SERIAL = False
NPOOL = 40

class Prog:
    ENGS = ("T", "V", "S", "Y", "G")
    def __init__(self, nc, stack):
        self.nc = nc
        self.q = {e: [] for e in self.ENGS}
        self.cnt = {e: 0 for e in self.ENGS}
        self.sem = {e: stack.enter_context(nc.semaphore("sem_" + e)) for e in self.ENGS}
        self.waited = {e: {} for e in self.ENGS}
        self.dpool = {q: [stack.enter_context(nc.semaphore(f"dp{q}{i}")) for i in range(NPOOL)]
                      for q in ("Y", "G")}
        self.dcnt = {q: [0] * NPOOL for q in ("Y", "G")}
        self.dnext = {"Y": 0, "G": 0}

    def _wait(self, e, tok):
        if tok[0] == "D":
            _, q, idx, v = tok
            key = ("D", q, idx)
            if self.waited[e].get(key, -1) >= v:
                return
            self.waited[e][key] = v
            sem = self.dpool[q][idx]
            self.q[e].append(lambda eng, sem=sem, v=v: eng.wait_ge(sem, v))
            return
        se, v = tok
        if self.waited[e].get(se, -1) >= v:
            return
        self.waited[e][se] = v
        sem = self.sem[se]
        self.q[e].append(lambda eng, sem=sem, v=v: eng.wait_ge(sem, v))

    def group(self, e):
        assert e in ("T", "V", "S", "G")
        return Group(self, e)

    def dma(self, e, dst, srcs, fns):
        assert e in ("Y", "G")
        if e not in dst.dsem:
            dst.dsem[e] = self.dnext[e]
            self.dnext[e] = (self.dnext[e] + 1) % NPOOL
        idx = dst.dsem[e]
        if dst.w is not None:
            self._wait(e, dst.w)
        for t in dst.rs:
            self._wait(e, t)
        for b in srcs:
            if b.w is not None:
                self._wait(e, b.w)
        if SERIAL:
            for se in ("T", "V", "S"):
                if self.cnt[se] > 0:
                    self._wait(e, (se, self.cnt[se]))
        sem = self.dpool[e][idx]
        if not isinstance(fns, (list, tuple)):
            fns = [fns]
        for f in fns:
            self.q[e].append(lambda eng, f=f, sem=sem: f(eng).then_inc(sem, 16))
            self.dcnt[e][idx] += 16
        tok = ("D", e, idx, self.dcnt[e][idx])
        dst.w = tok
        dst.rs = []
        for b in srcs:
            b.rs.append(tok)
        return tok

    def drain_dma(self):
        for e in ("Y", "G"):
            for q in ("Y", "G"):
                for idx in range(NPOOL):
                    if self.dcnt[q][idx] > 0:
                        self._wait(e, ("D", q, idx, self.dcnt[q][idx]))

    def barrier(self):
        for e in self.ENGS:
            for se in self.ENGS:
                if se not in ("Y", "G") and se != e and self.cnt[se] > 0:
                    self._wait(e, (se, self.cnt[se]))
            for q in ("Y", "G"):
                for idx in range(NPOOL):
                    if self.dcnt[q][idx] > 0:
                        self._wait(e, ("D", q, idx, self.dcnt[q][idx]))

    def flush(self, blk):
        for e, attr in (("T", "tensor"), ("V", "vector"), ("S", "scalar"),
                        ("Y", "sync"), ("G", "gpsimd")):
            ops = self.q[e]
            if not ops:
                continue
            def body(eng, ops=ops):
                for f in ops:
                    f(eng)
            getattr(blk, attr)(body)
            self.q[e] = []

class Group:
    def __init__(self, p, e):
        self.p = p
        self.e = e
        self.fns = []
        self.marks_r = []
        self.marks_w = []
        if SERIAL:
            for se in ("T", "V", "S", "G"):
                if se != e and p.cnt[se] > 0:
                    p._wait(e, (se, p.cnt[se]))
            for q in ("Y", "G"):
                for idx in range(NPOOL):
                    if p.dcnt[q][idx] > 0:
                        p._wait(e, ("D", q, idx, p.dcnt[q][idx]))
    def r(self, *bufs):
        for b in bufs:
            if b.w is not None:
                self.p._wait(self.e, b.w)
            self.marks_r.append(b)
        return self
    def w(self, *bufs):
        for b in bufs:
            if b.w is not None:
                self.p._wait(self.e, b.w)
            for t in b.rs:
                self.p._wait(self.e, t)
            self.marks_w.append(b)
        return self
    def add(self, fn):
        self.fns.append(fn)
        return self
    def fin(self):
        assert self.fns
        p, e = self.p, self.e
        sem = p.sem[e]
        last = self.fns[-1]
        for f in self.fns[:-1]:
            p.q[e].append(f)
        p.q[e].append(lambda eng, f=last, sem=sem: f(eng).then_inc(sem, 1))
        p.cnt[e] += 1
        tok = (e, p.cnt[e])
        for b in self.marks_w:
            b.w = tok
            b.rs = []
        for b in self.marks_r:
            b.rs.append(tok)
        return tok


_UID = [0]
def SB(nc, name, shape, dt):
    _UID[0] += 1
    return nc.sbuf_tensor(f"{name}_{_UID[0]}", shape, dt)

def PS(nc, name, shape, dt):
    _UID[0] += 1
    return nc.psum_tensor(f"{name}_{_UID[0]}", shape, dt)

def ap3(handle, off, dims):
    return bass.AP(handle, off, dims)

def g1(p, e, reads, writes, fn):
    g = p.group(e).r(*reads).w(*writes)
    g.add(fn)
    return g.fin()

def emit_ln_tile(p, ctx, xf, stats, xn_out):
    cmisc = ctx["cmisc"]
    g = p.group("V").r(xf).w(stats)
    g.add(lambda v: v.bn_stats(stats.h[:, 4:10], xf.h[:, 0:512]))
    g.add(lambda v: v.bn_stats(stats.h[:, 10:16], xf.h[:, 512:1024]))
    g.fin()
    g1(p, "V", [stats], [stats], lambda v: v.bn_aggr(stats.h[:, 0:2], stats.h[:, 4:16]))
    g1(p, "S", [stats, cmisc], [stats], lambda s: s.activation(
        stats.h[:, 2:3], stats.h[:, 1:2], AF.Sqrt, bias=cmisc.h[:, 0:1]))
    g1(p, "V", [stats], [stats], lambda v: v.reciprocal(stats.h[:, 3:4], stats.h[:, 2:3]))
    g1(p, "V", [xf, stats], [xn_out], lambda v: v.tensor_scalar(
        xn_out.h[:], xf.h[:], stats.h[:, 0:1], stats.h[:, 3:4], ALU.subtract, ALU.mult))

def transposes(p, ctx, src, psum_tr, dst, nch, width=128, src_off=0, dst_off=0,
               evac="S"):
    ident = ctx["ident"]
    g = p.group("T").r(src, ident).w(psum_tr)
    for c in range(nch):
        g.add(lambda t, c=c: t.transpose(
            psum_tr.h[0:width, c * 128:(c + 1) * 128],
            src.h[:, src_off + c * width:src_off + (c + 1) * width], ident.h[:]))
    g.fin()
    g = p.group(evac).r(psum_tr).w(dst)
    if evac == "S":
        g.add(lambda s: s.copy(dst.h[0:width, dst_off:dst_off + nch * 128],
                               psum_tr.h[0:width, 0:nch * 128]))
    else:
        g.add(lambda v: v.tensor_copy(dst.h[0:width, dst_off:dst_off + nch * 128],
                                      psum_tr.h[0:width, 0:nch * 128]))
    g.fin()

def mm_proj(p, xnT, w_sb, psum, nout, kchunks=8, wcol=None, kpart=128,
            xoff=0, woff=0, extra_r=()):
    if wcol is None:
        wcol = nout
    g = p.group("T").r(xnT, w_sb, *extra_r).w(psum)
    ns = [(o, min(512, nout - o)) for o in range(0, nout, 512)]
    for c in range(kchunks):
        kp = kpart if c < kchunks - 1 else kpart
        for (o, wd) in ns:
            g.add(lambda t, c=c, o=o, wd=wd, kp=kp: t.matmul(
                psum.h[:, o:o + wd],
                xnT.h[0:kp, xoff + c * 128:xoff + c * 128 + 128],
                w_sb.h[0:kp, woff + c * wcol + o:woff + c * wcol + o + wd],
                start=(c == 0), stop=(c == kchunks - 1)))
    return g.fin()

def load_w_bf16(nc, p, st2, ap, name, rows, cols, dst=None):
    E2 = st2.enter_context
    ch = (rows + 127) // 128
    if dst is None:
        dst = Buf(E2(SB(nc, name + "_bf", [128, ch * cols], BF16)))
    with contextlib.ExitStack() as stt:
        tmp = Buf(stt.enter_context(SB(nc, name + "_tmpf", [128, cols], F32)))
        for c in range(ch):
            rp = min(128, rows - c * 128)
            p.dma("Y", tmp, [], lambda e, c=c, rp=rp: e.dma_start(
                out=tmp.h[0:rp, :], in_=ap[name][c * 128:c * 128 + rp, :]))
            g1(p, "V", [tmp], [dst], lambda v, c=c, rp=rp: v.tensor_copy(
                dst.h[0:rp, c * cols:(c + 1) * cols], tmp.h[0:rp, :]))
        p.barrier()
    return dst

def conv_bf16(nc, p, st2, ap, name, rows, cols):
    E2 = st2.enter_context
    dst = Buf(E2(SB(nc, name + "_bf", [rows, cols], BF16)))
    tmp = Buf(E2(SB(nc, name + "_tf", [rows, cols], F32)))
    p.dma("Y", tmp, [], lambda e: e.dma_start(out=tmp.h[:], in_=ap[name]))
    g1(p, "V", [tmp], [dst], lambda v: v.tensor_copy(dst.h[:], tmp.h[:]))
    return dst


def phase_l0a(nc, p, st, ap, ctx, xin, xout):
  with contextlib.ExitStack() as stp:
    Ep = stp.enter_context
    kT = Buf(Ep(SB(nc, "l0a_kT", [128, 8 * T], BF16)))
    vb = Buf(Ep(SB(nc, "l0a_vb", [128, NT * 1032], BF16)))
    qTd = Buf(ap["qTd"])
    cmisc = ctx["cmisc"]
    g1(p, "V", [cmisc], [vb], lambda v: v.tensor_copy(
        ap3(vb.h, 1024, [[NT * 1032, 128], [1032, NT], [1, 8]]),
        ap3(cmisc.h, 3, [[16, 128], [0, NT], [0, 8]])))

    for proj in ("k", "q", "v"):
        _l0a_proj(nc, p, ap, ctx, xin, proj, kT, vb, qTd)

    _l0a_passB(nc, p, ap, ctx, xin, xout, kT, vb)


def _l0a_proj(nc, p, ap, ctx, xin, proj, kT, vb, qTd):
      cmisc = ctx["cmisc"]
      with contextlib.ExitStack() as st2:
        E2 = st2.enter_context
        wsb = load_w_bf16(nc, p, st2, ap, "l0a_w" + proj, D, D)
        xf = [Buf(E2(SB(nc, f"l0a_xf{i}", [128, D], F32))) for i in range(2)]
        stats = [Buf(E2(SB(nc, f"l0a_st{i}", [128, 16], F32))) for i in range(2)]
        xn = [Buf(E2(SB(nc, f"l0a_xn{i}", [128, D], BF16))) for i in range(2)]
        xnT = [Buf(E2(SB(nc, f"l0a_xnT{i}", [128, D], BF16))) for i in range(2)]
        osb = [Buf(E2(SB(nc, f"l0a_o{i}", [128, D], BF16))) for i in range(2)]
        ptr = [Buf(E2(PS(nc, f"l0a_ptr{i}", [128, D], BF16))) for i in range(2)]
        pmm = [Buf(E2(PS(nc, f"l0a_pmm{i}", [128, D], F32))) for i in range(2)]
        for t in range(NT):
            i = t % 2
            p.dma("Y", xf[i], [xin], lambda e, t=t, i=i: e.dma_start(
                out=xf[i].h[:], in_=xin.h[t * 128:(t + 1) * 128, :]))
            emit_ln_tile(p, ctx, xf[i], stats[i], xn[i])
            transposes(p, ctx, xn[i], ptr[i], xnT[i], 8)
            mm_proj(p, xnT[i], wsb, pmm[i], D)
            if proj == "v":
                g1(p, "S", [pmm[i]], [vb], lambda s, t=t, i=i: s.copy(
                    vb.h[:, t * 1032:t * 1032 + 1024], pmm[i].h[:]))
                continue
            g1(p, "S", [pmm[i]], [osb[i]], lambda s, i=i: s.copy(osb[i].h[:], pmm[i].h[:]))
            g = p.group("T").r(osb[i], ctx["ident"]).w(ptr[i])
            for c in range(8):
                g.add(lambda e, c=c, i=i: e.transpose(ptr[i].h[:, c * 128:(c + 1) * 128],
                                                 osb[i].h[:, c * 128:(c + 1) * 128],
                                                 ctx["ident"].h[:]))
            g.fin()
            if proj == "k":
                g1(p, "V", [ptr[i]], [kT], lambda v, t=t, i=i: v.tensor_copy(
                    ap3(kT.h, t * 128, [[8 * T, 128], [T, 8], [1, 128]]),
                    ap3(ptr[i].h, 0, [[1024, 128], [128, 8], [1, 128]])))
            else:
                g1(p, "V", [ptr[i]], [osb[i]], lambda v, i=i: v.tensor_copy(osb[i].h[:], ptr[i].h[:]))
                p.dma("G", qTd, [osb[i]], [
                    (lambda e, t=t, c=c, i=i: e.dma_start(
                        out=qTd.h[c * 128:(c + 1) * 128, t * 128:(t + 1) * 128],
                        in_=osb[i].h[:, c * 128:(c + 1) * 128]))
                    for c in range(8)])
        p.barrier()


def _l0a_passB(nc, p, ap, ctx, xin, xout, kT, vb):
    cmisc = ctx["cmisc"]
    with contextlib.ExitStack() as st2:
        E2 = st2.enter_context
        qTg = Buf(E2(SB(nc, "l0a_qTg", [128, 8 * 256], BF16)))
        psT = [Buf(E2(PS(nc, f"l0a_psT{i}", [128, 512], F32))) for i in range(2)]
        pnum = [Buf(E2(PS(nc, f"l0a_pnum{i}", [128, 1032], F32))) for i in range(2)]
        pT = [Buf(E2(SB(nc, f"l0a_pT{i}", [128, 256], BF16))) for i in range(2)]
        yf = Buf(E2(SB(nc, "l0a_yf", [128, D], F32)))
        rden = Buf(E2(SB(nc, "l0a_rd", [128, 2], F32)))
        xf2 = [Buf(E2(SB(nc, f"l0a_x2{i}", [128, D], F32))) for i in range(2)]
        maskF = ctx["maskF"]

        for qg in range(T // 256):
            p.dma("Y", qTg, [], [
                (lambda e, qg=qg, c=c: e.dma_start(
                    out=qTg.h[:, c * 256:(c + 1) * 256],
                    in_=ap["qTd"][c * 128:(c + 1) * 128, qg * 256:(qg + 1) * 256]))
                for c in range(8)])
            for qc in range(2):
                qt = 2 * qg + qc
                p.dma("Y", xf2[qc], [xin], lambda e, qt=qt, qc=qc: e.dma_start(
                    out=xf2[qc].h[:], in_=xin.h[qt * 128:(qt + 1) * 128, :]))
            nkt = 2 * qg + 2
            def emit_av(kt, qg=qg):
                par = kt % 2
                for qc in range(2):
                    qt = 2 * qg + qc
                    if kt > qt:
                        continue
                    g = p.group("T").r(pT[par], vb).w(pnum[qc])
                    for (o, wd) in ((0, 512), (512, 512), (1024, 8)):
                        g.add(lambda t, qc=qc, kt=kt, o=o, wd=wd, qt=qt, par=par: t.matmul(
                            pnum[qc].h[:, o:o + wd],
                            pT[par].h[:, qc * 128:qc * 128 + 128],
                            vb.h[:, kt * 1032 + o:kt * 1032 + o + wd],
                            start=(kt == 0), stop=(kt == qt)))
                    g.fin()
            for kt in range(nkt):
                par = kt % 2
                g = p.group("T").r(kT, qTg).w(psT[par])
                for c in range(8):
                    g.add(lambda t, c=c, kt=kt, par=par: t.matmul(
                        psT[par].h[:, 0:256],
                        kT.h[:, c * T + kt * 128:c * T + kt * 128 + 128],
                        qTg.h[:, c * 256:(c + 1) * 256],
                        start=(c == 0), stop=(c == 7)))
                g.fin()
                g1(p, "S", [psT[par], cmisc], [pT[par]], lambda s, par=par: s.activation(
                    pT[par].h[:], psT[par].h[:, 0:256], AF.Exp,
                    bias=cmisc.h[:, 1:2], scale=1.0 / 32.0))
                j = kt - 2 * qg
                if j >= 0:
                    g1(p, "V", [maskF, pT[par]], [pT[par]], lambda v, j=j, par=par: v.tensor_tensor(
                        pT[par].h[:], pT[par].h[:], maskF.h[:, j * 512:j * 512 + 256], ALU.mult))
                if kt > 0:
                    emit_av(kt - 1)
            emit_av(nkt - 1)
            for qc in range(2):
                qt = 2 * qg + qc
                g1(p, "V", [pnum[qc]], [rden], lambda v, qc=qc: v.reciprocal(
                    rden.h[:, qc:qc + 1], pnum[qc].h[:, 1024:1025]))
                g1(p, "V", [pnum[qc], rden, xf2[qc]], [yf],
                   lambda v, qc=qc: v.scalar_tensor_tensor(
                       yf.h[:], pnum[qc].h[:, 0:1024], rden.h[:, qc:qc + 1],
                       xf2[qc].h[:], ALU.mult, ALU.add))
                p.dma("G", xout, [yf], lambda e, qt=qt: e.dma_start(
                    out=xout.h[qt * 128:(qt + 1) * 128, :], in_=yf.h[:]))
        p.barrier()


def emit_shift_load(p, ybuf, xf, t, cmisc=None):
    p.dma("Y", xf, [ybuf], lambda e, t=t: e.dma_start(
        out=xf.h[:, 0:512], in_=ybuf.h[t * 128:(t + 1) * 128, 0:512]))
    if t == 0:
        p.dma("Y", xf, [ybuf], lambda e: e.dma_start(
            out=xf.h[1:128, 512:1024], in_=ybuf.h[0:127, 512:1024]))
        g1(p, "V", [cmisc], [xf], lambda v: v.tensor_copy(
            xf.h[0:1, 512:1024], ap3(cmisc.h, 2, [[16, 1], [0, 512]])))
    else:
        p.dma("Y", xf, [ybuf], lambda e, t=t: e.dma_start(
            out=xf.h[:, 512:1024], in_=ybuf.h[t * 128 - 1:t * 128 + 127, 512:1024]))


def l0f_inner_layer(nc, p, st, ap, ctx, li, yin, ymid, yout, xin, last):
    cmisc = ctx["cmisc"]
    with contextlib.ExitStack() as sa:
        Ea = sa.enter_context
        qTl = Buf(Ea(SB(nc, "f_qT", [128, 4 * T], BF16)))
        kTl = Buf(Ea(SB(nc, "f_kT", [128, 4 * T], BF16)))
        vsb = Buf(Ea(SB(nc, "f_v", [128, NT * 576], BF16)))
        osb = Buf(Ea(SB(nc, "f_o", [128, 4 * 512], BF16)))
        obuf = Buf(ap["obuf"])
        g1(p, "V", [cmisc], [vsb], lambda v: v.tensor_copy(
            ap3(vsb.h, 64, [[NT * 576, 128], [576, NT], [72, 8], [1, 8]]),
            ap3(cmisc.h, 3, [[16, 128], [0, NT], [0, 8], [0, 8]])))
        with contextlib.ExitStack() as s2:
            E2 = s2.enter_context
            wqkv = load_w_bf16(nc, p, s2, ap, f"l0f_wqkv{li}", D, 1536)
            qsr = Buf(E2(SB(nc, "f_qsr", [128, 512], F32)))
            ksr = Buf(E2(SB(nc, "f_ksr", [128, 512], F32)))
            p.dma("Y", qsr, [], lambda e: e.dma_start(out=qsr.h[:], in_=ap[f"l0f_qs{li}"]))
            p.dma("Y", ksr, [], lambda e: e.dma_start(out=ksr.h[:], in_=ap[f"l0f_ks{li}"]))
            xf = [Buf(E2(SB(nc, "f_xf0", [128, D], F32)))] * 2
            hbf = Buf(E2(SB(nc, "f_hbf", [128, D], BF16)))
            hT = Buf(E2(SB(nc, "f_hT", [128, D], BF16)))
            qkvf = Buf(E2(SB(nc, "f_qkvf", [128, 1536], F32)))
            sqt = Buf(E2(SB(nc, "f_sq", [128, 512], F32)))
            ns = Buf(E2(SB(nc, "f_ns", [128, 32], F32)))
            rot = Buf(E2(SB(nc, "f_rot", [128, 512], F32)))
            tq1 = Buf(E2(SB(nc, "f_tq1", [128, 512], F32)))
            tq2 = Buf(E2(SB(nc, "f_tq2", [128, 512], F32)))
            qhat = Buf(E2(SB(nc, "f_qhat", [128, 512], BF16)))
            khat = Buf(E2(SB(nc, "f_khat", [128, 512], BF16)))
            tabs = [Buf(E2(SB(nc, f"f_tab{i}", [128, 512], F32))) for i in range(4)]
            pqkv = Buf(E2(PS(nc, "f_pqkv", [128, 1536], F32)))
            ptr = Buf(E2(PS(nc, "f_ptr", [128, D], BF16)))
            for t in range(NT):
                i = t % 2
                emit_shift_load(p, yin, xf[i], t, cmisc)
                g1(p, "V", [xf[i]], [hbf], lambda v, i=i: v.tensor_copy(hbf.h[:], xf[i].h[:]))
                transposes(p, ctx, hbf, ptr, hT, 8)
                mm_proj(p, hT, wqkv, pqkv, 1536, wcol=1536)
                g1(p, "V", [pqkv], [qkvf], lambda v: v.tensor_copy(qkvf.h[:], pqkv.h[:]))
                g1(p, "S", [qkvf], [sqt], lambda s: s.activation(sqt.h[:], qkvf.h[:, 0:512], AF.Square))
                g = p.group("V").r(sqt).w(ns)
                for h in range(8):
                    g.add(lambda v, h=h: v.reduce_sum(ns.h[:, h:h + 1], sqt.h[:, h * 64:(h + 1) * 64], axis=AX.X))
                g.fin()
                g1(p, "S", [qkvf], [sqt], lambda s: s.activation(sqt.h[:], qkvf.h[:, 512:1024], AF.Square))
                g = p.group("V").r(sqt).w(ns)
                for h in range(8):
                    g.add(lambda v, h=h: v.reduce_sum(ns.h[:, 8 + h:9 + h], sqt.h[:, h * 64:(h + 1) * 64], axis=AX.X))
                g.fin()
                g1(p, "S", [ns, cmisc], [ns], lambda s: s.activation(
                    ns.h[:, 16:32], ns.h[:, 0:16], AF.Sqrt, bias=cmisc.h[:, 5:6]))
                g1(p, "V", [ns], [ns], lambda v: v.reciprocal(
                    ns.h[:, 16:32], ns.h[:, 16:32]))
                g = p.group("V").r(qkvf, ns).w(qkvf)
                for h in range(8):
                    g.add(lambda v, h=h: v.tensor_scalar(
                        qkvf.h[:, h * 64:(h + 1) * 64], qkvf.h[:, h * 64:(h + 1) * 64],
                        ns.h[:, 16 + h:17 + h], None, ALU.mult))
                for h in range(8):
                    g.add(lambda v, h=h: v.tensor_scalar(
                        qkvf.h[:, 512 + h * 64:512 + (h + 1) * 64],
                        qkvf.h[:, 512 + h * 64:512 + (h + 1) * 64],
                        ns.h[:, 24 + h:25 + h], None, ALU.mult))
                g.fin()
                g1(p, "V", [qkvf, qsr], [qkvf], lambda v: v.tensor_tensor(
                    qkvf.h[:, 0:512], qkvf.h[:, 0:512], qsr.h[:], ALU.mult))
                g1(p, "V", [qkvf, ksr], [qkvf], lambda v: v.tensor_tensor(
                    qkvf.h[:, 512:1024], qkvf.h[:, 512:1024], ksr.h[:], ALU.mult))
                for ti, nm in enumerate(("c_csq", "c_snq", "c_csk", "c_snk")):
                    p.dma("Y", tabs[ti], [], lambda e, t=t, ti=ti, nm=nm: e.dma_start(
                        out=tabs[ti].h[:], in_=ap[nm][t * 128:(t + 1) * 128, :]))
                for (qoff, cs, sn, dstb) in ((0, tabs[0], tabs[1], qhat),
                                             (512, tabs[2], tabs[3], khat)):
                    g = p.group("G").r(qkvf).w(rot)
                    g.add(lambda v, qoff=qoff: v.tensor_single_scalar(
                        ap3(rot.h, 0, [[512, 128], [64, 8], [1, 32]]),
                        ap3(qkvf.h, qoff + 32, [[1536, 128], [64, 8], [1, 32]]),
                        -1.0, ALU.mult))
                    g.add(lambda v, qoff=qoff: v.tensor_copy(
                        ap3(rot.h, 32, [[512, 128], [64, 8], [1, 32]]),
                        ap3(qkvf.h, qoff, [[1536, 128], [64, 8], [1, 32]])))
                    g.fin()
                    g1(p, "V", [qkvf, cs], [tq1], lambda v, qoff=qoff, cs=cs: v.tensor_tensor(
                        tq1.h[:], qkvf.h[:, qoff:qoff + 512], cs.h[:], ALU.mult))
                    g1(p, "G", [rot, sn], [tq2], lambda v, sn=sn: v.tensor_tensor(
                        tq2.h[:], rot.h[:], sn.h[:], ALU.mult))
                    g1(p, "V", [tq1, tq2], [dstb], lambda v, dstb=dstb: v.tensor_tensor(
                        dstb.h[:], tq1.h[:], tq2.h[:], ALU.add))
                for (src, dstT) in ((qhat, qTl), (khat, kTl)):
                    g = p.group("T").r(src, ctx["ident"]).w(ptr)
                    for c in range(4):
                        g.add(lambda e, c=c, src=src: e.transpose(
                            ptr.h[:, c * 128:(c + 1) * 128],
                            src.h[:, c * 128:(c + 1) * 128], ctx["ident"].h[:]))
                    g.fin()
                    g1(p, "V", [ptr], [dstT], lambda v, t=t, dstT=dstT: v.tensor_copy(
                        ap3(dstT.h, t * 128, [[4 * T, 128], [T, 4], [1, 128]]),
                        ap3(ptr.h, 0, [[1024, 128], [128, 4], [1, 128]])))
                g1(p, "V", [qkvf], [vsb], lambda v, t=t: v.tensor_copy(
                    ap3(vsb.h, t * 576, [[NT * 576, 128], [72, 8], [1, 64]]),
                    ap3(qkvf.h, 1024, [[1536, 128], [64, 8], [1, 64]])))
            p.barrier()
        with contextlib.ExitStack() as s2:
            E2 = s2.enter_context
            maskW = ctx["maskW"]
            psT = [Buf(E2(PS(nc, f"f_psT{i}", [128, 512], F32))) for i in range(2)]
            pacc = [Buf(E2(PS(nc, f"f_pacc{i}", [128, 128], F32))) for i in range(4)]
            Pb = [Buf(E2(SB(nc, f"f_Pb{i}", [128, 512], BF16))) for i in range(2)]
            stmp = [Buf(E2(SB(nc, f"f_stmp{i}", [128, 512], F32))) for i in range(2)]
            rd = Buf(E2(SB(nc, "f_rd", [128, 4], F32)))
            for w in range(NW):
                for h in range(8):
                    kts = list(range(8)) if w > 0 else list(range(4, 8))
                    def emit_avf(kt, h=h, w=w, kts=kts):
                        par = kt % 2
                        tk = 4 * (w - 1) + kt
                        for qc in range(4):
                            g1(p, "T", [Pb[par], vsb], [pacc[qc]],
                               lambda t, qc=qc, tk=tk, h=h, kt=kt, kts=kts, par=par: t.matmul(
                                   pacc[qc].h[:, 0:72],
                                   Pb[par].h[:, qc * 128:qc * 128 + 128],
                                   vsb.h[:, tk * 576 + h * 72:tk * 576 + h * 72 + 72],
                                   start=(kt == kts[0]), stop=(kt == kts[-1])))
                    for kt in kts:
                        par = kt % 2
                        tk = 4 * (w - 1) + kt
                        g1(p, "T", [kTl, qTl], [psT[par]], lambda t, h=h, tk=tk, w=w, par=par: t.matmul(
                            psT[par].h[:],
                            kTl.h[(h % 2) * 64:(h % 2) * 64 + 64,
                                  (h // 2) * T + tk * 128:(h // 2) * T + tk * 128 + 128],
                            qTl.h[(h % 2) * 64:(h % 2) * 64 + 64,
                                  (h // 2) * T + w * 512:(h // 2) * T + w * 512 + 512],
                            start=True, stop=True))
                        g1(p, "V", [psT[par], maskW], [stmp[par]], lambda v, kt=kt, par=par: v.tensor_tensor(
                            stmp[par].h[:], psT[par].h[:], maskW.h[:, kt * 512:(kt + 1) * 512], ALU.add))
                        g1(p, "S", [stmp[par], cmisc], [Pb[par]], lambda s, par=par: s.activation(
                            Pb[par].h[:], stmp[par].h[:], AF.Exp, bias=cmisc.h[:, 1:2], scale=8.0))
                        if kt > kts[0]:
                            emit_avf(kt - 1)
                    emit_avf(kts[-1])
                    for qc in range(4):
                        g1(p, "V", [pacc[qc]], [rd], lambda v, qc=qc: v.reciprocal(
                            rd.h[:, qc:qc + 1], pacc[qc].h[:, 64:65]))
                        g1(p, "V", [pacc[qc], rd], [osb], lambda v, qc=qc, h=h: v.tensor_scalar(
                            osb.h[:, qc * 512 + h * 64:qc * 512 + h * 64 + 64],
                            pacc[qc].h[:, 0:64], rd.h[:, qc:qc + 1], None, ALU.mult))
                p.dma("G", obuf, [osb], [
                    (lambda e, w=w, qc=qc: e.dma_start(
                        out=obuf.h[(w * 4 + qc) * 128:(w * 4 + qc + 1) * 128, :],
                        in_=osb.h[:, qc * 512:(qc + 1) * 512]))
                    for qc in range(4)])
            p.barrier()
        with contextlib.ExitStack() as s2:
            E2 = s2.enter_context
            wo = load_w_bf16(nc, p, s2, ap, f"l0f_wo{li}", 512, D)
            olb = Buf(E2(SB(nc, "f_olb", [128, 512], BF16)))
            oT = Buf(E2(SB(nc, "f_oT", [128, 512], BF16)))
            ptrW = Buf(E2(PS(nc, "f_ptr2", [128, 512], BF16)))
            pwo = Buf(E2(PS(nc, "f_pwo", [128, D], F32)))
            wyf = [Buf(E2(SB(nc, f"f_yf{i}", [128, D], F32))) for i in range(2)]
            wyn = Buf(E2(SB(nc, "f_yn", [128, D], F32)))
            for t in range(NT):
                i = t % 2
                p.dma("Y", wyf[i], [yin], lambda e, t=t, i=i: e.dma_start(
                    out=wyf[i].h[:], in_=yin.h[t * 128:(t + 1) * 128, :]))
                p.dma("Y", olb, [obuf], lambda e, t=t: e.dma_start(
                    out=olb.h[:], in_=obuf.h[t * 128:(t + 1) * 128, :]))
                g = p.group("T").r(olb, ctx["ident"]).w(ptrW)
                for c in range(4):
                    g.add(lambda e, c=c: e.transpose(
                        ptrW.h[:, c * 128:(c + 1) * 128],
                        olb.h[:, c * 128:(c + 1) * 128],
                        ctx["ident"].h[:]))
                g.fin()
                g1(p, "V", [ptrW], [oT], lambda v: v.tensor_copy(oT.h[:], ptrW.h[:, 0:512]))
                mm_proj(p, oT, wo, pwo, D, kchunks=4)
                g1(p, "V", [pwo, wyf[i]], [wyn], lambda v, i=i: v.tensor_tensor(
                    wyn.h[:], pwo.h[:], wyf[i].h[:], ALU.add))
                p.dma("G", ymid, [wyn], lambda e, t=t: e.dma_start(
                    out=ymid.h[t * 128:(t + 1) * 128, :], in_=wyn.h[:]))
            p.barrier()
    pieces = [(o, min(512, 2730 - o)) for o in range(0, 2730, 512)]
    with contextlib.ExitStack() as s2:
        E2 = s2.enter_context
        w1 = load_w_bf16(nc, p, s2, ap, f"l0f_w1{li}", D, 5460)
        lgr = Buf(E2(SB(nc, "f_lgr", [128, D], F32)))
        lbr = Buf(E2(SB(nc, "f_lbr", [128, D], F32)))
        b1r = Buf(E2(SB(nc, "f_b1r", [128, 5460], F32)))
        p.dma("Y", lgr, [], lambda e: e.dma_start(out=lgr.h[:], in_=ap[f"l0f_lg{li}"]))
        p.dma("Y", lbr, [], lambda e: e.dma_start(out=lbr.h[:], in_=ap[f"l0f_lb{li}"]))
        p.dma("Y", b1r, [], lambda e: e.dma_start(out=b1r.h[:], in_=ap[f"l0f_b1{li}"]))
        mxf = [Buf(E2(SB(nc, f"f_mx{i}", [128, D], F32))) for i in range(2)]
        mstats = Buf(E2(SB(nc, "f_mst", [128, 16], F32)))
        mnf = Buf(E2(SB(nc, "f_nf", [128, D], F32)))
        n2 = Buf(E2(SB(nc, "f_n2", [128, D], BF16)))
        n2T = Buf(E2(SB(nc, "f_n2T", [128, D], BF16)))
        aa = Buf(E2(SB(nc, "f_aa", [128, 512], F32)))
        gg = Buf(E2(SB(nc, "f_gg", [128, 512], F32)))
        hm = Buf(E2(SB(nc, "f_hm", [128, 2730], BF16)))
        hmTs = Buf(E2(SB(nc, "f_hmTs", [128, 2816], BF16)))
        ptrM = Buf(E2(PS(nc, "f_ptr3", [128, D], BF16)))
        pA = Buf(E2(PS(nc, "f_pA", [128, 512], F32)))
        pG = Buf(E2(PS(nc, "f_pG", [128, 512], F32)))
        hmT = Buf(ap["hmT"])
        for t in range(NT):
            i = t % 2
            emit_shift_load(p, ymid, mxf[i], t, ctx['cmisc'])
            emit_ln_tile(p, ctx, mxf[i], mstats, mnf)
            g1(p, "V", [mnf, lgr], [mnf], lambda v: v.tensor_tensor(mnf.h[:], mnf.h[:], lgr.h[:], ALU.mult))
            g1(p, "V", [mnf, lbr], [n2], lambda v: v.tensor_tensor(n2.h[:], mnf.h[:], lbr.h[:], ALU.add))
            transposes(p, ctx, n2, ptrM, n2T, 8)
            for (off, wd) in pieces:
                mm_proj(p, n2T, w1, pA, wd, wcol=5460, woff=off)
                mm_proj(p, n2T, w1, pG, wd, wcol=5460, woff=2730 + off)
                g1(p, "V", [pA, b1r], [aa], lambda v, off=off, wd=wd: v.tensor_tensor(
                    aa.h[:, 0:wd], pA.h[:, 0:wd], b1r.h[:, off:off + wd], ALU.add))
                g1(p, "V", [pG, b1r], [gg], lambda v, off=off, wd=wd: v.tensor_tensor(
                    gg.h[:, 0:wd], pG.h[:, 0:wd], b1r.h[:, 2730 + off:2730 + off + wd], ALU.add))
                g1(p, "S", [gg], [gg], lambda s, wd=wd: s.activation(
                    gg.h[:, 0:wd], gg.h[:, 0:wd], AF.Gelu))
                g1(p, "V", [aa, gg], [hm], lambda v, off=off, wd=wd: v.tensor_tensor(
                    hm.h[:, off:off + wd], aa.h[:, 0:wd], gg.h[:, 0:wd], ALU.mult))
            for cb in range(3):
                c0 = cb * 8
                nch = min(8, 22 - c0)
                g = p.group("T").r(hm, ctx["ident"]).w(ptrM)
                for c in range(c0, c0 + nch):
                    wd = min(128, 2730 - c * 128)
                    g.add(lambda e, c=c, wd=wd, c0=c0: e.transpose(
                        ptrM.h[0:wd, (c - c0) * 128:(c - c0) * 128 + 128],
                        hm.h[:, c * 128:c * 128 + wd], ctx["ident"].h[:]))
                g.fin()
                nfull = nch - 1 if c0 + nch == 22 else nch
                if nfull > 0:
                    g1(p, "V", [ptrM], [hmTs], lambda v, c0=c0, nfull=nfull: v.tensor_copy(
                        hmTs.h[:, c0 * 128:(c0 + nfull) * 128], ptrM.h[:, 0:nfull * 128]))
                if nfull < nch:
                    g1(p, "V", [ptrM], [hmTs], lambda v, c0=c0, nch=nch: v.tensor_copy(
                        hmTs.h[0:42, (c0 + nch - 1) * 128:(c0 + nch) * 128],
                        ptrM.h[0:42, (nch - 1) * 128:nch * 128]))
            p.dma("G", hmT, [hmTs], [
                (lambda e, t=t, c=c, rp=min(128, 2730 - c * 128): e.dma_start(
                    out=hmT.h[c * 128:c * 128 + rp, t * 128:(t + 1) * 128],
                    in_=hmTs.h[0:rp, c * 128:(c + 1) * 128]))
                for c in range(22)])
        p.barrier()
    with contextlib.ExitStack() as s2:
        E2 = s2.enter_context
        w2 = load_w_bf16(nc, p, s2, ap, f"l0f_w2{li}", 2730, D)
        b2r = Buf(E2(SB(nc, "f_b2r", [128, D], F32)))
        p.dma("Y", b2r, [], lambda e: e.dma_start(out=b2r.h[:], in_=ap[f"l0f_b2{li}"]))
        hmTl = Buf(E2(SB(nc, "f_hmTl", [128, 2816], BF16)))
        yf2 = [Buf(E2(SB(nc, f"f_wyf{i}", [128, D], F32))) for i in range(2)]
        xr = Buf(E2(SB(nc, "f_wxr", [128, D], F32)))
        yn2 = Buf(E2(SB(nc, "f_wyn", [128, D], F32)))
        pm = Buf(E2(PS(nc, "f_pm", [128, D], F32)))
        hmT2 = Buf(ap["hmT"])
        for t in range(NT):
            i = t % 2
            p.dma("Y", hmTl, [hmT2], [
                (lambda e, t=t, c=c, rp=min(128, 2730 - c * 128): e.dma_start(
                    out=hmTl.h[0:rp, c * 128:(c + 1) * 128],
                    in_=hmT2.h[c * 128:c * 128 + rp, t * 128:(t + 1) * 128]))
                for c in range(22)])
            p.dma("Y", yf2[i], [ymid], lambda e, t=t, i=i: e.dma_start(
                out=yf2[i].h[:], in_=ymid.h[t * 128:(t + 1) * 128, :]))
            g = p.group("T").r(hmTl, w2).w(pm)
            for c in range(22):
                rp = min(128, 2730 - c * 128)
                for o in (0, 512):
                    g.add(lambda t_, c=c, o=o, rp=rp: t_.matmul(
                        pm.h[:, o:o + 512],
                        hmTl.h[0:rp, c * 128:(c + 1) * 128],
                        w2.h[0:rp, c * D + o:c * D + o + 512],
                        start=(c == 0), stop=(c == 21)))
            g.fin()
            g1(p, "V", [pm, b2r], [yn2], lambda v: v.tensor_tensor(
                yn2.h[:], pm.h[:], b2r.h[:], ALU.add))
            g1(p, "V", [yn2, yf2[i]], [yn2], lambda v, i=i: v.tensor_tensor(
                yn2.h[:], yn2.h[:], yf2[i].h[:], ALU.add))
            if last:
                p.dma("Y", xr, [xin], lambda e, t=t: e.dma_start(
                    out=xr.h[:], in_=xin.h[t * 128:(t + 1) * 128, :]))
                g1(p, "V", [yn2, xr], [yn2], lambda v: v.tensor_tensor(
                    yn2.h[:], yn2.h[:], xr.h[:], ALU.add))
            p.dma("G", yout, [yn2], lambda e, t=t: e.dma_start(
                out=yout.h[t * 128:(t + 1) * 128, :], in_=yn2.h[:]))
        p.barrier()


def phase_l0f(nc, p, st, ap, ctx, xin, xout):
    ybA = Buf(ap["ybA"]); ybB = Buf(ap["ybB"])
    with contextlib.ExitStack() as s2:
        E2 = s2.enter_context
        xf = [Buf(E2(SB(nc, f"f_lx{i}", [128, D], F32))) for i in range(2)]
        stats = Buf(E2(SB(nc, "f_lst", [128, 16], F32)))
        nf = Buf(E2(SB(nc, "f_lnf", [128, D], F32)))
        for t in range(NT):
            i = t % 2
            p.dma("Y", xf[i], [xin], lambda e, t=t, i=i: e.dma_start(
                out=xf[i].h[:], in_=xin.h[t * 128:(t + 1) * 128, :]))
            emit_ln_tile(p, ctx, xf[i], stats, nf)
            p.dma("G", ybA, [nf], lambda e, t=t: e.dma_start(
                out=ybA.h[t * 128:(t + 1) * 128, :], in_=nf.h[:]))
        p.barrier()
    l0f_inner_layer(nc, p, st, ap, ctx, 0, ybA, ybB, ybA, xin, last=False)
    l0f_inner_layer(nc, p, st, ap, ctx, 1, ybA, ybB, xout, xin, last=True)


def phase_l1a(nc, p, st, ap, ctx, xin, xout):
    cmisc = ctx["cmisc"]
    maskF = ctx["maskF"]
    with contextlib.ExitStack() as s2:
        E2 = s2.enter_context
        wq = load_w_bf16(nc, p, s2, ap, "l1a_wqT", D, D)
        wk = load_w_bf16(nc, p, s2, ap, "l1a_wkT", D, D)
        wv = load_w_bf16(nc, p, s2, ap, "l1a_wvT", D, D)
        wo = load_w_bf16(nc, p, s2, ap, "l1a_woT", D, D)
        brs = {}
        for nm in ("bq", "bk", "bv", "bo"):
            brs[nm] = Buf(E2(SB(nc, "a_" + nm, [128, D], F32)))
            p.dma("Y", brs[nm], [], lambda e, nm=nm: e.dma_start(
                out=brs[nm].h[:], in_=ap["l1a_" + nm]))
        Sf = Buf(E2(SB(nc, "a_Sf", [128, 16 * 72], F32)))
        Sz = Buf(E2(SB(nc, "a_Sz", [128, 16 * 72], BF16)))
        g1(p, "V", [cmisc], [Sf], lambda v: v.tensor_copy(
            Sf.h[:], ap3(cmisc.h, 2, [[16, 128], [0, 16 * 72]])))
        g1(p, "V", [cmisc], [Sz], lambda v: v.tensor_copy(
            Sz.h[:], ap3(cmisc.h, 2, [[16, 128], [0, 16 * 72]])))
        xf = [Buf(E2(SB(nc, f"a_xf{i}", [128, D], F32))) for i in range(2)]
        stats = Buf(E2(SB(nc, "a_st", [128, 16], F32)))
        xn = Buf(E2(SB(nc, "a_xn", [128, D], BF16)))
        xnT = Buf(E2(SB(nc, "a_xnT", [128, D], BF16)))
        tmpf = Buf(E2(SB(nc, "a_tmpf", [128, D], F32)))
        qe = Buf(E2(SB(nc, "a_qe", [128, D], BF16)))
        ke = Buf(E2(SB(nc, "a_ke", [128, D], BF16)))
        qeT = Buf(E2(SB(nc, "a_qeT", [128, D], BF16)))
        keT = Buf(E2(SB(nc, "a_keT", [128, D], BF16)))
        vsb = Buf(E2(SB(nc, "a_vsb", [128, 1152], BF16)))
        Abf = Buf(E2(SB(nc, "a_Abf", [128, 128], BF16)))
        ysb = Buf(E2(SB(nc, "a_ysb", [128, D], F32)))
        ybf = Buf(E2(SB(nc, "a_ybf", [128, D], BF16)))
        rd = Buf(E2(SB(nc, "a_rd", [128, 1], F32)))
        ptr = Buf(E2(PS(nc, "a_ptr", [128, D], BF16)))
        pmm = Buf(E2(PS(nc, "a_pmm", [128, D], F32)))
        pA = Buf(E2(PS(nc, "a_pA", [128, 128], F32)))
        pO = Buf(E2(PS(nc, "a_pO", [128, 128], F32)))
        pS = Buf(E2(PS(nc, "a_pS", [128, 128], F32)))
        for t in range(NT):
            i = t % 2
            p.dma("Y", xf[i], [xin], lambda e, t=t, i=i: e.dma_start(
                out=xf[i].h[:], in_=xin.h[t * 128:(t + 1) * 128, :]))
            emit_ln_tile(p, ctx, xf[i], stats, xn)
            transposes(p, ctx, xn, ptr, xnT, 8)
            for (wsb, br, dst) in ((wq, "bq", qe), (wk, "bk", ke), (wv, "bv", None)):
                mm_proj(p, xnT, wsb, pmm, D)
                g1(p, "V", [pmm, brs[br]], [tmpf], lambda v, br=br: v.tensor_tensor(
                    tmpf.h[:], pmm.h[:], brs[br].h[:], ALU.add))
                if dst is None:
                    g1(p, "V", [tmpf], [vsb], lambda v: v.tensor_copy(
                        ap3(vsb.h, 0, [[1152, 128], [72, 16], [1, 64]]),
                        ap3(tmpf.h, 0, [[1024, 128], [64, 16], [1, 64]])))
                    g1(p, "V", [cmisc], [vsb], lambda v: v.tensor_copy(
                        ap3(vsb.h, 64, [[1152, 128], [72, 16], [1, 8]]),
                        ap3(cmisc.h, 3, [[16, 128], [0, 16], [0, 8]])))
                else:
                    g1(p, "S", [tmpf], [dst], lambda s, dst=dst: s.activation(
                        dst.h[:], tmpf.h[:], AF.Exp))
            for (src, dT) in ((qe, qeT), (ke, keT)):
                g = p.group("T").r(src, ctx["ident"]).w(ptr)
                for c in range(8):
                    g.add(lambda e, c=c, src=src: e.transpose(
                        ptr.h[:, c * 128:(c + 1) * 128],
                        src.h[:, c * 128:(c + 1) * 128], ctx["ident"].h[:]))
                g.fin()
                g1(p, "V", [ptr], [dT], lambda v, dT=dT: v.tensor_copy(dT.h[:], ptr.h[:]))
            for h in range(16):
                po = (h % 2) * 64
                ch = h // 2
                g1(p, "T", [keT, qeT], [pA], lambda t_, po=po, ch=ch: t_.matmul(
                    pA.h[:], keT.h[po:po + 64, ch * 128:(ch + 1) * 128],
                    qeT.h[po:po + 64, ch * 128:(ch + 1) * 128], start=True, stop=True))
                g1(p, "V", [pA, maskF], [Abf], lambda v: v.tensor_tensor(
                    Abf.h[:], pA.h[:], maskF.h[:, 0:128], ALU.mult))
                g = p.group("T").r(Abf, vsb, qeT, Sz).w(pO)
                g.add(lambda t_, h=h: t_.matmul(
                    pO.h[:, 0:72], Abf.h[:], vsb.h[:, h * 72:(h + 1) * 72],
                    start=True, stop=False))
                g.add(lambda t_, h=h, ch=ch: t_.matmul(
                    pO.h[:, 0:72], qeT.h[:, ch * 128:(ch + 1) * 128],
                    Sz.h[:, h * 72:(h + 1) * 72], start=False, stop=True))
                g.fin()
                g1(p, "T", [ke, vsb], [pS], lambda t_, h=h, po=po: t_.matmul(
                    pS.h[po:po + 64, 0:72], ke.h[:, h * 64:(h + 1) * 64],
                    vsb.h[:, h * 72:(h + 1) * 72], start=True, stop=True))
                g1(p, "V", [pS, Sf], [Sf], lambda v, h=h, po=po: v.tensor_tensor(
                    Sf.h[po:po + 64, h * 72:(h + 1) * 72],
                    Sf.h[po:po + 64, h * 72:(h + 1) * 72],
                    pS.h[po:po + 64, 0:72], ALU.add))
                g1(p, "V", [Sf], [Sz], lambda v, h=h, po=po: v.tensor_copy(
                    Sz.h[po:po + 64, h * 72:(h + 1) * 72],
                    Sf.h[po:po + 64, h * 72:(h + 1) * 72]))
                g1(p, "V", [pO], [rd], lambda v: v.reciprocal(rd.h[:], pO.h[:, 64:65]))
                g1(p, "V", [pO, rd], [ysb], lambda v, h=h: v.tensor_scalar(
                    ysb.h[:, h * 64:(h + 1) * 64], pO.h[:, 0:64], rd.h[:], None, ALU.mult))
            g1(p, "V", [ysb], [ybf], lambda v: v.tensor_copy(ybf.h[:], ysb.h[:]))
            transposes(p, ctx, ybf, ptr, xnT, 8)
            mm_proj(p, xnT, wo, pmm, D)
            g1(p, "V", [pmm, brs["bo"]], [tmpf], lambda v: v.tensor_tensor(
                tmpf.h[:], pmm.h[:], brs["bo"].h[:], ALU.add))
            g1(p, "V", [tmpf, xf[i]], [ysb], lambda v, i=i: v.tensor_tensor(
                ysb.h[:], tmpf.h[:], xf[i].h[:], ALU.add))
            p.dma("G", xout, [ysb], lambda e, t=t: e.dma_start(
                out=xout.h[t * 128:(t + 1) * 128, :], in_=ysb.h[:]))
        p.barrier()


def phase_l0b(nc, p, st, ap, ctx, xin, xout):
  with contextlib.ExitStack() as stp:
    Ep = stp.enter_context
    kT = Buf(Ep(SB(nc, "c_kT", [128, 8 * T], BF16)))
    vb = Buf(Ep(SB(nc, "c_vb", [128, NT * 1024], BF16)))
    kbosT = Buf(Ep(SB(nc, "c_kbosT", [128, 8], BF16)))
    vbosrep = Buf(Ep(SB(nc, "c_vbosrep", [128, D], BF16)))
    xnb = Buf(ap["ybA"])
    qTd = Buf(ap["qTd"])

    with contextlib.ExitStack() as s2:
        E2 = s2.enter_context
        xf = [Buf(E2(SB(nc, "c_lx0", [128, D], F32)))] * 2
        stats = Buf(E2(SB(nc, "c_lst", [128, 16], F32)))
        nf = Buf(E2(SB(nc, "c_lnf", [128, D], F32)))
        for t in range(NT):
            i = t % 2
            p.dma("Y", xf[i], [xin], lambda e, t=t, i=i: e.dma_start(
                out=xf[i].h[:], in_=xin.h[t * 128:(t + 1) * 128, :]))
            emit_ln_tile(p, ctx, xf[i], stats, nf)
            p.dma("G", xnb, [nf], lambda e, t=t: e.dma_start(
                out=xnb.h[t * 128:(t + 1) * 128, :], in_=nf.h[:]))
        p.barrier()

    for proj in ("k", "q", "v"):
        _l0b_proj(nc, p, ap, ctx, proj, xnb, kT, vb, kbosT, vbosrep, qTd)
    _l0b_attn(nc, p, ap, ctx, kT, vb, kbosT, vbosrep, qTd)
    _l0b_out(nc, p, ap, ctx, xin, xout, vbosrep)


def _l0b_proj(nc, p, ap, ctx, proj, xnb, kT, vb, kbosT, vbosrep, qTd):
    cmisc = ctx["cmisc"]
    obuf2 = Buf(ap["obuf2"])
    with contextlib.ExitStack() as st2:
        E2 = st2.enter_context
        wsb = load_w_bf16(nc, p, st2, ap, "l0b_w" + proj, D, D)
        xff = [Buf(E2(SB(nc, "c_xff0", [128, D], F32)))] * 2
        xf = [Buf(E2(SB(nc, "c_xf0", [128, D], BF16)))] * 2
        osb = Buf(E2(SB(nc, "c_osb", [128, D], BF16)))
        ptr = Buf(E2(PS(nc, "c_ptr", [128, D], BF16)))
        pmm = Buf(E2(PS(nc, "c_pmm", [128, D], F32)))
        if proj in ("k", "v"):
            p.dma("Y", xff[0], [xnb], lambda e: e.dma_start(
                out=xff[0].h[:], in_=xnb.h[0:128, :]))
            g1(p, "V", [xff[0]], [xf[0]], lambda v: v.tensor_copy(xf[0].h[:], xff[0].h[:]))
            transposes(p, ctx, xf[0], ptr, osb, 8)
            mm_proj(p, osb, wsb, pmm, D)
            if proj == "v":
                g1(p, "S", [pmm], [osb], lambda s: s.copy(osb.h[:], pmm.h[:]))
                vbd = Buf(ap["vbos_d"])
                p.dma("G", vbd, [osb], lambda e: e.dma_start(
                    out=vbd.h[:], in_=osb.h[0:1, :]))
                p.dma("Y", vbosrep, [vbd], lambda e: e.dma_start(
                    out=vbosrep.h[:], in_=ap3(vbd.h.tensor, 0, [[0, 128], [1, D]])))
            else:
                g1(p, "S", [pmm], [osb], lambda s: s.copy(osb.h[:], pmm.h[:]))
                g = p.group("T").r(osb, ctx["ident"]).w(ptr)
                for c in range(8):
                    g.add(lambda e, c=c: e.transpose(
                        ptr.h[:, c * 128:(c + 1) * 128],
                        osb.h[:, c * 128:(c + 1) * 128], ctx["ident"].h[:]))
                g.fin()
                g1(p, "V", [ptr], [kbosT], lambda v: v.tensor_copy(
                    ap3(kbosT.h, 0, [[8, 128], [1, 8]]),
                    ap3(ptr.h, 0, [[1024, 128], [128, 8]])))
        for t in range(NT):
            i = t % 2
            if t < NT - 1:
                p.dma("Y", xff[i], [xnb], lambda e, t=t, i=i: e.dma_start(
                    out=xff[i].h[:], in_=xnb.h[t * 128 + 1:t * 128 + 129, :]))
            else:
                p.dma("Y", xff[i], [xnb], lambda e, t=t, i=i: e.dma_start(
                    out=xff[i].h[0:127, :], in_=xnb.h[t * 128 + 1:t * 128 + 128, :]))
                p.dma("Y", xff[i], [], lambda e, i=i: e.dma_start(
                    out=xff[i].h[127:128, :], in_=ap["c_zrow"]))
            g1(p, "V", [xff[i]], [xf[i]], lambda v, i=i: v.tensor_copy(xf[i].h[:], xff[i].h[:]))
            transposes(p, ctx, xf[i], ptr, osb, 8)
            mm_proj(p, osb, wsb, pmm, D)
            if proj == "v":
                g1(p, "S", [pmm], [vb], lambda s, t=t: s.copy(
                    vb.h[:, t * 1024:(t + 1) * 1024], pmm.h[:]))
                continue
            g1(p, "S", [pmm], [osb], lambda s: s.copy(osb.h[:], pmm.h[:]))
            g = p.group("T").r(osb, ctx["ident"]).w(ptr)
            for c in range(8):
                g.add(lambda e, c=c: e.transpose(ptr.h[:, c * 128:(c + 1) * 128],
                                                 osb.h[:, c * 128:(c + 1) * 128],
                                                 ctx["ident"].h[:]))
            g.fin()
            if proj == "k":
                g1(p, "V", [ptr], [kT], lambda v, t=t: v.tensor_copy(
                    ap3(kT.h, t * 128, [[8 * T, 128], [T, 8], [1, 128]]),
                    ap3(ptr.h, 0, [[1024, 128], [128, 8], [1, 128]])))
            else:
                g1(p, "V", [ptr], [osb], lambda v: v.tensor_copy(osb.h[:], ptr.h[:]))
                p.dma("G", qTd, [osb], [
                    (lambda e, t=t, c=c: e.dma_start(
                        out=qTd.h[c * 128:(c + 1) * 128, t * 128:(t + 1) * 128],
                        in_=osb.h[:, c * 128:(c + 1) * 128]))
                    for c in range(8)])
        p.barrier()


def _l0b_attn(nc, p, ap, ctx, kT, vb, kbosT, vbosrep, qTd):
    cmisc = ctx["cmisc"]
    obuf2 = Buf(ap["obuf2"])
    with contextlib.ExitStack() as st2:
        E2 = st2.enter_context
        wthr = Buf(E2(SB(nc, "c_wthr", [128, 256], F32)))
        p.dma("Y", wthr, [], lambda e: e.dma_start(out=wthr.h[:], in_=ap["l0b_wthr"]))
        mks = Buf(E2(SB(nc, "c_mks", [128, 2 * 392], F32)))
        p.dma("Y", mks, [], lambda e: e.dma_start(out=mks.h[:], in_=ap["l0b_masks"]))
        qTl = Buf(E2(SB(nc, "c_qTl", [128, 8 * 128], BF16)))
        Pall = Buf(E2(SB(nc, "c_Pall", [128, 16 * 392], BF16)))
        PgE = {e: Buf(E2(SB(nc, "c_Pg" + e, [128, 392], BF16))) for e in "VG"}
        PgcE = {e: Buf(E2(SB(nc, "c_Pgc" + e, [128, 1], F32))) for e in "VG"}
        maccE = {e: [Buf(E2(SB(nc, f"c_macc{e}{i}", [128, 392], BF16))) for i in range(2)]
                 for e in "VG"}
        ysbE = {e: Buf(E2(SB(nc, "c_ysb" + e, [128, D], F32))) for e in "VG"}
        smf = Buf(E2(SB(nc, "c_smf", [128, 392], F32)))
        den = Buf(E2(SB(nc, "c_den", [128, 2], F32)))
        PTE = {e: Buf(E2(SB(nc, "c_PT" + e, [128, 384], BF16))) for e in "VG"}
        obf = Buf(E2(SB(nc, "c_obf", [128, D], BF16)))
        psc = Buf(E2(PS(nc, "c_psc", [128, 512], F32)))
        ptr2 = Buf(E2(PS(nc, "c_ptr2", [128, 512], BF16)))
        pacc = [Buf(E2(PS(nc, f"c_pacc{i}", [128, 128], F32))) for i in range(2)]
        p.dma("G", obuf2, [vbosrep], lambda e: e.dma_start(
            out=obuf2.h[0:1, :], in_=vbosrep.h[0:1, :]))
        for t in range(NT):
            wlo = max(0, 128 * (t - 1))
            whi = min(T, 128 * (t + 2))
            width = whi - wlo
            mko = 0 if t == 0 else 392
            nch = width // 128
            p.dma("Y", qTl, [], [
                (lambda e, t=t, c=c: e.dma_start(
                    out=qTl.h[:, c * 128:(c + 1) * 128],
                    in_=ap["qTd"][c * 128:(c + 1) * 128, t * 128:(t + 1) * 128]))
                for c in range(8)])
            for h in range(16):
                po = (h % 2) * 64
                ch = h // 2
                g = p.group("T").r(qTl, kbosT, kT).w(psc)
                g.add(lambda t_, po=po, ch=ch: t_.matmul(
                    psc.h[:, 0:1], qTl.h[po:po + 64, ch * 128:(ch + 1) * 128],
                    kbosT.h[po:po + 64, ch:ch + 1], start=True, stop=True))
                g.add(lambda t_, po=po, ch=ch, wlo=wlo, width=width: t_.matmul(
                    psc.h[:, 1:1 + width],
                    qTl.h[po:po + 64, ch * 128:(ch + 1) * 128],
                    kT.h[po:po + 64, ch * T + wlo:ch * T + wlo + width],
                    start=True, stop=True))
                g.fin()
                g1(p, "V", [psc, mks], [smf], lambda v, mko=mko, width=width: v.tensor_tensor(
                    smf.h[:, 0:width + 1], psc.h[:, 0:width + 1],
                    mks.h[:, mko:mko + width + 1], ALU.add))
                g1(p, "S", [smf, cmisc], [Pall, den], lambda s, h=h, width=width: s.activation(
                    Pall.h[:, h * 392:h * 392 + width + 1], smf.h[:, 0:width + 1],
                    AF.Exp, bias=cmisc.h[:, 4:5], scale=0.125,
                    accum_out=den.h[:, 0:1]))
                g1(p, "V", [den], [den], lambda v: v.reciprocal(den.h[:, 1:2], den.h[:, 0:1]))
                g1(p, "V", [Pall, den], [Pall], lambda v, h=h, width=width: v.tensor_scalar(
                    Pall.h[:, h * 392:h * 392 + width + 1],
                    Pall.h[:, h * 392:h * 392 + width + 1],
                    den.h[:, 1:2], None, ALU.mult))
            for g_ in range(16):
                eng = "V"
                par2 = "V" if g_ % 2 == 0 else "G"
                macc = maccE[par2]; Pg = PgE[par2]; Pgc = PgcE[par2]; ysbe = ysbE[par2]
                pac = pacc[g_ % 2]
                PTe = PTE[par2]
                g1(p, eng, [Pall, wthr], [macc[0]], lambda v, g_=g_, width=width, macc=macc: v.tensor_scalar(
                    macc[0].h[:, 0:width + 1], Pall.h[:, 0:width + 1],
                    wthr.h[:, g_ * 16:g_ * 16 + 1], None, ALU.mult))
                for h in range(1, 16):
                    g1(p, eng, [Pall, wthr, macc[(h - 1) % 2]], [macc[h % 2]],
                       lambda v, g_=g_, h=h, width=width, macc=macc: v.scalar_tensor_tensor(
                           macc[h % 2].h[:, 0:width + 1],
                           Pall.h[:, h * 392:h * 392 + width + 1],
                           wthr.h[:, g_ * 16 + h:g_ * 16 + h + 1],
                           macc[(h - 1) % 2].h[:, 0:width + 1], ALU.mult, ALU.add))
                g1(p, eng, [macc[1]], [Pg], lambda v, width=width, macc=macc, Pg=Pg: v.tensor_copy(
                    Pg.h[:, 0:width + 1], macc[1].h[:, 0:width + 1]))
                g1(p, eng, [macc[1]], [Pgc], lambda v, macc=macc, Pgc=Pgc: v.tensor_copy(
                    Pgc.h[:], macc[1].h[:, 0:1]))
                g = p.group("T").r(Pg, ctx["ident"]).w(ptr2)
                for c in range(nch):
                    g.add(lambda e, c=c, Pg=Pg: e.transpose(
                        ptr2.h[:, c * 128:(c + 1) * 128],
                        Pg.h[:, 1 + c * 128:1 + (c + 1) * 128], ctx["ident"].h[:]))
                g.fin()
                g1(p, "S", [ptr2], [PTe], lambda ss, nch=nch, PTe=PTe: ss.copy(
                    PTe.h[:, 0:nch * 128], ptr2.h[:, 0:nch * 128]))
                g = p.group("T").r(PTe, vb).w(pac)
                t0 = wlo // 128
                for c in range(nch):
                    g.add(lambda t_, c=c, t0=t0, g_=g_, nch=nch, PTe=PTe, pac=pac: t_.matmul(
                        pac.h[:, 0:64], PTe.h[:, c * 128:(c + 1) * 128],
                        vb.h[:, (t0 + c) * 1024 + g_ * 64:(t0 + c) * 1024 + g_ * 64 + 64],
                        start=(c == 0), stop=(c == nch - 1)))
                g.fin()
                g1(p, "V", [vbosrep, Pgc, pac], [ysbe],
                   lambda v, g_=g_, Pgc=Pgc, pac=pac, ysbe=ysbe: v.scalar_tensor_tensor(
                       ysbe.h[:, g_ * 64:(g_ + 1) * 64], vbosrep.h[:, g_ * 64:(g_ + 1) * 64],
                       Pgc.h[:], pac.h[:, 0:64], ALU.mult, ALU.add))
            g1(p, "V", [ysbE["V"]], [obf], lambda v: v.tensor_copy(
                ap3(obf.h, 0, [[1024, 128], [128, 8], [1, 64]]),
                ap3(ysbE["V"].h, 0, [[1024, 128], [128, 8], [1, 64]])))
            g1(p, "V", [ysbE["G"]], [obf], lambda v: v.tensor_copy(
                ap3(obf.h, 64, [[1024, 128], [128, 8], [1, 64]]),
                ap3(ysbE["G"].h, 64, [[1024, 128], [128, 8], [1, 64]])))
            if t < NT - 1:
                p.dma("G", obuf2, [obf], lambda e, t=t: e.dma_start(
                    out=obuf2.h[t * 128 + 1:t * 128 + 129, :], in_=obf.h[:]))
            else:
                p.dma("G", obuf2, [obf], lambda e, t=t: e.dma_start(
                    out=obuf2.h[t * 128 + 1:t * 128 + 128, :], in_=obf.h[0:127, :]))
        p.barrier()


def _l0b_out(nc, p, ap, ctx, xin, xout, vbosrep):
    obuf2 = Buf(ap["obuf2"])
    with contextlib.ExitStack() as st2:
        E2 = st2.enter_context
        wsb = load_w_bf16(nc, p, st2, ap, "l0b_wout", D, D)
        br = Buf(E2(SB(nc, "c_obr", [128, D], F32)))
        p.dma("Y", br, [], lambda e: e.dma_start(out=br.h[:], in_=ap["l0b_bout"]))
        ob = [Buf(E2(SB(nc, "c_oob", [128, D], BF16)))] * 2
        xr = Buf(E2(SB(nc, "c_oxr", [128, D], F32)))
        oT = Buf(E2(SB(nc, "c_ooT", [128, D], BF16)))
        yn = Buf(E2(SB(nc, "c_oyn", [128, D], F32)))
        ptr = Buf(E2(PS(nc, "c_optr", [128, D], BF16)))
        pmm = Buf(E2(PS(nc, "c_opmm", [128, D], F32)))
        for t in range(NT):
            i = t % 2
            p.dma("Y", ob[i], [obuf2], lambda e, t=t, i=i: e.dma_start(
                out=ob[i].h[:], in_=obuf2.h[t * 128:(t + 1) * 128, :]))
            p.dma("Y", xr, [xin], lambda e, t=t: e.dma_start(
                out=xr.h[:], in_=xin.h[t * 128:(t + 1) * 128, :]))
            transposes(p, ctx, ob[i], ptr, oT, 8)
            mm_proj(p, oT, wsb, pmm, D)
            g1(p, "V", [pmm, br], [yn], lambda v: v.tensor_tensor(
                yn.h[:], pmm.h[:], br.h[:], ALU.add))
            g1(p, "V", [yn, xr], [yn], lambda v: v.tensor_tensor(
                yn.h[:], yn.h[:], xr.h[:], ALU.add))
            p.dma("G", xout, [yn], lambda e, t=t: e.dma_start(
                out=xout.h[t * 128:(t + 1) * 128, :], in_=yn.h[:]))
        p.barrier()


def phase_l1b(nc, p, st, ap, ctx, xin, xout):
  with contextlib.ExitStack() as stp:
    Ep = stp.enter_context
    kT = Buf(Ep(SB(nc, "b_kT", [128, 8 * T], BF16)))
    vb = Buf(Ep(SB(nc, "b_vb", [128, NT * 1152], BF16)))
    Vs = Buf(Ep(SB(nc, "b_Vs", [NT, 1152], BF16)))
    cmisc = ctx["cmisc"]
    qTd = Buf(ap["qTd"])
    g1(p, "V", [cmisc], [vb], lambda v: v.tensor_copy(
        ap3(vb.h, 64, [[NT * 1152, 128], [1152, NT], [72, 16], [1, 8]]),
        ap3(cmisc.h, 3, [[16, 128], [0, NT], [0, 16], [0, 8]])))

    for proj in ("k", "q", "v"):
        _l1b_proj(nc, p, ap, ctx, xin, proj, kT, vb, Vs, qTd)
    _l1b_attn(nc, p, ap, ctx, xin, kT, vb, Vs)
    _l1b_out(nc, p, ap, ctx, xin, xout)


def _l1b_proj(nc, p, ap, ctx, xin, proj, kT, vb, Vs, qTd):
    cmisc = ctx["cmisc"]
    with contextlib.ExitStack() as st2:
        E2 = st2.enter_context
        wsb = load_w_bf16(nc, p, st2, ap, "l1b_w" + proj, D, D)
        br = Buf(E2(SB(nc, "b_br", [128, D], F32)))
        p.dma("Y", br, [], lambda e: e.dma_start(out=br.h[:], in_=ap["l1b_b" + proj]))
        onec = Buf(E2(SB(nc, "b_onec", [128, 8], BF16)))
        g1(p, "V", [cmisc], [onec], lambda v: v.tensor_copy(
            onec.h[:], ap3(cmisc.h, 3, [[16, 128], [0, 8]])))
        xf = [Buf(E2(SB(nc, f"b_xf{i}", [128, D], F32))) for i in range(2)]
        stats = [Buf(E2(SB(nc, f"b_st{i}", [128, 16], F32))) for i in range(2)]
        xn = [Buf(E2(SB(nc, f"b_xn{i}", [128, D], BF16))) for i in range(2)]
        xnT = [Buf(E2(SB(nc, f"b_xnT{i}", [128, D], BF16))) for i in range(2)]
        tmpf = [Buf(E2(SB(nc, "b_tmpf0", [128, D], F32)))] * 2
        osb = [Buf(E2(SB(nc, "b_osb0", [128, D], BF16)))] * 2
        vrow = Buf(E2(SB(nc, "b_vrow", [1, 1152], BF16)))
        ptr = [Buf(E2(PS(nc, f"b_ptr{i}", [128, D], BF16))) for i in range(2)]
        pmm = [Buf(E2(PS(nc, f"b_pmm{i}", [128, D], F32))) for i in range(2)]
        pvs = Buf(E2(PS(nc, "b_pvs", [1, 576], F32)))
        for t in range(NT):
            i = t % 2
            p.dma("Y", xf[i], [xin], lambda e, t=t, i=i: e.dma_start(
                out=xf[i].h[:], in_=xin.h[t * 128:(t + 1) * 128, :]))
            emit_ln_tile(p, ctx, xf[i], stats[i], xn[i])
            transposes(p, ctx, xn[i], ptr[i], xnT[i], 8)
            mm_proj(p, xnT[i], wsb, pmm[i], D)
            g1(p, "V", [pmm[i], br], [tmpf[i]], lambda v, i=i: v.tensor_tensor(
                tmpf[i].h[:], pmm[i].h[:], br.h[:], ALU.add))
            if proj == "v":
                g1(p, "V", [tmpf[i]], [vb], lambda v, t=t, i=i: v.tensor_copy(
                    ap3(vb.h, t * 1152, [[NT * 1152, 128], [72, 16], [1, 64]]),
                    ap3(tmpf[i].h, 0, [[1024, 128], [64, 16], [1, 64]])))
                for hb in (0, 576):
                    g = p.group("T").r(vb, onec).w(pvs)
                    for (o, wd) in ((0, 512), (512, 64)):
                        g.add(lambda t_, t=t, o=o, wd=wd, hb=hb: t_.matmul(
                            pvs.h[:, o:o + wd], onec.h[:, 0:1],
                            vb.h[:, t * 1152 + hb + o:t * 1152 + hb + o + wd],
                            start=True, stop=True))
                    g.fin()
                    g1(p, "S", [pvs], [vrow], lambda s, hb=hb: s.copy(
                        vrow.h[:, hb:hb + 576], pvs.h[0:1, 0:576]))
                p.dma("G", Vs, [vrow], lambda e, t=t: e.dma_start(
                    out=Vs.h[t:t + 1, :], in_=vrow.h[:]))
                continue
            g1(p, "V", [tmpf[i]], [osb[i]], lambda v, i=i: v.tensor_copy(osb[i].h[:], tmpf[i].h[:]))
            g = p.group("T").r(osb[i], ctx["ident"]).w(ptr[i])
            for c in range(8):
                g.add(lambda e, c=c, i=i: e.transpose(ptr[i].h[:, c * 128:(c + 1) * 128],
                                                 osb[i].h[:, c * 128:(c + 1) * 128],
                                                 ctx["ident"].h[:]))
            g.fin()
            if proj == "k":
                g1(p, "V", [ptr[i]], [kT], lambda v, t=t, i=i: v.tensor_copy(
                    ap3(kT.h, t * 128, [[8 * T, 128], [T, 8], [1, 128]]),
                    ap3(ptr[i].h, 0, [[1024, 128], [128, 8], [1, 128]])))
            else:
                g1(p, "V", [ptr[i]], [osb[i]], lambda v, i=i: v.tensor_copy(osb[i].h[:], ptr[i].h[:]))
                p.dma("G", qTd, [osb[i]], [
                    (lambda e, t=t, c=c, i=i: e.dma_start(
                        out=qTd.h[c * 128:(c + 1) * 128, t * 128:(t + 1) * 128],
                        in_=osb[i].h[:, c * 128:(c + 1) * 128]))
                    for c in range(8)])
        p.barrier()


def _l1b_attn(nc, p, ap, ctx, xin, kT, vb, Vs):
    cmisc = ctx["cmisc"]
    maskF = ctx["maskF"]
    utc = ctx["utc"]
    u32 = ctx["u32"]
    ydr = Buf(ap["ybB"])
    with contextlib.ExitStack() as st2:
        E2 = st2.enter_context
        qTg = Buf(E2(SB(nc, "b_qTg", [128, 8 * 512], BF16)))
        futv = Buf(E2(SB(nc, "b_futv", [128, 4 * 1152], F32)))
        pT = [Buf(E2(SB(nc, f"b_pT{i}", [128, 512], BF16))) for i in range(2)]
        nf = Buf(E2(SB(nc, "b_nf", [128, 64], F32)))
        dn = Buf(E2(SB(nc, "b_dn", [128, 2], F32)))
        ysb = [Buf(E2(SB(nc, f"b_ysb{i}", [128, D], F32))) for i in range(4)]
        psT = [Buf(E2(PS(nc, f"b_psT{i}", [128, 512], F32))) for i in range(2)]
        pacc = [Buf(E2(PS(nc, f"b_pacc{i}", [128, 128], F32))) for i in range(4)]
        pfut = Buf(E2(PS(nc, "b_pfut", [128, 576], F32)))
        for qg in range(T // 512):
            p.dma("Y", qTg, [], [
                (lambda e, qg=qg, c=c: e.dma_start(
                    out=qTg.h[:, c * 512:(c + 1) * 512],
                    in_=ap["qTd"][c * 128:(c + 1) * 128, qg * 512:(qg + 1) * 512]))
                for c in range(8)])
            for qc in range(4):
                qt = 4 * qg + qc
                for half in range(2):
                    hb = half * 576
                    g = p.group("T").r(utc, vb, u32, Vs).w(pfut)
                    for (o, wd) in ((0, 512), (512, 64)):
                        g.add(lambda t_, qt=qt, o=o, wd=wd, hb=hb: t_.matmul(
                            pfut.h[:, o:o + wd], utc.h[:],
                            vb.h[:, qt * 1152 + hb + o:qt * 1152 + hb + o + wd],
                            start=True, stop=False))
                        g.add(lambda t_, qt=qt, o=o, wd=wd, hb=hb: t_.matmul(
                            pfut.h[:, o:o + wd],
                            ap3(u32.h, qt, [[NT, NT], [0, 128]]),
                            Vs.h[:, hb + o:hb + o + wd],
                            start=False, stop=True))
                    g.fin()
                    g1(p, "S", [pfut], [futv], lambda s, qc=qc, hb=hb: s.activation(
                        futv.h[:, qc * 1152 + hb:qc * 1152 + hb + 576], pfut.h[:, 0:576],
                        AF.Copy, scale=EM8))
            for h in range(16):
                po = (h % 2) * 64
                ch = h // 2
                def emit_avb(kt, h=h, qg=qg):
                    par = kt % 2
                    j = kt - 4 * qg
                    for qc in range(max(0, j), 4):
                        qt = 4 * qg + qc
                        g1(p, "T", [pT[par], vb], [pacc[qc]],
                           lambda t_, qc=qc, kt=kt, h=h, qt=qt, par=par: t_.matmul(
                               pacc[qc].h[:, 0:72],
                               pT[par].h[:, qc * 128:qc * 128 + 128],
                               vb.h[:, kt * 1152 + h * 72:kt * 1152 + h * 72 + 72],
                               start=(kt == 0), stop=(kt == qt)))
                for kt in range(4 * qg + 4):
                    par = kt % 2
                    g1(p, "T", [kT, qTg], [psT[par]], lambda t_, po=po, ch=ch, kt=kt, par=par: t_.matmul(
                        psT[par].h[:],
                        kT.h[po:po + 64, ch * T + kt * 128:ch * T + kt * 128 + 128],
                        qTg.h[po:po + 64, ch * 512:(ch + 1) * 512],
                        start=True, stop=True))
                    g1(p, "S", [psT[par], cmisc], [pT[par]], lambda s, par=par: s.activation(
                        pT[par].h[:], psT[par].h[:], AF.Exp, bias=cmisc.h[:, 4:5], scale=0.125))
                    j = kt - 4 * qg
                    if j >= 0:
                        g1(p, "V", [maskF, pT[par]], [pT[par]], lambda v, j=j, par=par: v.tensor_tensor(
                            pT[par].h[:], pT[par].h[:], maskF.h[:, j * 512:(j + 1) * 512], ALU.mult))
                    if kt > 0:
                        emit_avb(kt - 1)
                emit_avb(4 * qg + 3)
                for qc in range(4):
                    g1(p, "V", [pacc[qc], futv], [dn], lambda v, qc=qc, h=h: v.tensor_tensor(
                        dn.h[:, 0:1], pacc[qc].h[:, 64:65],
                        futv.h[:, qc * 1152 + h * 72 + 64:qc * 1152 + h * 72 + 65], ALU.add))
                    g1(p, "V", [dn], [dn], lambda v: v.reciprocal(dn.h[:, 1:2], dn.h[:, 0:1]))
                    g1(p, "V", [pacc[qc], futv], [nf], lambda v, qc=qc, h=h: v.tensor_tensor(
                        nf.h[:], pacc[qc].h[:, 0:64],
                        futv.h[:, qc * 1152 + h * 72:qc * 1152 + h * 72 + 64], ALU.add))
                    g1(p, "V", [nf, dn], [ysb[qc]], lambda v, qc=qc, h=h: v.tensor_scalar(
                        ysb[qc].h[:, h * 64:(h + 1) * 64], nf.h[:], dn.h[:, 1:2],
                        None, ALU.mult))
            for qc in range(4):
                qt = 4 * qg + qc
                p.dma("G", ydr, [ysb[qc]], lambda e, qt=qt, qc=qc: e.dma_start(
                    out=ydr.h[qt * 128:(qt + 1) * 128, :], in_=ysb[qc].h[:]))
        p.barrier()


def _l1b_out(nc, p, ap, ctx, xin, xout):
    ydr = Buf(ap["ybB"])
    with contextlib.ExitStack() as st2:
        E2 = st2.enter_context
        wsb = load_w_bf16(nc, p, st2, ap, "l1b_wout", D, D)
        br = Buf(E2(SB(nc, "b_bro", [128, D], F32)))
        p.dma("Y", br, [], lambda e: e.dma_start(out=br.h[:], in_=ap["l1b_bout"]))
        yf = [Buf(E2(SB(nc, "b_oyf", [128, D], F32)))] * 2
        xr = Buf(E2(SB(nc, "b_oxr", [128, D], F32)))
        ybf = Buf(E2(SB(nc, "b_oybf", [128, D], BF16)))
        yT = Buf(E2(SB(nc, "b_oyT", [128, D], BF16)))
        yn = Buf(E2(SB(nc, "b_oyn", [128, D], F32)))
        ptr = Buf(E2(PS(nc, "b_optr", [128, D], BF16)))
        pmm = Buf(E2(PS(nc, "b_opmm", [128, D], F32)))
        for t in range(NT):
            i = t % 2
            p.dma("Y", yf[i], [ydr], lambda e, t=t, i=i: e.dma_start(
                out=yf[i].h[:], in_=ydr.h[t * 128:(t + 1) * 128, :]))
            p.dma("Y", xr, [xin], lambda e, t=t: e.dma_start(
                out=xr.h[:], in_=xin.h[t * 128:(t + 1) * 128, :]))
            g1(p, "V", [yf[i]], [ybf], lambda v, i=i: v.tensor_copy(ybf.h[:], yf[i].h[:]))
            transposes(p, ctx, ybf, ptr, yT, 8)
            mm_proj(p, yT, wsb, pmm, D)
            g1(p, "V", [pmm, br], [yn], lambda v: v.tensor_tensor(
                yn.h[:], pmm.h[:], br.h[:], ALU.add))
            g1(p, "V", [yn, xr], [yn], lambda v: v.tensor_tensor(
                yn.h[:], yn.h[:], xr.h[:], ALU.add))
            p.dma("G", xout, [yn], lambda e, t=t: e.dma_start(
                out=xout.h[t * 128:(t + 1) * 128, :], in_=yn.h[:]))
        p.barrier()


def phase_l1f(nc, p, st, ap, ctx, xin, xout):
    _l1f_half(nc, p, st, ap, ctx, xin, Buf(ap["ybB"]), range(0, 4), first=True)
    _l1f_half(nc, p, st, ap, ctx, xin, xout, range(4, 8), first=False, partial=Buf(ap["ybB"]))


def _l1f_half(nc, p, st, ap, ctx, xin, xout, erange, first, partial=None):
    cmisc = ctx["cmisc"]
    gsave = Buf(ap["l1f_gsave"])
    with contextlib.ExitStack() as s2:
        E2 = s2.enter_context
        ew = [load_w_bf16(nc, p, s2, ap, f"l1f_ew{e}", D, D) for e in erange]
        if first:
            gwr = Buf(E2(SB(nc, "m_gwr", [128, 8192], F32)))
            p.dma("Y", gwr, [], lambda e: e.dma_start(out=gwr.h[:], in_=ap["l1f_gwr"]))
            gbr = Buf(E2(SB(nc, "m_gbr", [128, 8], F32)))
            p.dma("Y", gbr, [], lambda e: e.dma_start(out=gbr.h[:], in_=ap["l1f_gb"]))
        ebr = Buf(E2(SB(nc, "m_ebr", [128, 4096], F32)))
        p.dma("Y", ebr, [], lambda e, e0=erange[0]: e.dma_start(
            out=ebr.h[:], in_=ap["l1f_eb"][:, e0 * D:(e0 + 4) * D]))
        xf = [Buf(E2(SB(nc, f"m_xf{i}", [128, D], F32))) for i in range(2)]
        pf = Buf(E2(SB(nc, "m_pf", [128, D], F32)))
        stats = Buf(E2(SB(nc, "m_st", [128, 16], F32)))
        xnf = Buf(E2(SB(nc, "m_xnf", [128, D], F32)))
        gt = Buf(E2(SB(nc, "m_gt", [128, D], F32)))
        xn = Buf(E2(SB(nc, "m_xn", [128, D], BF16)))
        xnT = Buf(E2(SB(nc, "m_xnT", [128, D], BF16)))
        gf = Buf(E2(SB(nc, "m_gf", [128, 32], F32)))
        acc = [Buf(E2(SB(nc, f"m_acc{i}", [128, D], F32))) for i in range(2)]
        tmp = Buf(E2(SB(nc, "m_tmp", [128, D], F32)))
        ptr = Buf(E2(PS(nc, "m_ptr", [128, D], BF16)))
        pe = [Buf(E2(PS(nc, f"m_pe{i}", [128, D], F32))) for i in range(2)]
        for t in range(NT):
            i = t % 2
            p.dma("Y", xf[i], [xin], lambda e, t=t, i=i: e.dma_start(
                out=xf[i].h[:], in_=xin.h[t * 128:(t + 1) * 128, :]))
            emit_ln_tile(p, ctx, xf[i], stats, xnf)
            g1(p, "V", [xnf], [xn], lambda v: v.tensor_copy(xn.h[:], xnf.h[:]))
            transposes(p, ctx, xn, ptr, xnT, 8)
            if first:
                for e in range(8):
                    g1(p, "V", [xnf, gwr], [gt], lambda v, e=e: v.tensor_tensor(
                        gt.h[:], xnf.h[:], gwr.h[:, e * D:(e + 1) * D], ALU.mult))
                    g1(p, "V", [gt], [gf], lambda v, e=e: v.reduce_sum(
                        gf.h[:, 24 + e:25 + e], gt.h[:], axis=AX.X))
                g1(p, "V", [gf, gbr], [gf], lambda v: v.tensor_tensor(
                    gf.h[:, 0:8], gf.h[:, 24:32], gbr.h[:], ALU.add))
                g1(p, "V", [gf], [gf], lambda v: v.reduce_max(gf.h[:, 16:17], gf.h[:, 0:8], axis=AX.X))
                g1(p, "V", [gf], [gf], lambda v: v.tensor_single_scalar(
                    gf.h[:, 17:18], gf.h[:, 16:17], -1.0, ALU.mult))
                g1(p, "S", [gf], [gf], lambda s: s.activation(
                    gf.h[:, 0:8], gf.h[:, 0:8], AF.Exp, bias=gf.h[:, 17:18],
                    accum_out=gf.h[:, 18:19]))
                g1(p, "V", [gf], [gf], lambda v: v.reciprocal(gf.h[:, 19:20], gf.h[:, 18:19]))
                g1(p, "V", [gf], [gf], lambda v: v.tensor_scalar(
                    gf.h[:, 0:8], gf.h[:, 0:8], gf.h[:, 19:20], None, ALU.mult))
                g1(p, "V", [gf], [gf], lambda v: v.tensor_single_scalar(
                    gf.h[:, 8:16], gf.h[:, 0:8], 0.1, ALU.is_ge))
                g1(p, "V", [gf], [gf], lambda v: v.tensor_tensor(
                    gf.h[:, 8:16], gf.h[:, 8:16], gf.h[:, 0:8], ALU.mult))
                g1(p, "V", [gf], [gf], lambda v: v.reduce_sum(gf.h[:, 20:21], gf.h[:, 8:16], axis=AX.X))
                g1(p, "V", [gf], [gf], lambda v: v.tensor_single_scalar(
                    gf.h[:, 21:22], gf.h[:, 20:21], 0.0, ALU.is_equal))
                g1(p, "V", [gf], [gf], lambda v: v.tensor_tensor(
                    gf.h[:, 20:21], gf.h[:, 20:21], gf.h[:, 21:22], ALU.add))
                g1(p, "V", [gf], [gf], lambda v: v.reciprocal(gf.h[:, 22:23], gf.h[:, 20:21]))
                g1(p, "V", [gf], [gf], lambda v: v.tensor_scalar(
                    gf.h[:, 8:16], gf.h[:, 8:16], gf.h[:, 22:23], None, ALU.mult))
                p.dma("G", gsave, [gf], lambda e, t=t: e.dma_start(
                    out=gsave.h[t * 128:(t + 1) * 128, :], in_=gf.h[:, 8:16]))
            else:
                p.dma("Y", gf, [gsave], lambda e, t=t: e.dma_start(
                    out=gf.h[:, 8:16], in_=gsave.h[t * 128:(t + 1) * 128, :]))
                p.dma("Y", pf, [partial], lambda e, t=t: e.dma_start(
                    out=pf.h[:], in_=partial.h[t * 128:(t + 1) * 128, :]))
            for j, e in enumerate(erange):
                jj = j % 2
                mm_proj(p, xnT, ew[j], pe[jj], D)
                g1(p, "V", [pe[jj], ebr], [tmp], lambda v, jj=jj, j=j: v.tensor_tensor(
                    tmp.h[:], pe[jj].h[:], ebr.h[:, j * D:(j + 1) * D], ALU.add))
                if j == 0:
                    base = xf[i] if first else pf
                    g1(p, "V", [tmp, gf, base], [acc[0]],
                       lambda v, base=base, e=e: v.scalar_tensor_tensor(
                           acc[0].h[:], tmp.h[:], gf.h[:, 8 + e:9 + e], base.h[:],
                           ALU.mult, ALU.add))
                else:
                    g1(p, "V", [tmp, gf, acc[(j - 1) % 2]], [acc[j % 2]],
                       lambda v, j=j, e=e: v.scalar_tensor_tensor(
                           acc[j % 2].h[:], tmp.h[:], gf.h[:, 8 + e:9 + e],
                           acc[(j - 1) % 2].h[:], ALU.mult, ALU.add))
            p.dma("G", xout, [acc[1]], lambda e_, t=t: e_.dma_start(
                out=xout.h[t * 128:(t + 1) * 128, :], in_=acc[1].h[:]))
        p.barrier()


def finalize_copy(nc, p, st, src, dst):
    with contextlib.ExitStack() as st2:
        tb = [Buf(st2.enter_context(SB(nc, f"fc{i}", [128, D], F32))) for i in range(2)]
        for t in range(NT):
            b = tb[t % 2]
            p.dma("Y", b, [src], lambda e, t=t, b=b: e.dma_start(
                out=b.h[:], in_=src.h[t * 128:(t + 1) * 128, :]))
            p.dma("G", dst, [b], lambda e, t=t, b=b: e.dma_start(
                out=dst.h[t * 128:(t + 1) * 128, :], in_=b.h[:]))
        p.barrier()


def build_nc():
    nc = bass.Bass("TRN2", target_bir_lowering=False)
    ap = {}
    def inp(name, shape, dt=F32):
        ap[name] = nc.dram_tensor(name, list(shape), dt, kind="ExternalInput").ap()
    def scr(name, shape, dt=F32):
        ap[name] = nc.dram_tensor(name, list(shape), dt).ap()

    inp("x", (T, D))
    inp("l0a_wq", (D, D)); inp("l0a_wk", (D, D)); inp("l0a_wv", (D, D))
    for li in range(2):
        inp(f"l0f_wqkv{li}", (D, 1536)); inp(f"l0f_wo{li}", (512, D))
        inp(f"l0f_w1{li}", (D, 5460)); inp(f"l0f_w2{li}", (2730, D))
        inp(f"l0f_qs{li}", (128, 512)); inp(f"l0f_ks{li}", (128, 512))
        inp(f"l0f_lg{li}", (128, D)); inp(f"l0f_lb{li}", (128, D))
        inp(f"l0f_b1{li}", (128, 5460)); inp(f"l0f_b2{li}", (128, D))
    for nm in ("wk", "wq", "wv", "wout"):
        inp("l0b_" + nm, (D, D))
    inp("l0b_bout", (128, D)); inp("l0b_wthr", (128, 256)); inp("l0b_masks", (128, 784))
    for nm in ("wk", "wq", "wv", "wout"):
        inp("l1b_" + nm, (D, D))
    for nm in ("bk", "bq", "bv", "bout"):
        inp("l1b_" + nm, (128, D))
    inp("c_utc", (128, 128))
    inp("c_zrow", (1, D))
    inp("c_u32", (NT, NT))
    for nm in ("wqT", "wkT", "wvT", "woT"):
        inp("l1a_" + nm, (D, D))
    for nm in ("bq", "bk", "bv", "bo"):
        inp("l1a_" + nm, (128, D))
    inp("l1f_gwr", (128, 8192)); inp("l1f_gb", (128, 8)); inp("l1f_eb", (128, 8192))
    for e in range(8):
        inp(f"l1f_ew{e}", (D, D))
    inp("c_ident", (128, 128))
    inp("c_misc", (128, 16))
    inp("c_maskF", (128, 4 * 512))
    inp("c_maskW", (128, 8 * 512))
    for nm in ("c_csq", "c_snq", "c_csk", "c_snk"):
        inp(nm, (T, 512))
    out_h = nc.dram_tensor("out", [T, D], F32, kind="ExternalOutput").ap()
    scr("sA", (T, D)); scr("sB", (T, D))
    scr("ybA", (T, D)); scr("ybB", (T, D))
    scr("qTd", (D, T), BF16)
    scr("l1f_gsave", (T, 8))
    scr("obuf", (T, 512), BF16)
    scr("obuf2", (T, D), BF16)
    scr("vbos_d", (1, D), BF16)
    scr("hmT", (2816, T), BF16)

    with contextlib.ExitStack() as st:
        p = Prog(nc, st)
        E = st.enter_context
        ident = Buf(E(SB(nc, "ident", [128, 128], BF16)))
        cmisc = Buf(E(SB(nc, "cmisc", [128, 16], F32)))
        maskF = Buf(E(SB(nc, "maskF", [128, 2048], BF16)))
        maskW = Buf(E(SB(nc, "maskW", [128, 4096], F32)))
        identf = Buf(E(SB(nc, "identf", [128, 128], F32)))
        utc = Buf(E(SB(nc, "utc", [128, 128], BF16)))
        u32 = Buf(E(SB(nc, "u32", [NT, NT], BF16)))
        p.dma("G", cmisc, [], lambda e: e.dma_start(out=cmisc.h[:], in_=ap["c_misc"]))
        p.dma("G", identf, [], lambda e: e.dma_start(out=identf.h[:], in_=ap["c_ident"]))
        g1(p, "V", [identf], [ident], lambda v: v.tensor_copy(ident.h[:], identf.h[:]))
        with contextlib.ExitStack() as s2:
            mtmp = Buf(s2.enter_context(SB(nc, "mtmp", [128, 4096], F32)))
            p.dma("G", mtmp, [], lambda e: e.dma_start(out=mtmp.h[:, 0:2048], in_=ap["c_maskF"]))
            g1(p, "V", [mtmp], [maskF], lambda v: v.tensor_copy(maskF.h[:], mtmp.h[:, 0:2048]))
            p.dma("G", mtmp, [maskF], lambda e: e.dma_start(out=mtmp.h[:, 0:128], in_=ap["c_utc"]))
            g1(p, "V", [mtmp], [utc], lambda v: v.tensor_copy(utc.h[:], mtmp.h[:, 0:128]))
            p.dma("G", mtmp, [utc], lambda e: e.dma_start(out=mtmp.h[0:NT, 0:NT], in_=ap["c_u32"]))
            g1(p, "V", [mtmp], [u32], lambda v: v.tensor_copy(u32.h[:], mtmp.h[0:NT, 0:NT]))
            p.barrier()
        p.dma("G", maskW, [], lambda e: e.dma_start(out=maskW.h[:], in_=ap["c_maskW"]))

        ctx = dict(ident=ident, cmisc=cmisc, maskF=maskF, maskW=maskW, utc=utc, u32=u32)

        bufs = {"x": Buf(ap["x"]), "sA": Buf(ap["sA"]), "sB": Buf(ap["sB"]),
                "out": Buf(out_h)}
        chain = [ph for ph in PHASES if ph in ENABLED]
        cur = "x"
        nxt_cycle = ["sA", "sB"]
        for i, ph in enumerate(chain):
            tgt = "out" if i == len(chain) - 1 else nxt_cycle[i % 2]
            if ph == "l0a":
                phase_l0a(nc, p, st, ap, ctx, bufs[cur], bufs[tgt])
            elif ph in ("l0f0", "l0f1"):
                phase_l0f(nc, p, st, ap, ctx, bufs[cur], bufs[tgt])
            elif ph == "l0b":
                phase_l0b(nc, p, st, ap, ctx, bufs[cur], bufs[tgt])
            elif ph == "l1b":
                phase_l1b(nc, p, st, ap, ctx, bufs[cur], bufs[tgt])
            elif ph == "l1a":
                phase_l1a(nc, p, st, ap, ctx, bufs[cur], bufs[tgt])
            elif ph in ("l1f0", "l1f1"):
                phase_l1f(nc, p, st, ap, ctx, bufs[cur], bufs[tgt])
            else:
                raise NotImplementedError(ph)
            cur = tgt
        if cur != "out":
            finalize_copy(nc, p, st, bufs[cur], bufs["out"])
        p.barrier()
        blk = st.enter_context(nc.Block())
        p.flush(blk)
    return nc


def host_consts():
    c = {}
    c["c_ident"] = np.eye(128, dtype=np.float32)
    mc = np.zeros((128, 16), dtype=np.float32)
    mc[:, 0] = 1e-5; mc[:, 1] = -12.0; mc[:, 2] = 0.0; mc[:, 3] = 1.0
    mc[:, 4] = -8.0; mc[:, 5] = 1e-24
    c["c_misc"] = mc
    c["c_zrow"] = np.zeros((1, D), dtype=np.float32)
    mf = np.zeros((128, 4 * 512), dtype=np.float32)
    kl = np.arange(128)[:, None]
    qf = np.arange(512)[None, :]
    for j in range(4):
        mf[:, j * 512:(j + 1) * 512] = (kl + 128 * j <= qf).astype(np.float32)
    c["c_maskF"] = mf
    mw = np.zeros((128, 8 * 512), dtype=np.float32)
    il = np.arange(512)[None, :]
    for kt in range(8):
        jl = kt * 128 + np.arange(128)[:, None]
        mw[:, kt * 512:(kt + 1) * 512] = np.where((jl - 512 <= il) & (il <= jl), 0.0, -1e30).astype(np.float32)
    c["c_maskW"] = mw
    mk = np.full((128, 784), -1e30, dtype=np.float32)
    ql = np.arange(128)[:, None]
    j = np.arange(384)[None, :]
    a0 = (j[:, :256] <= ql) & (np.abs(j[:, :256] // 16 - ql // 16) <= 2)
    mk[:, 0:1] = 0.0
    mk[:, 1:257] = np.where(a0, 0.0, -1e30)
    a1 = (j <= 128 + ql) & (6 <= j // 16 - ql // 16) & (j // 16 - ql // 16 <= 10)
    mk[:, 392:393] = 0.0
    mk[:, 393:777] = np.where(a1, 0.0, -1e30)
    c["l0b_masks_const"] = mk
    sgrid = np.arange(128)
    c["c_utc"] = (sgrid[:, None] > sgrid[None, :]).astype(np.float32)
    tg = np.arange(NT)
    c["c_u32"] = (tg[:, None] > tg[None, :]).astype(np.float32)
    half = np.arange(0, 64, 2, dtype=np.float64)
    inv_freq = 1.0 / 10000.0 ** (half / 64.0)
    tt = np.arange(T, dtype=np.float64)
    freqs = tt[:, None] * inv_freq[None, :]
    freqs = np.concatenate([freqs, freqs], axis=-1)
    base = (half + 0.4 * 64.0) / (1.4 * 64.0)
    power = (tt - T // 2) / 256.0
    scale = base[None, :] ** power[:, None]
    scale = np.concatenate([scale, scale], axis=-1)
    cos, sin = np.cos(freqs), np.sin(freqs)
    c["c_csq"] = np.tile((cos * scale).astype(np.float32), (1, 8))
    c["c_snq"] = np.tile((sin * scale).astype(np.float32), (1, 8))
    c["c_csk"] = np.tile((cos / scale).astype(np.float32), (1, 8))
    c["c_snk"] = np.tile((sin / scale).astype(np.float32), (1, 8))
    return c


def stage_l0f(m, inputs):
    for li in range(2):
        m[f"l0f_wqkv{li}"] = np.ascontiguousarray(inputs["L0_ffn_wqkv"][li], np.float32)
        m[f"l0f_wo{li}"] = np.ascontiguousarray(inputs["L0_ffn_wo"][li], np.float32)
        m[f"l0f_w1{li}"] = np.ascontiguousarray(inputs["L0_ffn_w1"][li], np.float32)
        m[f"l0f_w2{li}"] = np.ascontiguousarray(inputs["L0_ffn_w2"][li], np.float32)
        m[f"l0f_qs{li}"] = np.ascontiguousarray(np.broadcast_to(
            np.tile(np.asarray(inputs["L0_ffn_q_scale"][li], np.float32), 8)[None, :], (128, 512)))
        m[f"l0f_ks{li}"] = np.ascontiguousarray(np.broadcast_to(
            np.tile(np.asarray(inputs["L0_ffn_k_scale"][li], np.float32), 8)[None, :], (128, 512)))
        m[f"l0f_lg{li}"] = np.ascontiguousarray(np.broadcast_to(
            np.asarray(inputs["L0_ffn_ln_g"][li], np.float32)[None, :], (128, D)))
        m[f"l0f_lb{li}"] = np.ascontiguousarray(np.broadcast_to(
            np.asarray(inputs["L0_ffn_ln_b"][li], np.float32)[None, :], (128, D)))
        m[f"l0f_b1{li}"] = np.ascontiguousarray(np.broadcast_to(
            np.asarray(inputs["L0_ffn_b1"][li], np.float32)[None, :], (128, 5460)))
        m[f"l0f_b2{li}"] = np.ascontiguousarray(np.broadcast_to(
            np.asarray(inputs["L0_ffn_b2"][li], np.float32)[None, :], (128, D)))


def kernel(**inputs):
    nc = build_nc()
    consts = host_consts()
    masks_const = consts.pop("l0b_masks_const")
    x_full = np.ascontiguousarray(inputs["L0_attn_a_x"], dtype=np.float32)
    in_maps = []
    for core in range(8):
        m = dict(consts)
        m["x"] = np.ascontiguousarray(x_full[core])
        m["l0a_wq"] = np.ascontiguousarray(inputs["L0_attn_a_w_q"], np.float32)
        m["l0a_wk"] = np.ascontiguousarray(inputs["L0_attn_a_w_k"], np.float32)
        m["l0a_wv"] = np.ascontiguousarray(inputs["L0_attn_a_w_v"], np.float32)
        stage_l0f(m, inputs)
        wqkv_b = np.asarray(inputs["L0_attn_b_w_qkv"], np.float32)
        m["l0b_wq"] = np.ascontiguousarray(wqkv_b[:, 0:1024])
        m["l0b_wk"] = np.ascontiguousarray(wqkv_b[:, 1024:2048])
        m["l0b_wv"] = np.ascontiguousarray(wqkv_b[:, 2048:3072])
        m["l0b_wout"] = np.ascontiguousarray(inputs["L0_attn_b_w_out"], np.float32)
        m["l0b_bout"] = np.ascontiguousarray(np.broadcast_to(
            np.asarray(inputs["L0_attn_b_b_out"], np.float32)[None, :], (128, D)))
        m["l0b_wthr"] = np.ascontiguousarray(np.broadcast_to(
            np.asarray(inputs["L0_attn_b_w_th"], np.float32).reshape(1, 256), (128, 256)))
        m["l0b_masks"] = masks_const
        m["l1b_wk"] = np.ascontiguousarray(np.asarray(
            inputs["L1_attn_b_W_key"], np.float32).transpose(1, 0, 2).reshape(D, D))
        m["l1b_wq"] = np.ascontiguousarray(np.asarray(
            inputs["L1_attn_b_W_query"], np.float32).transpose(1, 0, 2).reshape(D, D))
        m["l1b_wv"] = np.ascontiguousarray(np.asarray(
            inputs["L1_attn_b_W_values"], np.float32).transpose(1, 0, 2).reshape(D, D))
        m["l1b_wout"] = np.ascontiguousarray(np.asarray(
            inputs["L1_attn_b_W_output"], np.float32).reshape(D, D))
        for nm, key in (("bk", "L1_attn_b_Bias_key"), ("bq", "L1_attn_b_Bias_query"),
                        ("bv", "L1_attn_b_Bias_values")):
            m["l1b_" + nm] = np.ascontiguousarray(np.broadcast_to(
                np.asarray(inputs[key], np.float32).reshape(1, D), (128, D)))
        m["l1b_bout"] = np.ascontiguousarray(np.broadcast_to(
            np.asarray(inputs["L1_attn_b_Bias_output"], np.float32)[None, :], (128, D)))
        for nm, key in (("wqT", "L1_attn_a_Wq"), ("wkT", "L1_attn_a_Wk"),
                        ("wvT", "L1_attn_a_Wv"), ("woT", "L1_attn_a_Wo")):
            m["l1a_" + nm] = np.ascontiguousarray(np.asarray(inputs[key], np.float32).T)
        for nm, key in (("bq", "L1_attn_a_bq"), ("bk", "L1_attn_a_bk"),
                        ("bv", "L1_attn_a_bv"), ("bo", "L1_attn_a_bo")):
            m["l1a_" + nm] = np.ascontiguousarray(np.broadcast_to(
                np.asarray(inputs[key], np.float32)[None, :], (128, D)))
        m["l1f_gwr"] = np.ascontiguousarray(np.broadcast_to(np.asarray(
            inputs["L1_ffn_gate_w"], np.float32).T.reshape(1, 8192), (128, 8192)))
        m["l1f_gb"] = np.ascontiguousarray(np.broadcast_to(
            np.asarray(inputs["L1_ffn_gate_b"], np.float32)[None, :], (128, 8)))
        m["l1f_eb"] = np.ascontiguousarray(np.broadcast_to(
            np.asarray(inputs["L1_ffn_expert_b"], np.float32).reshape(1, 8192), (128, 8192)))
        for e in range(8):
            m[f"l1f_ew{e}"] = np.ascontiguousarray(inputs["L1_ffn_expert_w"][e], np.float32)
        in_maps.append(m)
    res = run_bass_kernel_spmd(nc, in_maps, core_ids=list(range(8)))
    out = np.stack([r["out"] for r in res.results], axis=0)
    return out.astype(np.float32)


# version 09
# speedup vs baseline: 1.1831x; 1.0283x over previous
import contextlib
import numpy as np
import concourse.bass as bass
import concourse.mybir as mybir
from concourse.bass_utils import run_bass_kernel_spmd

F32 = mybir.dt.float32
BF16 = mybir.dt.bfloat16
AF = mybir.ActivationFunctionType
ALU = mybir.AluOpType
AX = mybir.AxisListType

T = 4096
D = 1024
NT = T // 128
NW = T // 512
NEG = -3.0e38
EM8 = float(np.exp(-8.0))

PHASES = ["l0a", "l0f0", "l0b", "l0f1", "l1a", "l1f0", "l1b", "l1f1"]
ENABLED = list(PHASES)

class Buf:
    __slots__ = ("h", "w", "rs", "dsem")
    def __init__(self, h):
        self.h = h
        self.w = None
        self.rs = []
        self.dsem = {}

SERIAL = False
NPOOL = 40

class Prog:
    ENGS = ("T", "V", "S", "Y", "G")
    def __init__(self, nc, stack):
        self.nc = nc
        self.q = {e: [] for e in self.ENGS}
        self.cnt = {e: 0 for e in self.ENGS}
        self.sem = {e: stack.enter_context(nc.semaphore("sem_" + e)) for e in self.ENGS}
        self.waited = {e: {} for e in self.ENGS}
        self.dpool = {q: [stack.enter_context(nc.semaphore(f"dp{q}{i}")) for i in range(NPOOL)]
                      for q in ("Y", "G")}
        self.dcnt = {q: [0] * NPOOL for q in ("Y", "G")}
        self.dnext = {"Y": 0, "G": 0}

    def _wait(self, e, tok):
        if tok[0] == "D":
            _, q, idx, v = tok
            key = ("D", q, idx)
            if self.waited[e].get(key, -1) >= v:
                return
            self.waited[e][key] = v
            sem = self.dpool[q][idx]
            self.q[e].append(lambda eng, sem=sem, v=v: eng.wait_ge(sem, v))
            return
        se, v = tok
        if self.waited[e].get(se, -1) >= v:
            return
        self.waited[e][se] = v
        sem = self.sem[se]
        self.q[e].append(lambda eng, sem=sem, v=v: eng.wait_ge(sem, v))

    def group(self, e):
        assert e in ("T", "V", "S", "G")
        return Group(self, e)

    def dma(self, e, dst, srcs, fns):
        assert e in ("Y", "G")
        if e not in dst.dsem:
            dst.dsem[e] = self.dnext[e]
            self.dnext[e] = (self.dnext[e] + 1) % NPOOL
        idx = dst.dsem[e]
        if dst.w is not None:
            self._wait(e, dst.w)
        for t in dst.rs:
            self._wait(e, t)
        for b in srcs:
            if b.w is not None:
                self._wait(e, b.w)
        if SERIAL:
            for se in ("T", "V", "S"):
                if self.cnt[se] > 0:
                    self._wait(e, (se, self.cnt[se]))
        sem = self.dpool[e][idx]
        if not isinstance(fns, (list, tuple)):
            fns = [fns]
        for f in fns:
            self.q[e].append(lambda eng, f=f, sem=sem: f(eng).then_inc(sem, 16))
            self.dcnt[e][idx] += 16
        tok = ("D", e, idx, self.dcnt[e][idx])
        dst.w = tok
        dst.rs = []
        for b in srcs:
            b.rs.append(tok)
        return tok

    def drain_dma(self):
        for e in ("Y", "G"):
            for q in ("Y", "G"):
                for idx in range(NPOOL):
                    if self.dcnt[q][idx] > 0:
                        self._wait(e, ("D", q, idx, self.dcnt[q][idx]))

    def barrier(self):
        for e in self.ENGS:
            for se in self.ENGS:
                if se not in ("Y", "G") and se != e and self.cnt[se] > 0:
                    self._wait(e, (se, self.cnt[se]))
            for q in ("Y", "G"):
                for idx in range(NPOOL):
                    if self.dcnt[q][idx] > 0:
                        self._wait(e, ("D", q, idx, self.dcnt[q][idx]))

    def flush(self, blk):
        for e, attr in (("T", "tensor"), ("V", "vector"), ("S", "scalar"),
                        ("Y", "sync"), ("G", "gpsimd")):
            ops = self.q[e]
            if not ops:
                continue
            def body(eng, ops=ops):
                for f in ops:
                    f(eng)
            getattr(blk, attr)(body)
            self.q[e] = []

class Group:
    def __init__(self, p, e):
        self.p = p
        self.e = e
        self.fns = []
        self.marks_r = []
        self.marks_w = []
        if SERIAL:
            for se in ("T", "V", "S", "G"):
                if se != e and p.cnt[se] > 0:
                    p._wait(e, (se, p.cnt[se]))
            for q in ("Y", "G"):
                for idx in range(NPOOL):
                    if p.dcnt[q][idx] > 0:
                        p._wait(e, ("D", q, idx, p.dcnt[q][idx]))
    def r(self, *bufs):
        for b in bufs:
            if b.w is not None:
                self.p._wait(self.e, b.w)
            self.marks_r.append(b)
        return self
    def w(self, *bufs):
        for b in bufs:
            if b.w is not None:
                self.p._wait(self.e, b.w)
            for t in b.rs:
                self.p._wait(self.e, t)
            self.marks_w.append(b)
        return self
    def add(self, fn):
        self.fns.append(fn)
        return self
    def fin(self):
        assert self.fns
        p, e = self.p, self.e
        sem = p.sem[e]
        last = self.fns[-1]
        for f in self.fns[:-1]:
            p.q[e].append(f)
        p.q[e].append(lambda eng, f=last, sem=sem: f(eng).then_inc(sem, 1))
        p.cnt[e] += 1
        tok = (e, p.cnt[e])
        for b in self.marks_w:
            b.w = tok
            b.rs = []
        for b in self.marks_r:
            b.rs.append(tok)
        return tok


_UID = [0]
def SB(nc, name, shape, dt):
    _UID[0] += 1
    return nc.sbuf_tensor(f"{name}_{_UID[0]}", shape, dt)

def PS(nc, name, shape, dt):
    _UID[0] += 1
    return nc.psum_tensor(f"{name}_{_UID[0]}", shape, dt)

def ap3(handle, off, dims):
    return bass.AP(handle, off, dims)

def g1(p, e, reads, writes, fn):
    g = p.group(e).r(*reads).w(*writes)
    g.add(fn)
    return g.fin()

def emit_ln_tile(p, ctx, xf, stats, xn_out):
    cmisc = ctx["cmisc"]
    g = p.group("V").r(xf).w(stats)
    g.add(lambda v: v.bn_stats(stats.h[:, 4:10], xf.h[:, 0:512]))
    g.add(lambda v: v.bn_stats(stats.h[:, 10:16], xf.h[:, 512:1024]))
    g.fin()
    g1(p, "V", [stats], [stats], lambda v: v.bn_aggr(stats.h[:, 0:2], stats.h[:, 4:16]))
    g1(p, "S", [stats, cmisc], [stats], lambda s: s.activation(
        stats.h[:, 2:3], stats.h[:, 1:2], AF.Sqrt, bias=cmisc.h[:, 0:1]))
    g1(p, "V", [stats], [stats], lambda v: v.reciprocal(stats.h[:, 3:4], stats.h[:, 2:3]))
    g1(p, "V", [xf, stats], [xn_out], lambda v: v.tensor_scalar(
        xn_out.h[:], xf.h[:], stats.h[:, 0:1], stats.h[:, 3:4], ALU.subtract, ALU.mult))

def transposes(p, ctx, src, psum_tr, dst, nch, width=128, src_off=0, dst_off=0,
               evac="S"):
    ident = ctx["ident"]
    g = p.group("T").r(src, ident).w(psum_tr)
    for c in range(nch):
        g.add(lambda t, c=c: t.transpose(
            psum_tr.h[0:width, c * 128:(c + 1) * 128],
            src.h[:, src_off + c * width:src_off + (c + 1) * width], ident.h[:]))
    g.fin()
    g = p.group(evac).r(psum_tr).w(dst)
    if evac == "S":
        g.add(lambda s: s.copy(dst.h[0:width, dst_off:dst_off + nch * 128],
                               psum_tr.h[0:width, 0:nch * 128]))
    else:
        g.add(lambda v: v.tensor_copy(dst.h[0:width, dst_off:dst_off + nch * 128],
                                      psum_tr.h[0:width, 0:nch * 128]))
    g.fin()

def mm_proj(p, xnT, w_sb, psum, nout, kchunks=8, wcol=None, kpart=128,
            xoff=0, woff=0, extra_r=()):
    if wcol is None:
        wcol = nout
    g = p.group("T").r(xnT, w_sb, *extra_r).w(psum)
    ns = [(o, min(512, nout - o)) for o in range(0, nout, 512)]
    for c in range(kchunks):
        kp = kpart if c < kchunks - 1 else kpart
        for (o, wd) in ns:
            g.add(lambda t, c=c, o=o, wd=wd, kp=kp: t.matmul(
                psum.h[:, o:o + wd],
                xnT.h[0:kp, xoff + c * 128:xoff + c * 128 + 128],
                w_sb.h[0:kp, woff + c * wcol + o:woff + c * wcol + o + wd],
                start=(c == 0), stop=(c == kchunks - 1)))
    return g.fin()

def load_w_bf16(nc, p, st2, ap, name, rows, cols, dst=None):
    E2 = st2.enter_context
    ch = (rows + 127) // 128
    if dst is None:
        dst = Buf(E2(SB(nc, name + "_bf", [128, ch * cols], BF16)))
    with contextlib.ExitStack() as stt:
        tmp = Buf(stt.enter_context(SB(nc, name + "_tmpf", [128, cols], F32)))
        for c in range(ch):
            rp = min(128, rows - c * 128)
            p.dma("Y", tmp, [], lambda e, c=c, rp=rp: e.dma_start(
                out=tmp.h[0:rp, :], in_=ap[name][c * 128:c * 128 + rp, :]))
            g1(p, "V", [tmp], [dst], lambda v, c=c, rp=rp: v.tensor_copy(
                dst.h[0:rp, c * cols:(c + 1) * cols], tmp.h[0:rp, :]))
        p.barrier()
    return dst

def conv_bf16(nc, p, st2, ap, name, rows, cols):
    E2 = st2.enter_context
    dst = Buf(E2(SB(nc, name + "_bf", [rows, cols], BF16)))
    tmp = Buf(E2(SB(nc, name + "_tf", [rows, cols], F32)))
    p.dma("Y", tmp, [], lambda e: e.dma_start(out=tmp.h[:], in_=ap[name]))
    g1(p, "V", [tmp], [dst], lambda v: v.tensor_copy(dst.h[:], tmp.h[:]))
    return dst


def phase_l0a(nc, p, st, ap, ctx, xin, xout):
  with contextlib.ExitStack() as stp:
    Ep = stp.enter_context
    kT = Buf(Ep(SB(nc, "l0a_kT", [128, 8 * T], BF16)))
    vb = Buf(Ep(SB(nc, "l0a_vb", [128, NT * 1032], BF16)))
    qTd = Buf(ap["qTd"])
    cmisc = ctx["cmisc"]
    g1(p, "V", [cmisc], [vb], lambda v: v.tensor_copy(
        ap3(vb.h, 1024, [[NT * 1032, 128], [1032, NT], [1, 8]]),
        ap3(cmisc.h, 3, [[16, 128], [0, NT], [0, 8]])))

    for proj in ("k", "q", "v"):
        _l0a_proj(nc, p, ap, ctx, xin, proj, kT, vb, qTd)

    _l0a_passB(nc, p, ap, ctx, xin, xout, kT, vb)


def _l0a_proj(nc, p, ap, ctx, xin, proj, kT, vb, qTd):
      cmisc = ctx["cmisc"]
      with contextlib.ExitStack() as st2:
        E2 = st2.enter_context
        wsb = load_w_bf16(nc, p, st2, ap, "l0a_w" + proj, D, D)
        xf = [Buf(E2(SB(nc, f"l0a_xf{i}", [128, D], F32))) for i in range(2)]
        stats = [Buf(E2(SB(nc, f"l0a_st{i}", [128, 16], F32))) for i in range(2)]
        xn = [Buf(E2(SB(nc, f"l0a_xn{i}", [128, D], BF16))) for i in range(2)]
        xnT = [Buf(E2(SB(nc, f"l0a_xnT{i}", [128, D], BF16))) for i in range(2)]
        osb = [Buf(E2(SB(nc, f"l0a_o{i}", [128, D], BF16))) for i in range(2)]
        ptr = [Buf(E2(PS(nc, f"l0a_ptr{i}", [128, D], BF16))) for i in range(2)]
        pmm = [Buf(E2(PS(nc, f"l0a_pmm{i}", [128, D], F32))) for i in range(2)]
        for t in range(NT):
            i = t % 2
            p.dma("Y", xf[i], [xin], lambda e, t=t, i=i: e.dma_start(
                out=xf[i].h[:], in_=xin.h[t * 128:(t + 1) * 128, :]))
            emit_ln_tile(p, ctx, xf[i], stats[i], xn[i])
            transposes(p, ctx, xn[i], ptr[i], xnT[i], 8)
            mm_proj(p, xnT[i], wsb, pmm[i], D)
            if proj == "v":
                g1(p, "S", [pmm[i]], [vb], lambda s, t=t, i=i: s.copy(
                    vb.h[:, t * 1032:t * 1032 + 1024], pmm[i].h[:]))
                continue
            g1(p, "S", [pmm[i]], [osb[i]], lambda s, i=i: s.copy(osb[i].h[:], pmm[i].h[:]))
            g = p.group("T").r(osb[i], ctx["ident"]).w(ptr[i])
            for c in range(8):
                g.add(lambda e, c=c, i=i: e.transpose(ptr[i].h[:, c * 128:(c + 1) * 128],
                                                 osb[i].h[:, c * 128:(c + 1) * 128],
                                                 ctx["ident"].h[:]))
            g.fin()
            if proj == "k":
                g1(p, "V", [ptr[i]], [kT], lambda v, t=t, i=i: v.tensor_copy(
                    ap3(kT.h, t * 128, [[8 * T, 128], [T, 8], [1, 128]]),
                    ap3(ptr[i].h, 0, [[1024, 128], [128, 8], [1, 128]])))
            else:
                g1(p, "V", [ptr[i]], [osb[i]], lambda v, i=i: v.tensor_copy(osb[i].h[:], ptr[i].h[:]))
                p.dma("G", qTd, [osb[i]], [
                    (lambda e, t=t, c=c, i=i: e.dma_start(
                        out=qTd.h[c * 128:(c + 1) * 128, t * 128:(t + 1) * 128],
                        in_=osb[i].h[:, c * 128:(c + 1) * 128]))
                    for c in range(8)])
        p.barrier()


def _l0a_passB(nc, p, ap, ctx, xin, xout, kT, vb):
    cmisc = ctx["cmisc"]
    with contextlib.ExitStack() as st2:
        E2 = st2.enter_context
        qTg = Buf(E2(SB(nc, "l0a_qTg", [128, 8 * 256], BF16)))
        psT = [Buf(E2(PS(nc, f"l0a_psT{i}", [128, 512], F32))) for i in range(2)]
        pnum = [Buf(E2(PS(nc, f"l0a_pnum{i}", [128, 1032], F32))) for i in range(2)]
        pT = [Buf(E2(SB(nc, f"l0a_pT{i}", [128, 256], BF16))) for i in range(2)]
        yf = Buf(E2(SB(nc, "l0a_yf", [128, D], F32)))
        rden = Buf(E2(SB(nc, "l0a_rd", [128, 2], F32)))
        xf2 = [Buf(E2(SB(nc, f"l0a_x2{i}", [128, D], F32))) for i in range(2)]
        maskF = ctx["maskF"]

        for qg in range(T // 256):
            p.dma("Y", qTg, [], [
                (lambda e, qg=qg, c=c: e.dma_start(
                    out=qTg.h[:, c * 256:(c + 1) * 256],
                    in_=ap["qTd"][c * 128:(c + 1) * 128, qg * 256:(qg + 1) * 256]))
                for c in range(8)])
            for qc in range(2):
                qt = 2 * qg + qc
                p.dma("Y", xf2[qc], [xin], lambda e, qt=qt, qc=qc: e.dma_start(
                    out=xf2[qc].h[:], in_=xin.h[qt * 128:(qt + 1) * 128, :]))
            nkt = 2 * qg + 2
            def emit_av(kt, qg=qg):
                par = kt % 2
                for qc in range(2):
                    qt = 2 * qg + qc
                    if kt > qt:
                        continue
                    g = p.group("T").r(pT[par], vb).w(pnum[qc])
                    for (o, wd) in ((0, 512), (512, 512), (1024, 8)):
                        g.add(lambda t, qc=qc, kt=kt, o=o, wd=wd, qt=qt, par=par: t.matmul(
                            pnum[qc].h[:, o:o + wd],
                            pT[par].h[:, qc * 128:qc * 128 + 128],
                            vb.h[:, kt * 1032 + o:kt * 1032 + o + wd],
                            start=(kt == 0), stop=(kt == qt)))
                    g.fin()
            for kt in range(nkt):
                par = kt % 2
                g = p.group("T").r(kT, qTg).w(psT[par])
                for c in range(8):
                    g.add(lambda t, c=c, kt=kt, par=par: t.matmul(
                        psT[par].h[:, 0:256],
                        kT.h[:, c * T + kt * 128:c * T + kt * 128 + 128],
                        qTg.h[:, c * 256:(c + 1) * 256],
                        start=(c == 0), stop=(c == 7)))
                g.fin()
                g1(p, "S", [psT[par], cmisc], [pT[par]], lambda s, par=par: s.activation(
                    pT[par].h[:], psT[par].h[:, 0:256], AF.Exp,
                    bias=cmisc.h[:, 1:2], scale=1.0 / 32.0))
                j = kt - 2 * qg
                if j >= 0:
                    g1(p, "V", [maskF, pT[par]], [pT[par]], lambda v, j=j, par=par: v.tensor_tensor(
                        pT[par].h[:], pT[par].h[:], maskF.h[:, j * 512:j * 512 + 256], ALU.mult))
                if kt > 0:
                    emit_av(kt - 1)
            emit_av(nkt - 1)
            for qc in range(2):
                qt = 2 * qg + qc
                g1(p, "V", [pnum[qc]], [rden], lambda v, qc=qc: v.reciprocal(
                    rden.h[:, qc:qc + 1], pnum[qc].h[:, 1024:1025]))
                g1(p, "V", [pnum[qc], rden, xf2[qc]], [yf],
                   lambda v, qc=qc: v.scalar_tensor_tensor(
                       yf.h[:], pnum[qc].h[:, 0:1024], rden.h[:, qc:qc + 1],
                       xf2[qc].h[:], ALU.mult, ALU.add))
                p.dma("G", xout, [yf], lambda e, qt=qt: e.dma_start(
                    out=xout.h[qt * 128:(qt + 1) * 128, :], in_=yf.h[:]))
        p.barrier()


def emit_shift_load(p, ybuf, xf, t, cmisc=None):
    p.dma("Y", xf, [ybuf], lambda e, t=t: e.dma_start(
        out=xf.h[:, 0:512], in_=ybuf.h[t * 128:(t + 1) * 128, 0:512]))
    if t == 0:
        p.dma("Y", xf, [ybuf], lambda e: e.dma_start(
            out=xf.h[1:128, 512:1024], in_=ybuf.h[0:127, 512:1024]))
        g1(p, "V", [cmisc], [xf], lambda v: v.tensor_copy(
            xf.h[0:1, 512:1024], ap3(cmisc.h, 2, [[16, 1], [0, 512]])))
    else:
        p.dma("Y", xf, [ybuf], lambda e, t=t: e.dma_start(
            out=xf.h[:, 512:1024], in_=ybuf.h[t * 128 - 1:t * 128 + 127, 512:1024]))


def l0f_inner_layer(nc, p, st, ap, ctx, li, yin, ymid, yout, xin, last):
    cmisc = ctx["cmisc"]
    with contextlib.ExitStack() as sa:
        Ea = sa.enter_context
        qTl = Buf(Ea(SB(nc, "f_qT", [128, 4 * T], BF16)))
        kTl = Buf(Ea(SB(nc, "f_kT", [128, 4 * T], BF16)))
        vsb = Buf(Ea(SB(nc, "f_v", [128, NT * 576], BF16)))
        osb = Buf(Ea(SB(nc, "f_o", [128, 4 * 512], BF16)))
        obuf = Buf(ap["obuf"])
        g1(p, "V", [cmisc], [vsb], lambda v: v.tensor_copy(
            ap3(vsb.h, 64, [[NT * 576, 128], [576, NT], [72, 8], [1, 8]]),
            ap3(cmisc.h, 3, [[16, 128], [0, NT], [0, 8], [0, 8]])))
        with contextlib.ExitStack() as s2:
            E2 = s2.enter_context
            wqkv = load_w_bf16(nc, p, s2, ap, f"l0f_wqkv{li}", D, 1536)
            qsr = Buf(E2(SB(nc, "f_qsr", [128, 512], F32)))
            ksr = Buf(E2(SB(nc, "f_ksr", [128, 512], F32)))
            p.dma("Y", qsr, [], lambda e: e.dma_start(out=qsr.h[:], in_=ap[f"l0f_qs{li}"]))
            p.dma("Y", ksr, [], lambda e: e.dma_start(out=ksr.h[:], in_=ap[f"l0f_ks{li}"]))
            xf = [Buf(E2(SB(nc, "f_xf0", [128, D], F32)))] * 2
            hbf = Buf(E2(SB(nc, "f_hbf", [128, D], BF16)))
            hT = Buf(E2(SB(nc, "f_hT", [128, D], BF16)))
            qkvf = Buf(E2(SB(nc, "f_qkvf", [128, 1536], F32)))
            sqt = Buf(E2(SB(nc, "f_sq", [128, 512], F32)))
            ns = Buf(E2(SB(nc, "f_ns", [128, 32], F32)))
            rot = Buf(E2(SB(nc, "f_rot", [128, 512], F32)))
            tq1 = Buf(E2(SB(nc, "f_tq1", [128, 512], F32)))
            tq2 = Buf(E2(SB(nc, "f_tq2", [128, 512], F32)))
            qhat = Buf(E2(SB(nc, "f_qhat", [128, 512], BF16)))
            khat = Buf(E2(SB(nc, "f_khat", [128, 512], BF16)))
            tabs = [Buf(E2(SB(nc, f"f_tab{i}", [128, 512], F32))) for i in range(4)]
            pqkv = Buf(E2(PS(nc, "f_pqkv", [128, 1536], F32)))
            ptr = Buf(E2(PS(nc, "f_ptr", [128, D], BF16)))
            for t in range(NT):
                i = t % 2
                emit_shift_load(p, yin, xf[i], t, cmisc)
                g1(p, "V", [xf[i]], [hbf], lambda v, i=i: v.tensor_copy(hbf.h[:], xf[i].h[:]))
                transposes(p, ctx, hbf, ptr, hT, 8)
                mm_proj(p, hT, wqkv, pqkv, 1536, wcol=1536)
                g1(p, "V", [pqkv], [qkvf], lambda v: v.tensor_copy(qkvf.h[:], pqkv.h[:]))
                g1(p, "S", [qkvf], [sqt], lambda s: s.activation(sqt.h[:], qkvf.h[:, 0:512], AF.Square))
                g = p.group("V").r(sqt).w(ns)
                for h in range(8):
                    g.add(lambda v, h=h: v.reduce_sum(ns.h[:, h:h + 1], sqt.h[:, h * 64:(h + 1) * 64], axis=AX.X))
                g.fin()
                g1(p, "S", [qkvf], [sqt], lambda s: s.activation(sqt.h[:], qkvf.h[:, 512:1024], AF.Square))
                g = p.group("V").r(sqt).w(ns)
                for h in range(8):
                    g.add(lambda v, h=h: v.reduce_sum(ns.h[:, 8 + h:9 + h], sqt.h[:, h * 64:(h + 1) * 64], axis=AX.X))
                g.fin()
                g1(p, "S", [ns, cmisc], [ns], lambda s: s.activation(
                    ns.h[:, 16:32], ns.h[:, 0:16], AF.Sqrt, bias=cmisc.h[:, 5:6]))
                g1(p, "V", [ns], [ns], lambda v: v.reciprocal(
                    ns.h[:, 16:32], ns.h[:, 16:32]))
                g = p.group("V").r(qkvf, ns).w(qkvf)
                for h in range(8):
                    g.add(lambda v, h=h: v.tensor_scalar(
                        qkvf.h[:, h * 64:(h + 1) * 64], qkvf.h[:, h * 64:(h + 1) * 64],
                        ns.h[:, 16 + h:17 + h], None, ALU.mult))
                for h in range(8):
                    g.add(lambda v, h=h: v.tensor_scalar(
                        qkvf.h[:, 512 + h * 64:512 + (h + 1) * 64],
                        qkvf.h[:, 512 + h * 64:512 + (h + 1) * 64],
                        ns.h[:, 24 + h:25 + h], None, ALU.mult))
                g.fin()
                g1(p, "V", [qkvf, qsr], [qkvf], lambda v: v.tensor_tensor(
                    qkvf.h[:, 0:512], qkvf.h[:, 0:512], qsr.h[:], ALU.mult))
                g1(p, "V", [qkvf, ksr], [qkvf], lambda v: v.tensor_tensor(
                    qkvf.h[:, 512:1024], qkvf.h[:, 512:1024], ksr.h[:], ALU.mult))
                for ti, nm in enumerate(("c_csq", "c_snq", "c_csk", "c_snk")):
                    p.dma("Y", tabs[ti], [], lambda e, t=t, ti=ti, nm=nm: e.dma_start(
                        out=tabs[ti].h[:], in_=ap[nm][t * 128:(t + 1) * 128, :]))
                for (qoff, cs, sn, dstb) in ((0, tabs[0], tabs[1], qhat),
                                             (512, tabs[2], tabs[3], khat)):
                    g = p.group("V").r(qkvf).w(rot)
                    g.add(lambda v, qoff=qoff: v.tensor_single_scalar(
                        ap3(rot.h, 0, [[512, 128], [64, 8], [1, 32]]),
                        ap3(qkvf.h, qoff + 32, [[1536, 128], [64, 8], [1, 32]]),
                        -1.0, ALU.mult))
                    g.add(lambda v, qoff=qoff: v.tensor_copy(
                        ap3(rot.h, 32, [[512, 128], [64, 8], [1, 32]]),
                        ap3(qkvf.h, qoff, [[1536, 128], [64, 8], [1, 32]])))
                    g.fin()
                    g1(p, "V", [qkvf, cs], [tq1], lambda v, qoff=qoff, cs=cs: v.tensor_tensor(
                        tq1.h[:], qkvf.h[:, qoff:qoff + 512], cs.h[:], ALU.mult))
                    g1(p, "V", [rot, sn], [tq2], lambda v, sn=sn: v.tensor_tensor(
                        tq2.h[:], rot.h[:], sn.h[:], ALU.mult))
                    g1(p, "V", [tq1, tq2], [dstb], lambda v, dstb=dstb: v.tensor_tensor(
                        dstb.h[:], tq1.h[:], tq2.h[:], ALU.add))
                for (src, dstT) in ((qhat, qTl), (khat, kTl)):
                    g = p.group("T").r(src, ctx["ident"]).w(ptr)
                    for c in range(4):
                        g.add(lambda e, c=c, src=src: e.transpose(
                            ptr.h[:, c * 128:(c + 1) * 128],
                            src.h[:, c * 128:(c + 1) * 128], ctx["ident"].h[:]))
                    g.fin()
                    g1(p, "V", [ptr], [dstT], lambda v, t=t, dstT=dstT: v.tensor_copy(
                        ap3(dstT.h, t * 128, [[4 * T, 128], [T, 4], [1, 128]]),
                        ap3(ptr.h, 0, [[1024, 128], [128, 4], [1, 128]])))
                g1(p, "V", [qkvf], [vsb], lambda v, t=t: v.tensor_copy(
                    ap3(vsb.h, t * 576, [[NT * 576, 128], [72, 8], [1, 64]]),
                    ap3(qkvf.h, 1024, [[1536, 128], [64, 8], [1, 64]])))
            p.barrier()
        with contextlib.ExitStack() as s2:
            E2 = s2.enter_context
            maskW = ctx["maskW"]
            psT = [Buf(E2(PS(nc, f"f_psT{i}", [128, 512], F32))) for i in range(2)]
            pacc = [Buf(E2(PS(nc, f"f_pacc{i}", [128, 128], F32))) for i in range(4)]
            Pb = [Buf(E2(SB(nc, f"f_Pb{i}", [128, 512], BF16))) for i in range(2)]
            stmp = [Buf(E2(SB(nc, f"f_stmp{i}", [128, 512], F32))) for i in range(2)]
            rd = Buf(E2(SB(nc, "f_rd", [128, 4], F32)))
            for w in range(NW):
                for h in range(8):
                    kts = list(range(8)) if w > 0 else list(range(4, 8))
                    def emit_avf(kt, h=h, w=w, kts=kts):
                        par = kt % 2
                        tk = 4 * (w - 1) + kt
                        for qc in range(4):
                            g1(p, "T", [Pb[par], vsb], [pacc[qc]],
                               lambda t, qc=qc, tk=tk, h=h, kt=kt, kts=kts, par=par: t.matmul(
                                   pacc[qc].h[:, 0:72],
                                   Pb[par].h[:, qc * 128:qc * 128 + 128],
                                   vsb.h[:, tk * 576 + h * 72:tk * 576 + h * 72 + 72],
                                   start=(kt == kts[0]), stop=(kt == kts[-1])))
                    for kt in kts:
                        par = kt % 2
                        tk = 4 * (w - 1) + kt
                        g1(p, "T", [kTl, qTl], [psT[par]], lambda t, h=h, tk=tk, w=w, par=par: t.matmul(
                            psT[par].h[:],
                            kTl.h[(h % 2) * 64:(h % 2) * 64 + 64,
                                  (h // 2) * T + tk * 128:(h // 2) * T + tk * 128 + 128],
                            qTl.h[(h % 2) * 64:(h % 2) * 64 + 64,
                                  (h // 2) * T + w * 512:(h // 2) * T + w * 512 + 512],
                            start=True, stop=True))
                        g1(p, "V", [psT[par], maskW], [stmp[par]], lambda v, kt=kt, par=par: v.tensor_tensor(
                            stmp[par].h[:], psT[par].h[:], maskW.h[:, kt * 512:(kt + 1) * 512], ALU.add))
                        g1(p, "S", [stmp[par], cmisc], [Pb[par]], lambda s, par=par: s.activation(
                            Pb[par].h[:], stmp[par].h[:], AF.Exp, bias=cmisc.h[:, 1:2], scale=8.0))
                        if kt > kts[0]:
                            emit_avf(kt - 1)
                    emit_avf(kts[-1])
                    for qc in range(4):
                        g1(p, "V", [pacc[qc]], [rd], lambda v, qc=qc: v.reciprocal(
                            rd.h[:, qc:qc + 1], pacc[qc].h[:, 64:65]))
                        g1(p, "V", [pacc[qc], rd], [osb], lambda v, qc=qc, h=h: v.tensor_scalar(
                            osb.h[:, qc * 512 + h * 64:qc * 512 + h * 64 + 64],
                            pacc[qc].h[:, 0:64], rd.h[:, qc:qc + 1], None, ALU.mult))
                p.dma("G", obuf, [osb], [
                    (lambda e, w=w, qc=qc: e.dma_start(
                        out=obuf.h[(w * 4 + qc) * 128:(w * 4 + qc + 1) * 128, :],
                        in_=osb.h[:, qc * 512:(qc + 1) * 512]))
                    for qc in range(4)])
            p.barrier()
        with contextlib.ExitStack() as s2:
            E2 = s2.enter_context
            wo = load_w_bf16(nc, p, s2, ap, f"l0f_wo{li}", 512, D)
            olb = Buf(E2(SB(nc, "f_olb", [128, 512], BF16)))
            oT = Buf(E2(SB(nc, "f_oT", [128, 512], BF16)))
            ptrW = Buf(E2(PS(nc, "f_ptr2", [128, 512], BF16)))
            pwo = Buf(E2(PS(nc, "f_pwo", [128, D], F32)))
            wyf = [Buf(E2(SB(nc, f"f_yf{i}", [128, D], F32))) for i in range(2)]
            wyn = Buf(E2(SB(nc, "f_yn", [128, D], F32)))
            for t in range(NT):
                i = t % 2
                p.dma("Y", wyf[i], [yin], lambda e, t=t, i=i: e.dma_start(
                    out=wyf[i].h[:], in_=yin.h[t * 128:(t + 1) * 128, :]))
                p.dma("Y", olb, [obuf], lambda e, t=t: e.dma_start(
                    out=olb.h[:], in_=obuf.h[t * 128:(t + 1) * 128, :]))
                g = p.group("T").r(olb, ctx["ident"]).w(ptrW)
                for c in range(4):
                    g.add(lambda e, c=c: e.transpose(
                        ptrW.h[:, c * 128:(c + 1) * 128],
                        olb.h[:, c * 128:(c + 1) * 128],
                        ctx["ident"].h[:]))
                g.fin()
                g1(p, "V", [ptrW], [oT], lambda v: v.tensor_copy(oT.h[:], ptrW.h[:, 0:512]))
                mm_proj(p, oT, wo, pwo, D, kchunks=4)
                g1(p, "V", [pwo, wyf[i]], [wyn], lambda v, i=i: v.tensor_tensor(
                    wyn.h[:], pwo.h[:], wyf[i].h[:], ALU.add))
                p.dma("G", ymid, [wyn], lambda e, t=t: e.dma_start(
                    out=ymid.h[t * 128:(t + 1) * 128, :], in_=wyn.h[:]))
            p.barrier()
    pieces = [(o, min(512, 2730 - o)) for o in range(0, 2730, 512)]
    with contextlib.ExitStack() as s2:
        E2 = s2.enter_context
        w1 = load_w_bf16(nc, p, s2, ap, f"l0f_w1{li}", D, 5460)
        lgr = Buf(E2(SB(nc, "f_lgr", [128, D], F32)))
        lbr = Buf(E2(SB(nc, "f_lbr", [128, D], F32)))
        b1r = Buf(E2(SB(nc, "f_b1r", [128, 5460], F32)))
        p.dma("Y", lgr, [], lambda e: e.dma_start(out=lgr.h[:], in_=ap[f"l0f_lg{li}"]))
        p.dma("Y", lbr, [], lambda e: e.dma_start(out=lbr.h[:], in_=ap[f"l0f_lb{li}"]))
        p.dma("Y", b1r, [], lambda e: e.dma_start(out=b1r.h[:], in_=ap[f"l0f_b1{li}"]))
        mxf = [Buf(E2(SB(nc, f"f_mx{i}", [128, D], F32))) for i in range(2)]
        mstats = Buf(E2(SB(nc, "f_mst", [128, 16], F32)))
        mnf = Buf(E2(SB(nc, "f_nf", [128, D], F32)))
        n2 = Buf(E2(SB(nc, "f_n2", [128, D], BF16)))
        n2T = Buf(E2(SB(nc, "f_n2T", [128, D], BF16)))
        aa = Buf(E2(SB(nc, "f_aa", [128, 512], F32)))
        gg = Buf(E2(SB(nc, "f_gg", [128, 512], F32)))
        hm = Buf(E2(SB(nc, "f_hm", [128, 2730], BF16)))
        hmTs = Buf(E2(SB(nc, "f_hmTs", [128, 2816], BF16)))
        ptrM = Buf(E2(PS(nc, "f_ptr3", [128, D], BF16)))
        pA = Buf(E2(PS(nc, "f_pA", [128, 512], F32)))
        pG = Buf(E2(PS(nc, "f_pG", [128, 512], F32)))
        hmT = Buf(ap["hmT"])
        for t in range(NT):
            i = t % 2
            emit_shift_load(p, ymid, mxf[i], t, ctx['cmisc'])
            emit_ln_tile(p, ctx, mxf[i], mstats, mnf)
            g1(p, "V", [mnf, lgr], [mnf], lambda v: v.tensor_tensor(mnf.h[:], mnf.h[:], lgr.h[:], ALU.mult))
            g1(p, "V", [mnf, lbr], [n2], lambda v: v.tensor_tensor(n2.h[:], mnf.h[:], lbr.h[:], ALU.add))
            transposes(p, ctx, n2, ptrM, n2T, 8)
            for (off, wd) in pieces:
                mm_proj(p, n2T, w1, pA, wd, wcol=5460, woff=off)
                mm_proj(p, n2T, w1, pG, wd, wcol=5460, woff=2730 + off)
                g1(p, "V", [pA, b1r], [aa], lambda v, off=off, wd=wd: v.tensor_tensor(
                    aa.h[:, 0:wd], pA.h[:, 0:wd], b1r.h[:, off:off + wd], ALU.add))
                g1(p, "V", [pG, b1r], [gg], lambda v, off=off, wd=wd: v.tensor_tensor(
                    gg.h[:, 0:wd], pG.h[:, 0:wd], b1r.h[:, 2730 + off:2730 + off + wd], ALU.add))
                g1(p, "S", [gg], [gg], lambda s, wd=wd: s.activation(
                    gg.h[:, 0:wd], gg.h[:, 0:wd], AF.Gelu))
                g1(p, "V", [aa, gg], [hm], lambda v, off=off, wd=wd: v.tensor_tensor(
                    hm.h[:, off:off + wd], aa.h[:, 0:wd], gg.h[:, 0:wd], ALU.mult))
            for cb in range(3):
                c0 = cb * 8
                nch = min(8, 22 - c0)
                g = p.group("T").r(hm, ctx["ident"]).w(ptrM)
                for c in range(c0, c0 + nch):
                    wd = min(128, 2730 - c * 128)
                    g.add(lambda e, c=c, wd=wd, c0=c0: e.transpose(
                        ptrM.h[0:wd, (c - c0) * 128:(c - c0) * 128 + 128],
                        hm.h[:, c * 128:c * 128 + wd], ctx["ident"].h[:]))
                g.fin()
                nfull = nch - 1 if c0 + nch == 22 else nch
                if nfull > 0:
                    g1(p, "V", [ptrM], [hmTs], lambda v, c0=c0, nfull=nfull: v.tensor_copy(
                        hmTs.h[:, c0 * 128:(c0 + nfull) * 128], ptrM.h[:, 0:nfull * 128]))
                if nfull < nch:
                    g1(p, "V", [ptrM], [hmTs], lambda v, c0=c0, nch=nch: v.tensor_copy(
                        hmTs.h[0:42, (c0 + nch - 1) * 128:(c0 + nch) * 128],
                        ptrM.h[0:42, (nch - 1) * 128:nch * 128]))
            p.dma("G", hmT, [hmTs], [
                (lambda e, t=t, c=c, rp=min(128, 2730 - c * 128): e.dma_start(
                    out=hmT.h[c * 128:c * 128 + rp, t * 128:(t + 1) * 128],
                    in_=hmTs.h[0:rp, c * 128:(c + 1) * 128]))
                for c in range(22)])
        p.barrier()
    with contextlib.ExitStack() as s2:
        E2 = s2.enter_context
        w2 = load_w_bf16(nc, p, s2, ap, f"l0f_w2{li}", 2730, D)
        b2r = Buf(E2(SB(nc, "f_b2r", [128, D], F32)))
        p.dma("Y", b2r, [], lambda e: e.dma_start(out=b2r.h[:], in_=ap[f"l0f_b2{li}"]))
        hmTl = Buf(E2(SB(nc, "f_hmTl", [128, 2816], BF16)))
        yf2 = [Buf(E2(SB(nc, f"f_wyf{i}", [128, D], F32))) for i in range(2)]
        xr = Buf(E2(SB(nc, "f_wxr", [128, D], F32)))
        yn2 = Buf(E2(SB(nc, "f_wyn", [128, D], F32)))
        pm = Buf(E2(PS(nc, "f_pm", [128, D], F32)))
        hmT2 = Buf(ap["hmT"])
        for t in range(NT):
            i = t % 2
            p.dma("Y", hmTl, [hmT2], [
                (lambda e, t=t, c=c, rp=min(128, 2730 - c * 128): e.dma_start(
                    out=hmTl.h[0:rp, c * 128:(c + 1) * 128],
                    in_=hmT2.h[c * 128:c * 128 + rp, t * 128:(t + 1) * 128]))
                for c in range(22)])
            p.dma("Y", yf2[i], [ymid], lambda e, t=t, i=i: e.dma_start(
                out=yf2[i].h[:], in_=ymid.h[t * 128:(t + 1) * 128, :]))
            g = p.group("T").r(hmTl, w2).w(pm)
            for c in range(22):
                rp = min(128, 2730 - c * 128)
                for o in (0, 512):
                    g.add(lambda t_, c=c, o=o, rp=rp: t_.matmul(
                        pm.h[:, o:o + 512],
                        hmTl.h[0:rp, c * 128:(c + 1) * 128],
                        w2.h[0:rp, c * D + o:c * D + o + 512],
                        start=(c == 0), stop=(c == 21)))
            g.fin()
            g1(p, "V", [pm, b2r], [yn2], lambda v: v.tensor_tensor(
                yn2.h[:], pm.h[:], b2r.h[:], ALU.add))
            g1(p, "V", [yn2, yf2[i]], [yn2], lambda v, i=i: v.tensor_tensor(
                yn2.h[:], yn2.h[:], yf2[i].h[:], ALU.add))
            if last:
                p.dma("Y", xr, [xin], lambda e, t=t: e.dma_start(
                    out=xr.h[:], in_=xin.h[t * 128:(t + 1) * 128, :]))
                g1(p, "V", [yn2, xr], [yn2], lambda v: v.tensor_tensor(
                    yn2.h[:], yn2.h[:], xr.h[:], ALU.add))
            p.dma("G", yout, [yn2], lambda e, t=t: e.dma_start(
                out=yout.h[t * 128:(t + 1) * 128, :], in_=yn2.h[:]))
        p.barrier()


def phase_l0f(nc, p, st, ap, ctx, xin, xout):
    ybA = Buf(ap["ybA"]); ybB = Buf(ap["ybB"])
    with contextlib.ExitStack() as s2:
        E2 = s2.enter_context
        xf = [Buf(E2(SB(nc, f"f_lx{i}", [128, D], F32))) for i in range(2)]
        stats = Buf(E2(SB(nc, "f_lst", [128, 16], F32)))
        nf = Buf(E2(SB(nc, "f_lnf", [128, D], F32)))
        for t in range(NT):
            i = t % 2
            p.dma("Y", xf[i], [xin], lambda e, t=t, i=i: e.dma_start(
                out=xf[i].h[:], in_=xin.h[t * 128:(t + 1) * 128, :]))
            emit_ln_tile(p, ctx, xf[i], stats, nf)
            p.dma("G", ybA, [nf], lambda e, t=t: e.dma_start(
                out=ybA.h[t * 128:(t + 1) * 128, :], in_=nf.h[:]))
        p.barrier()
    l0f_inner_layer(nc, p, st, ap, ctx, 0, ybA, ybB, ybA, xin, last=False)
    l0f_inner_layer(nc, p, st, ap, ctx, 1, ybA, ybB, xout, xin, last=True)


def phase_l1a(nc, p, st, ap, ctx, xin, xout):
    cmisc = ctx["cmisc"]
    maskF = ctx["maskF"]
    with contextlib.ExitStack() as s2:
        E2 = s2.enter_context
        wq = load_w_bf16(nc, p, s2, ap, "l1a_wqT", D, D)
        wk = load_w_bf16(nc, p, s2, ap, "l1a_wkT", D, D)
        wv = load_w_bf16(nc, p, s2, ap, "l1a_wvT", D, D)
        wo = load_w_bf16(nc, p, s2, ap, "l1a_woT", D, D)
        brs = {}
        for nm in ("bq", "bk", "bv", "bo"):
            brs[nm] = Buf(E2(SB(nc, "a_" + nm, [128, D], F32)))
            p.dma("Y", brs[nm], [], lambda e, nm=nm: e.dma_start(
                out=brs[nm].h[:], in_=ap["l1a_" + nm]))
        Sf = Buf(E2(SB(nc, "a_Sf", [128, 16 * 72], F32)))
        Sz = Buf(E2(SB(nc, "a_Sz", [128, 16 * 72], BF16)))
        g1(p, "V", [cmisc], [Sf], lambda v: v.tensor_copy(
            Sf.h[:], ap3(cmisc.h, 2, [[16, 128], [0, 16 * 72]])))
        g1(p, "V", [cmisc], [Sz], lambda v: v.tensor_copy(
            Sz.h[:], ap3(cmisc.h, 2, [[16, 128], [0, 16 * 72]])))
        xf = [Buf(E2(SB(nc, f"a_xf{i}", [128, D], F32))) for i in range(2)]
        stats = Buf(E2(SB(nc, "a_st", [128, 16], F32)))
        xn = Buf(E2(SB(nc, "a_xn", [128, D], BF16)))
        xnT = Buf(E2(SB(nc, "a_xnT", [128, D], BF16)))
        tmpf = Buf(E2(SB(nc, "a_tmpf", [128, D], F32)))
        qe = Buf(E2(SB(nc, "a_qe", [128, D], BF16)))
        ke = Buf(E2(SB(nc, "a_ke", [128, D], BF16)))
        qeT = Buf(E2(SB(nc, "a_qeT", [128, D], BF16)))
        keT = Buf(E2(SB(nc, "a_keT", [128, D], BF16)))
        vsb = Buf(E2(SB(nc, "a_vsb", [128, 1152], BF16)))
        Abf = Buf(E2(SB(nc, "a_Abf", [128, 128], BF16)))
        ysb = Buf(E2(SB(nc, "a_ysb", [128, D], F32)))
        ybf = Buf(E2(SB(nc, "a_ybf", [128, D], BF16)))
        rd = Buf(E2(SB(nc, "a_rd", [128, 1], F32)))
        ptr = Buf(E2(PS(nc, "a_ptr", [128, D], BF16)))
        pmm = Buf(E2(PS(nc, "a_pmm", [128, D], F32)))
        pA = Buf(E2(PS(nc, "a_pA", [128, 128], F32)))
        pO = Buf(E2(PS(nc, "a_pO", [128, 128], F32)))
        pS = Buf(E2(PS(nc, "a_pS", [128, 128], F32)))
        for t in range(NT):
            i = t % 2
            p.dma("Y", xf[i], [xin], lambda e, t=t, i=i: e.dma_start(
                out=xf[i].h[:], in_=xin.h[t * 128:(t + 1) * 128, :]))
            emit_ln_tile(p, ctx, xf[i], stats, xn)
            transposes(p, ctx, xn, ptr, xnT, 8)
            for (wsb, br, dst) in ((wq, "bq", qe), (wk, "bk", ke), (wv, "bv", None)):
                mm_proj(p, xnT, wsb, pmm, D)
                g1(p, "V", [pmm, brs[br]], [tmpf], lambda v, br=br: v.tensor_tensor(
                    tmpf.h[:], pmm.h[:], brs[br].h[:], ALU.add))
                if dst is None:
                    g1(p, "V", [tmpf], [vsb], lambda v: v.tensor_copy(
                        ap3(vsb.h, 0, [[1152, 128], [72, 16], [1, 64]]),
                        ap3(tmpf.h, 0, [[1024, 128], [64, 16], [1, 64]])))
                    g1(p, "V", [cmisc], [vsb], lambda v: v.tensor_copy(
                        ap3(vsb.h, 64, [[1152, 128], [72, 16], [1, 8]]),
                        ap3(cmisc.h, 3, [[16, 128], [0, 16], [0, 8]])))
                else:
                    g1(p, "S", [tmpf], [dst], lambda s, dst=dst: s.activation(
                        dst.h[:], tmpf.h[:], AF.Exp))
            for (src, dT) in ((qe, qeT), (ke, keT)):
                g = p.group("T").r(src, ctx["ident"]).w(ptr)
                for c in range(8):
                    g.add(lambda e, c=c, src=src: e.transpose(
                        ptr.h[:, c * 128:(c + 1) * 128],
                        src.h[:, c * 128:(c + 1) * 128], ctx["ident"].h[:]))
                g.fin()
                g1(p, "V", [ptr], [dT], lambda v, dT=dT: v.tensor_copy(dT.h[:], ptr.h[:]))
            for h in range(16):
                po = (h % 2) * 64
                ch = h // 2
                g1(p, "T", [keT, qeT], [pA], lambda t_, po=po, ch=ch: t_.matmul(
                    pA.h[:], keT.h[po:po + 64, ch * 128:(ch + 1) * 128],
                    qeT.h[po:po + 64, ch * 128:(ch + 1) * 128], start=True, stop=True))
                g1(p, "V", [pA, maskF], [Abf], lambda v: v.tensor_tensor(
                    Abf.h[:], pA.h[:], maskF.h[:, 0:128], ALU.mult))
                g = p.group("T").r(Abf, vsb, qeT, Sz).w(pO)
                g.add(lambda t_, h=h: t_.matmul(
                    pO.h[:, 0:72], Abf.h[:], vsb.h[:, h * 72:(h + 1) * 72],
                    start=True, stop=False))
                g.add(lambda t_, h=h, ch=ch: t_.matmul(
                    pO.h[:, 0:72], qeT.h[:, ch * 128:(ch + 1) * 128],
                    Sz.h[:, h * 72:(h + 1) * 72], start=False, stop=True))
                g.fin()
                g1(p, "T", [ke, vsb], [pS], lambda t_, h=h, po=po: t_.matmul(
                    pS.h[po:po + 64, 0:72], ke.h[:, h * 64:(h + 1) * 64],
                    vsb.h[:, h * 72:(h + 1) * 72], start=True, stop=True))
                g1(p, "V", [pS, Sf], [Sf], lambda v, h=h, po=po: v.tensor_tensor(
                    Sf.h[po:po + 64, h * 72:(h + 1) * 72],
                    Sf.h[po:po + 64, h * 72:(h + 1) * 72],
                    pS.h[po:po + 64, 0:72], ALU.add))
                g1(p, "V", [Sf], [Sz], lambda v, h=h, po=po: v.tensor_copy(
                    Sz.h[po:po + 64, h * 72:(h + 1) * 72],
                    Sf.h[po:po + 64, h * 72:(h + 1) * 72]))
                g1(p, "V", [pO], [rd], lambda v: v.reciprocal(rd.h[:], pO.h[:, 64:65]))
                g1(p, "V", [pO, rd], [ysb], lambda v, h=h: v.tensor_scalar(
                    ysb.h[:, h * 64:(h + 1) * 64], pO.h[:, 0:64], rd.h[:], None, ALU.mult))
            g1(p, "V", [ysb], [ybf], lambda v: v.tensor_copy(ybf.h[:], ysb.h[:]))
            transposes(p, ctx, ybf, ptr, xnT, 8)
            mm_proj(p, xnT, wo, pmm, D)
            g1(p, "V", [pmm, brs["bo"]], [tmpf], lambda v: v.tensor_tensor(
                tmpf.h[:], pmm.h[:], brs["bo"].h[:], ALU.add))
            g1(p, "V", [tmpf, xf[i]], [ysb], lambda v, i=i: v.tensor_tensor(
                ysb.h[:], tmpf.h[:], xf[i].h[:], ALU.add))
            p.dma("G", xout, [ysb], lambda e, t=t: e.dma_start(
                out=xout.h[t * 128:(t + 1) * 128, :], in_=ysb.h[:]))
        p.barrier()


def phase_l0b(nc, p, st, ap, ctx, xin, xout):
  with contextlib.ExitStack() as stp:
    Ep = stp.enter_context
    kT = Buf(Ep(SB(nc, "c_kT", [128, 8 * T], BF16)))
    vb = Buf(Ep(SB(nc, "c_vb", [128, NT * 1024], BF16)))
    kbosT = Buf(Ep(SB(nc, "c_kbosT", [128, 8], BF16)))
    vbosrep = Buf(Ep(SB(nc, "c_vbosrep", [128, D], BF16)))
    xnb = Buf(ap["ybA"])
    qTd = Buf(ap["qTd"])

    with contextlib.ExitStack() as s2:
        E2 = s2.enter_context
        xf = [Buf(E2(SB(nc, "c_lx0", [128, D], F32)))] * 2
        stats = Buf(E2(SB(nc, "c_lst", [128, 16], F32)))
        nf = Buf(E2(SB(nc, "c_lnf", [128, D], F32)))
        for t in range(NT):
            i = t % 2
            p.dma("Y", xf[i], [xin], lambda e, t=t, i=i: e.dma_start(
                out=xf[i].h[:], in_=xin.h[t * 128:(t + 1) * 128, :]))
            emit_ln_tile(p, ctx, xf[i], stats, nf)
            p.dma("G", xnb, [nf], lambda e, t=t: e.dma_start(
                out=xnb.h[t * 128:(t + 1) * 128, :], in_=nf.h[:]))
        p.barrier()

    for proj in ("k", "q", "v"):
        _l0b_proj(nc, p, ap, ctx, proj, xnb, kT, vb, kbosT, vbosrep, qTd)
    _l0b_attn(nc, p, ap, ctx, kT, vb, kbosT, vbosrep, qTd)
    _l0b_out(nc, p, ap, ctx, xin, xout, vbosrep)


def _l0b_proj(nc, p, ap, ctx, proj, xnb, kT, vb, kbosT, vbosrep, qTd):
    cmisc = ctx["cmisc"]
    obuf2 = Buf(ap["obuf2"])
    with contextlib.ExitStack() as st2:
        E2 = st2.enter_context
        wsb = load_w_bf16(nc, p, st2, ap, "l0b_w" + proj, D, D)
        xff = [Buf(E2(SB(nc, "c_xff0", [128, D], F32)))] * 2
        xf = [Buf(E2(SB(nc, "c_xf0", [128, D], BF16)))] * 2
        osb = Buf(E2(SB(nc, "c_osb", [128, D], BF16)))
        ptr = Buf(E2(PS(nc, "c_ptr", [128, D], BF16)))
        pmm = Buf(E2(PS(nc, "c_pmm", [128, D], F32)))
        if proj in ("k", "v"):
            p.dma("Y", xff[0], [xnb], lambda e: e.dma_start(
                out=xff[0].h[:], in_=xnb.h[0:128, :]))
            g1(p, "V", [xff[0]], [xf[0]], lambda v: v.tensor_copy(xf[0].h[:], xff[0].h[:]))
            transposes(p, ctx, xf[0], ptr, osb, 8)
            mm_proj(p, osb, wsb, pmm, D)
            if proj == "v":
                g1(p, "S", [pmm], [osb], lambda s: s.copy(osb.h[:], pmm.h[:]))
                vbd = Buf(ap["vbos_d"])
                p.dma("G", vbd, [osb], lambda e: e.dma_start(
                    out=vbd.h[:], in_=osb.h[0:1, :]))
                p.dma("Y", vbosrep, [vbd], lambda e: e.dma_start(
                    out=vbosrep.h[:], in_=ap3(vbd.h.tensor, 0, [[0, 128], [1, D]])))
            else:
                g1(p, "S", [pmm], [osb], lambda s: s.copy(osb.h[:], pmm.h[:]))
                g = p.group("T").r(osb, ctx["ident"]).w(ptr)
                for c in range(8):
                    g.add(lambda e, c=c: e.transpose(
                        ptr.h[:, c * 128:(c + 1) * 128],
                        osb.h[:, c * 128:(c + 1) * 128], ctx["ident"].h[:]))
                g.fin()
                g1(p, "V", [ptr], [kbosT], lambda v: v.tensor_copy(
                    ap3(kbosT.h, 0, [[8, 128], [1, 8]]),
                    ap3(ptr.h, 0, [[1024, 128], [128, 8]])))
        for t in range(NT):
            i = t % 2
            if t < NT - 1:
                p.dma("Y", xff[i], [xnb], lambda e, t=t, i=i: e.dma_start(
                    out=xff[i].h[:], in_=xnb.h[t * 128 + 1:t * 128 + 129, :]))
            else:
                p.dma("Y", xff[i], [xnb], lambda e, t=t, i=i: e.dma_start(
                    out=xff[i].h[0:127, :], in_=xnb.h[t * 128 + 1:t * 128 + 128, :]))
                p.dma("Y", xff[i], [], lambda e, i=i: e.dma_start(
                    out=xff[i].h[127:128, :], in_=ap["c_zrow"]))
            g1(p, "V", [xff[i]], [xf[i]], lambda v, i=i: v.tensor_copy(xf[i].h[:], xff[i].h[:]))
            transposes(p, ctx, xf[i], ptr, osb, 8)
            mm_proj(p, osb, wsb, pmm, D)
            if proj == "v":
                g1(p, "S", [pmm], [vb], lambda s, t=t: s.copy(
                    vb.h[:, t * 1024:(t + 1) * 1024], pmm.h[:]))
                continue
            g1(p, "S", [pmm], [osb], lambda s: s.copy(osb.h[:], pmm.h[:]))
            g = p.group("T").r(osb, ctx["ident"]).w(ptr)
            for c in range(8):
                g.add(lambda e, c=c: e.transpose(ptr.h[:, c * 128:(c + 1) * 128],
                                                 osb.h[:, c * 128:(c + 1) * 128],
                                                 ctx["ident"].h[:]))
            g.fin()
            if proj == "k":
                g1(p, "V", [ptr], [kT], lambda v, t=t: v.tensor_copy(
                    ap3(kT.h, t * 128, [[8 * T, 128], [T, 8], [1, 128]]),
                    ap3(ptr.h, 0, [[1024, 128], [128, 8], [1, 128]])))
            else:
                g1(p, "V", [ptr], [osb], lambda v: v.tensor_copy(osb.h[:], ptr.h[:]))
                p.dma("G", qTd, [osb], [
                    (lambda e, t=t, c=c: e.dma_start(
                        out=qTd.h[c * 128:(c + 1) * 128, t * 128:(t + 1) * 128],
                        in_=osb.h[:, c * 128:(c + 1) * 128]))
                    for c in range(8)])
        p.barrier()


def _l0b_attn(nc, p, ap, ctx, kT, vb, kbosT, vbosrep, qTd):
    cmisc = ctx["cmisc"]
    obuf2 = Buf(ap["obuf2"])
    with contextlib.ExitStack() as st2:
        E2 = st2.enter_context
        wthr = Buf(E2(SB(nc, "c_wthr", [128, 256], F32)))
        p.dma("Y", wthr, [], lambda e: e.dma_start(out=wthr.h[:], in_=ap["l0b_wthr"]))
        mks = Buf(E2(SB(nc, "c_mks", [128, 2 * 392], F32)))
        p.dma("Y", mks, [], lambda e: e.dma_start(out=mks.h[:], in_=ap["l0b_masks"]))
        qTl = Buf(E2(SB(nc, "c_qTl", [128, 8 * 128], BF16)))
        Pall = Buf(E2(SB(nc, "c_Pall", [128, 16 * 392], BF16)))
        PgE = {e: Buf(E2(SB(nc, "c_Pg" + e, [128, 392], BF16))) for e in "VG"}
        PgcE = {e: Buf(E2(SB(nc, "c_Pgc" + e, [128, 1], F32))) for e in "VG"}
        maccE = {e: [Buf(E2(SB(nc, f"c_macc{e}{i}", [128, 392], BF16))) for i in range(2)]
                 for e in "VG"}
        ysbE = {e: Buf(E2(SB(nc, "c_ysb" + e, [128, D], F32))) for e in "VG"}
        smf = Buf(E2(SB(nc, "c_smf", [128, 392], F32)))
        den = Buf(E2(SB(nc, "c_den", [128, 2], F32)))
        PTE = {e: Buf(E2(SB(nc, "c_PT" + e, [128, 384], BF16))) for e in "VG"}
        obf = Buf(E2(SB(nc, "c_obf", [128, D], BF16)))
        psc = Buf(E2(PS(nc, "c_psc", [128, 512], F32)))
        ptr2 = Buf(E2(PS(nc, "c_ptr2", [128, 512], BF16)))
        pacc = [Buf(E2(PS(nc, f"c_pacc{i}", [128, 128], F32))) for i in range(2)]
        p.dma("G", obuf2, [vbosrep], lambda e: e.dma_start(
            out=obuf2.h[0:1, :], in_=vbosrep.h[0:1, :]))
        for t in range(NT):
            wlo = max(0, 128 * (t - 1))
            whi = min(T, 128 * (t + 2))
            width = whi - wlo
            mko = 0 if t == 0 else 392
            nch = width // 128
            p.dma("Y", qTl, [], [
                (lambda e, t=t, c=c: e.dma_start(
                    out=qTl.h[:, c * 128:(c + 1) * 128],
                    in_=ap["qTd"][c * 128:(c + 1) * 128, t * 128:(t + 1) * 128]))
                for c in range(8)])
            for h in range(16):
                po = (h % 2) * 64
                ch = h // 2
                g = p.group("T").r(qTl, kbosT, kT).w(psc)
                g.add(lambda t_, po=po, ch=ch: t_.matmul(
                    psc.h[:, 0:1], qTl.h[po:po + 64, ch * 128:(ch + 1) * 128],
                    kbosT.h[po:po + 64, ch:ch + 1], start=True, stop=True))
                g.add(lambda t_, po=po, ch=ch, wlo=wlo, width=width: t_.matmul(
                    psc.h[:, 1:1 + width],
                    qTl.h[po:po + 64, ch * 128:(ch + 1) * 128],
                    kT.h[po:po + 64, ch * T + wlo:ch * T + wlo + width],
                    start=True, stop=True))
                g.fin()
                g1(p, "V", [psc, mks], [smf], lambda v, mko=mko, width=width: v.tensor_tensor(
                    smf.h[:, 0:width + 1], psc.h[:, 0:width + 1],
                    mks.h[:, mko:mko + width + 1], ALU.add))
                g1(p, "S", [smf, cmisc], [Pall, den], lambda s, h=h, width=width: s.activation(
                    Pall.h[:, h * 392:h * 392 + width + 1], smf.h[:, 0:width + 1],
                    AF.Exp, bias=cmisc.h[:, 4:5], scale=0.125,
                    accum_out=den.h[:, 0:1]))
                g1(p, "V", [den], [den], lambda v: v.reciprocal(den.h[:, 1:2], den.h[:, 0:1]))
                g1(p, "V", [Pall, den], [Pall], lambda v, h=h, width=width: v.tensor_scalar(
                    Pall.h[:, h * 392:h * 392 + width + 1],
                    Pall.h[:, h * 392:h * 392 + width + 1],
                    den.h[:, 1:2], None, ALU.mult))
            for g_ in range(16):
                eng = "V"
                par2 = "V" if g_ % 2 == 0 else "G"
                macc = maccE[par2]; Pg = PgE[par2]; Pgc = PgcE[par2]; ysbe = ysbE[par2]
                pac = pacc[g_ % 2]
                PTe = PTE[par2]
                g1(p, eng, [Pall, wthr], [macc[0]], lambda v, g_=g_, width=width, macc=macc: v.tensor_scalar(
                    macc[0].h[:, 0:width + 1], Pall.h[:, 0:width + 1],
                    wthr.h[:, g_ * 16:g_ * 16 + 1], None, ALU.mult))
                for h in range(1, 16):
                    g1(p, eng, [Pall, wthr, macc[(h - 1) % 2]], [macc[h % 2]],
                       lambda v, g_=g_, h=h, width=width, macc=macc: v.scalar_tensor_tensor(
                           macc[h % 2].h[:, 0:width + 1],
                           Pall.h[:, h * 392:h * 392 + width + 1],
                           wthr.h[:, g_ * 16 + h:g_ * 16 + h + 1],
                           macc[(h - 1) % 2].h[:, 0:width + 1], ALU.mult, ALU.add))
                g1(p, eng, [macc[1]], [Pg], lambda v, width=width, macc=macc, Pg=Pg: v.tensor_copy(
                    Pg.h[:, 0:width + 1], macc[1].h[:, 0:width + 1]))
                g1(p, eng, [macc[1]], [Pgc], lambda v, macc=macc, Pgc=Pgc: v.tensor_copy(
                    Pgc.h[:], macc[1].h[:, 0:1]))
                g = p.group("T").r(Pg, ctx["ident"]).w(ptr2)
                for c in range(nch):
                    g.add(lambda e, c=c, Pg=Pg: e.transpose(
                        ptr2.h[:, c * 128:(c + 1) * 128],
                        Pg.h[:, 1 + c * 128:1 + (c + 1) * 128], ctx["ident"].h[:]))
                g.fin()
                g1(p, "S", [ptr2], [PTe], lambda ss, nch=nch, PTe=PTe: ss.copy(
                    PTe.h[:, 0:nch * 128], ptr2.h[:, 0:nch * 128]))
                g = p.group("T").r(PTe, vb).w(pac)
                t0 = wlo // 128
                for c in range(nch):
                    g.add(lambda t_, c=c, t0=t0, g_=g_, nch=nch, PTe=PTe, pac=pac: t_.matmul(
                        pac.h[:, 0:64], PTe.h[:, c * 128:(c + 1) * 128],
                        vb.h[:, (t0 + c) * 1024 + g_ * 64:(t0 + c) * 1024 + g_ * 64 + 64],
                        start=(c == 0), stop=(c == nch - 1)))
                g.fin()
                g1(p, "V", [vbosrep, Pgc, pac], [ysbe],
                   lambda v, g_=g_, Pgc=Pgc, pac=pac, ysbe=ysbe: v.scalar_tensor_tensor(
                       ysbe.h[:, g_ * 64:(g_ + 1) * 64], vbosrep.h[:, g_ * 64:(g_ + 1) * 64],
                       Pgc.h[:], pac.h[:, 0:64], ALU.mult, ALU.add))
            g1(p, "V", [ysbE["V"]], [obf], lambda v: v.tensor_copy(
                ap3(obf.h, 0, [[1024, 128], [128, 8], [1, 64]]),
                ap3(ysbE["V"].h, 0, [[1024, 128], [128, 8], [1, 64]])))
            g1(p, "V", [ysbE["G"]], [obf], lambda v: v.tensor_copy(
                ap3(obf.h, 64, [[1024, 128], [128, 8], [1, 64]]),
                ap3(ysbE["G"].h, 64, [[1024, 128], [128, 8], [1, 64]])))
            if t < NT - 1:
                p.dma("G", obuf2, [obf], lambda e, t=t: e.dma_start(
                    out=obuf2.h[t * 128 + 1:t * 128 + 129, :], in_=obf.h[:]))
            else:
                p.dma("G", obuf2, [obf], lambda e, t=t: e.dma_start(
                    out=obuf2.h[t * 128 + 1:t * 128 + 128, :], in_=obf.h[0:127, :]))
        p.barrier()


def _l0b_out(nc, p, ap, ctx, xin, xout, vbosrep):
    obuf2 = Buf(ap["obuf2"])
    with contextlib.ExitStack() as st2:
        E2 = st2.enter_context
        wsb = load_w_bf16(nc, p, st2, ap, "l0b_wout", D, D)
        br = Buf(E2(SB(nc, "c_obr", [128, D], F32)))
        p.dma("Y", br, [], lambda e: e.dma_start(out=br.h[:], in_=ap["l0b_bout"]))
        ob = [Buf(E2(SB(nc, "c_oob", [128, D], BF16)))] * 2
        xr = Buf(E2(SB(nc, "c_oxr", [128, D], F32)))
        oT = Buf(E2(SB(nc, "c_ooT", [128, D], BF16)))
        yn = Buf(E2(SB(nc, "c_oyn", [128, D], F32)))
        ptr = Buf(E2(PS(nc, "c_optr", [128, D], BF16)))
        pmm = Buf(E2(PS(nc, "c_opmm", [128, D], F32)))
        for t in range(NT):
            i = t % 2
            p.dma("Y", ob[i], [obuf2], lambda e, t=t, i=i: e.dma_start(
                out=ob[i].h[:], in_=obuf2.h[t * 128:(t + 1) * 128, :]))
            p.dma("Y", xr, [xin], lambda e, t=t: e.dma_start(
                out=xr.h[:], in_=xin.h[t * 128:(t + 1) * 128, :]))
            transposes(p, ctx, ob[i], ptr, oT, 8)
            mm_proj(p, oT, wsb, pmm, D)
            g1(p, "V", [pmm, br], [yn], lambda v: v.tensor_tensor(
                yn.h[:], pmm.h[:], br.h[:], ALU.add))
            g1(p, "V", [yn, xr], [yn], lambda v: v.tensor_tensor(
                yn.h[:], yn.h[:], xr.h[:], ALU.add))
            p.dma("G", xout, [yn], lambda e, t=t: e.dma_start(
                out=xout.h[t * 128:(t + 1) * 128, :], in_=yn.h[:]))
        p.barrier()


def phase_l1b(nc, p, st, ap, ctx, xin, xout):
  with contextlib.ExitStack() as stp:
    Ep = stp.enter_context
    kT = Buf(Ep(SB(nc, "b_kT", [128, 8 * T], BF16)))
    vb = Buf(Ep(SB(nc, "b_vb", [128, NT * 1152], BF16)))
    Vs = Buf(Ep(SB(nc, "b_Vs", [NT, 1152], BF16)))
    cmisc = ctx["cmisc"]
    qTd = Buf(ap["qTd"])
    g1(p, "V", [cmisc], [vb], lambda v: v.tensor_copy(
        ap3(vb.h, 64, [[NT * 1152, 128], [1152, NT], [72, 16], [1, 8]]),
        ap3(cmisc.h, 3, [[16, 128], [0, NT], [0, 16], [0, 8]])))

    for proj in ("k", "q", "v"):
        _l1b_proj(nc, p, ap, ctx, xin, proj, kT, vb, Vs, qTd)
    _l1b_attn(nc, p, ap, ctx, xin, kT, vb, Vs)
    _l1b_out(nc, p, ap, ctx, xin, xout)


def _l1b_proj(nc, p, ap, ctx, xin, proj, kT, vb, Vs, qTd):
    cmisc = ctx["cmisc"]
    with contextlib.ExitStack() as st2:
        E2 = st2.enter_context
        wsb = load_w_bf16(nc, p, st2, ap, "l1b_w" + proj, D, D)
        br = Buf(E2(SB(nc, "b_br", [128, D], F32)))
        p.dma("Y", br, [], lambda e: e.dma_start(out=br.h[:], in_=ap["l1b_b" + proj]))
        onec = Buf(E2(SB(nc, "b_onec", [128, 8], BF16)))
        g1(p, "V", [cmisc], [onec], lambda v: v.tensor_copy(
            onec.h[:], ap3(cmisc.h, 3, [[16, 128], [0, 8]])))
        xf = [Buf(E2(SB(nc, f"b_xf{i}", [128, D], F32))) for i in range(2)]
        stats = [Buf(E2(SB(nc, f"b_st{i}", [128, 16], F32))) for i in range(2)]
        xn = [Buf(E2(SB(nc, f"b_xn{i}", [128, D], BF16))) for i in range(2)]
        xnT = [Buf(E2(SB(nc, f"b_xnT{i}", [128, D], BF16))) for i in range(2)]
        tmpf = [Buf(E2(SB(nc, "b_tmpf0", [128, D], F32)))] * 2
        osb = [Buf(E2(SB(nc, "b_osb0", [128, D], BF16)))] * 2
        vrow = Buf(E2(SB(nc, "b_vrow", [1, 1152], BF16)))
        ptr = [Buf(E2(PS(nc, f"b_ptr{i}", [128, D], BF16))) for i in range(2)]
        pmm = [Buf(E2(PS(nc, f"b_pmm{i}", [128, D], F32))) for i in range(2)]
        pvs = Buf(E2(PS(nc, "b_pvs", [1, 576], F32)))
        for t in range(NT):
            i = t % 2
            p.dma("Y", xf[i], [xin], lambda e, t=t, i=i: e.dma_start(
                out=xf[i].h[:], in_=xin.h[t * 128:(t + 1) * 128, :]))
            emit_ln_tile(p, ctx, xf[i], stats[i], xn[i])
            transposes(p, ctx, xn[i], ptr[i], xnT[i], 8)
            mm_proj(p, xnT[i], wsb, pmm[i], D)
            g1(p, "V", [pmm[i], br], [tmpf[i]], lambda v, i=i: v.tensor_tensor(
                tmpf[i].h[:], pmm[i].h[:], br.h[:], ALU.add))
            if proj == "v":
                g1(p, "V", [tmpf[i]], [vb], lambda v, t=t, i=i: v.tensor_copy(
                    ap3(vb.h, t * 1152, [[NT * 1152, 128], [72, 16], [1, 64]]),
                    ap3(tmpf[i].h, 0, [[1024, 128], [64, 16], [1, 64]])))
                for hb in (0, 576):
                    g = p.group("T").r(vb, onec).w(pvs)
                    for (o, wd) in ((0, 512), (512, 64)):
                        g.add(lambda t_, t=t, o=o, wd=wd, hb=hb: t_.matmul(
                            pvs.h[:, o:o + wd], onec.h[:, 0:1],
                            vb.h[:, t * 1152 + hb + o:t * 1152 + hb + o + wd],
                            start=True, stop=True))
                    g.fin()
                    g1(p, "S", [pvs], [vrow], lambda s, hb=hb: s.copy(
                        vrow.h[:, hb:hb + 576], pvs.h[0:1, 0:576]))
                p.dma("G", Vs, [vrow], lambda e, t=t: e.dma_start(
                    out=Vs.h[t:t + 1, :], in_=vrow.h[:]))
                continue
            g1(p, "V", [tmpf[i]], [osb[i]], lambda v, i=i: v.tensor_copy(osb[i].h[:], tmpf[i].h[:]))
            g = p.group("T").r(osb[i], ctx["ident"]).w(ptr[i])
            for c in range(8):
                g.add(lambda e, c=c, i=i: e.transpose(ptr[i].h[:, c * 128:(c + 1) * 128],
                                                 osb[i].h[:, c * 128:(c + 1) * 128],
                                                 ctx["ident"].h[:]))
            g.fin()
            if proj == "k":
                g1(p, "V", [ptr[i]], [kT], lambda v, t=t, i=i: v.tensor_copy(
                    ap3(kT.h, t * 128, [[8 * T, 128], [T, 8], [1, 128]]),
                    ap3(ptr[i].h, 0, [[1024, 128], [128, 8], [1, 128]])))
            else:
                g1(p, "V", [ptr[i]], [osb[i]], lambda v, i=i: v.tensor_copy(osb[i].h[:], ptr[i].h[:]))
                p.dma("G", qTd, [osb[i]], [
                    (lambda e, t=t, c=c, i=i: e.dma_start(
                        out=qTd.h[c * 128:(c + 1) * 128, t * 128:(t + 1) * 128],
                        in_=osb[i].h[:, c * 128:(c + 1) * 128]))
                    for c in range(8)])
        p.barrier()


def _l1b_attn(nc, p, ap, ctx, xin, kT, vb, Vs):
    cmisc = ctx["cmisc"]
    maskF = ctx["maskF"]
    utc = ctx["utc"]
    u32 = ctx["u32"]
    ydr = Buf(ap["ybB"])
    with contextlib.ExitStack() as st2:
        E2 = st2.enter_context
        qTg = Buf(E2(SB(nc, "b_qTg", [128, 8 * 512], BF16)))
        futv = Buf(E2(SB(nc, "b_futv", [128, 4 * 1152], F32)))
        pT = [Buf(E2(SB(nc, f"b_pT{i}", [128, 512], BF16))) for i in range(2)]
        nf = Buf(E2(SB(nc, "b_nf", [128, 64], F32)))
        dn = Buf(E2(SB(nc, "b_dn", [128, 2], F32)))
        ysb = [Buf(E2(SB(nc, f"b_ysb{i}", [128, D], F32))) for i in range(4)]
        psT = [Buf(E2(PS(nc, f"b_psT{i}", [128, 512], F32))) for i in range(2)]
        pacc = [Buf(E2(PS(nc, f"b_pacc{i}", [128, 128], F32))) for i in range(4)]
        pfut = Buf(E2(PS(nc, "b_pfut", [128, 576], F32)))
        for qg in range(T // 512):
            p.dma("Y", qTg, [], [
                (lambda e, qg=qg, c=c: e.dma_start(
                    out=qTg.h[:, c * 512:(c + 1) * 512],
                    in_=ap["qTd"][c * 128:(c + 1) * 128, qg * 512:(qg + 1) * 512]))
                for c in range(8)])
            for qc in range(4):
                qt = 4 * qg + qc
                for half in range(2):
                    hb = half * 576
                    g = p.group("T").r(utc, vb, u32, Vs).w(pfut)
                    for (o, wd) in ((0, 512), (512, 64)):
                        g.add(lambda t_, qt=qt, o=o, wd=wd, hb=hb: t_.matmul(
                            pfut.h[:, o:o + wd], utc.h[:],
                            vb.h[:, qt * 1152 + hb + o:qt * 1152 + hb + o + wd],
                            start=True, stop=False))
                        g.add(lambda t_, qt=qt, o=o, wd=wd, hb=hb: t_.matmul(
                            pfut.h[:, o:o + wd],
                            ap3(u32.h, qt, [[NT, NT], [0, 128]]),
                            Vs.h[:, hb + o:hb + o + wd],
                            start=False, stop=True))
                    g.fin()
                    g1(p, "S", [pfut], [futv], lambda s, qc=qc, hb=hb: s.activation(
                        futv.h[:, qc * 1152 + hb:qc * 1152 + hb + 576], pfut.h[:, 0:576],
                        AF.Copy, scale=EM8))
            for h in range(16):
                po = (h % 2) * 64
                ch = h // 2
                def emit_avb(kt, h=h, qg=qg):
                    par = kt % 2
                    j = kt - 4 * qg
                    for qc in range(max(0, j), 4):
                        qt = 4 * qg + qc
                        g1(p, "T", [pT[par], vb], [pacc[qc]],
                           lambda t_, qc=qc, kt=kt, h=h, qt=qt, par=par: t_.matmul(
                               pacc[qc].h[:, 0:72],
                               pT[par].h[:, qc * 128:qc * 128 + 128],
                               vb.h[:, kt * 1152 + h * 72:kt * 1152 + h * 72 + 72],
                               start=(kt == 0), stop=(kt == qt)))
                for kt in range(4 * qg + 4):
                    par = kt % 2
                    g1(p, "T", [kT, qTg], [psT[par]], lambda t_, po=po, ch=ch, kt=kt, par=par: t_.matmul(
                        psT[par].h[:],
                        kT.h[po:po + 64, ch * T + kt * 128:ch * T + kt * 128 + 128],
                        qTg.h[po:po + 64, ch * 512:(ch + 1) * 512],
                        start=True, stop=True))
                    g1(p, "S", [psT[par], cmisc], [pT[par]], lambda s, par=par: s.activation(
                        pT[par].h[:], psT[par].h[:], AF.Exp, bias=cmisc.h[:, 4:5], scale=0.125))
                    j = kt - 4 * qg
                    if j >= 0:
                        g1(p, "V", [maskF, pT[par]], [pT[par]], lambda v, j=j, par=par: v.tensor_tensor(
                            pT[par].h[:], pT[par].h[:], maskF.h[:, j * 512:(j + 1) * 512], ALU.mult))
                    if kt > 0:
                        emit_avb(kt - 1)
                emit_avb(4 * qg + 3)
                for qc in range(4):
                    g1(p, "V", [pacc[qc], futv], [dn], lambda v, qc=qc, h=h: v.tensor_tensor(
                        dn.h[:, 0:1], pacc[qc].h[:, 64:65],
                        futv.h[:, qc * 1152 + h * 72 + 64:qc * 1152 + h * 72 + 65], ALU.add))
                    g1(p, "V", [dn], [dn], lambda v: v.reciprocal(dn.h[:, 1:2], dn.h[:, 0:1]))
                    g1(p, "V", [pacc[qc], futv], [nf], lambda v, qc=qc, h=h: v.tensor_tensor(
                        nf.h[:], pacc[qc].h[:, 0:64],
                        futv.h[:, qc * 1152 + h * 72:qc * 1152 + h * 72 + 64], ALU.add))
                    g1(p, "V", [nf, dn], [ysb[qc]], lambda v, qc=qc, h=h: v.tensor_scalar(
                        ysb[qc].h[:, h * 64:(h + 1) * 64], nf.h[:], dn.h[:, 1:2],
                        None, ALU.mult))
            for qc in range(4):
                qt = 4 * qg + qc
                p.dma("G", ydr, [ysb[qc]], lambda e, qt=qt, qc=qc: e.dma_start(
                    out=ydr.h[qt * 128:(qt + 1) * 128, :], in_=ysb[qc].h[:]))
        p.barrier()


def _l1b_out(nc, p, ap, ctx, xin, xout):
    ydr = Buf(ap["ybB"])
    with contextlib.ExitStack() as st2:
        E2 = st2.enter_context
        wsb = load_w_bf16(nc, p, st2, ap, "l1b_wout", D, D)
        br = Buf(E2(SB(nc, "b_bro", [128, D], F32)))
        p.dma("Y", br, [], lambda e: e.dma_start(out=br.h[:], in_=ap["l1b_bout"]))
        yf = [Buf(E2(SB(nc, "b_oyf", [128, D], F32)))] * 2
        xr = Buf(E2(SB(nc, "b_oxr", [128, D], F32)))
        ybf = Buf(E2(SB(nc, "b_oybf", [128, D], BF16)))
        yT = Buf(E2(SB(nc, "b_oyT", [128, D], BF16)))
        yn = Buf(E2(SB(nc, "b_oyn", [128, D], F32)))
        ptr = Buf(E2(PS(nc, "b_optr", [128, D], BF16)))
        pmm = Buf(E2(PS(nc, "b_opmm", [128, D], F32)))
        for t in range(NT):
            i = t % 2
            p.dma("Y", yf[i], [ydr], lambda e, t=t, i=i: e.dma_start(
                out=yf[i].h[:], in_=ydr.h[t * 128:(t + 1) * 128, :]))
            p.dma("Y", xr, [xin], lambda e, t=t: e.dma_start(
                out=xr.h[:], in_=xin.h[t * 128:(t + 1) * 128, :]))
            g1(p, "V", [yf[i]], [ybf], lambda v, i=i: v.tensor_copy(ybf.h[:], yf[i].h[:]))
            transposes(p, ctx, ybf, ptr, yT, 8)
            mm_proj(p, yT, wsb, pmm, D)
            g1(p, "V", [pmm, br], [yn], lambda v: v.tensor_tensor(
                yn.h[:], pmm.h[:], br.h[:], ALU.add))
            g1(p, "V", [yn, xr], [yn], lambda v: v.tensor_tensor(
                yn.h[:], yn.h[:], xr.h[:], ALU.add))
            p.dma("G", xout, [yn], lambda e, t=t: e.dma_start(
                out=xout.h[t * 128:(t + 1) * 128, :], in_=yn.h[:]))
        p.barrier()


def phase_l1f(nc, p, st, ap, ctx, xin, xout):
    _l1f_half(nc, p, st, ap, ctx, xin, Buf(ap["ybB"]), range(0, 4), first=True)
    _l1f_half(nc, p, st, ap, ctx, xin, xout, range(4, 8), first=False, partial=Buf(ap["ybB"]))


def _l1f_half(nc, p, st, ap, ctx, xin, xout, erange, first, partial=None):
    cmisc = ctx["cmisc"]
    gsave = Buf(ap["l1f_gsave"])
    with contextlib.ExitStack() as s2:
        E2 = s2.enter_context
        ew = [load_w_bf16(nc, p, s2, ap, f"l1f_ew{e}", D, D) for e in erange]
        if first:
            gwr = Buf(E2(SB(nc, "m_gwr", [128, 8192], F32)))
            p.dma("Y", gwr, [], lambda e: e.dma_start(out=gwr.h[:], in_=ap["l1f_gwr"]))
            gbr = Buf(E2(SB(nc, "m_gbr", [128, 8], F32)))
            p.dma("Y", gbr, [], lambda e: e.dma_start(out=gbr.h[:], in_=ap["l1f_gb"]))
        ebr = Buf(E2(SB(nc, "m_ebr", [128, 4096], F32)))
        p.dma("Y", ebr, [], lambda e, e0=erange[0]: e.dma_start(
            out=ebr.h[:], in_=ap["l1f_eb"][:, e0 * D:(e0 + 4) * D]))
        xf = [Buf(E2(SB(nc, f"m_xf{i}", [128, D], F32))) for i in range(2)]
        pf = Buf(E2(SB(nc, "m_pf", [128, D], F32)))
        stats = Buf(E2(SB(nc, "m_st", [128, 16], F32)))
        xnf = Buf(E2(SB(nc, "m_xnf", [128, D], F32)))
        gt = Buf(E2(SB(nc, "m_gt", [128, D], F32)))
        xn = Buf(E2(SB(nc, "m_xn", [128, D], BF16)))
        xnT = Buf(E2(SB(nc, "m_xnT", [128, D], BF16)))
        gf = Buf(E2(SB(nc, "m_gf", [128, 32], F32)))
        acc = [Buf(E2(SB(nc, f"m_acc{i}", [128, D], F32))) for i in range(2)]
        tmp = Buf(E2(SB(nc, "m_tmp", [128, D], F32)))
        ptr = Buf(E2(PS(nc, "m_ptr", [128, D], BF16)))
        pe = [Buf(E2(PS(nc, f"m_pe{i}", [128, D], F32))) for i in range(2)]
        for t in range(NT):
            i = t % 2
            p.dma("Y", xf[i], [xin], lambda e, t=t, i=i: e.dma_start(
                out=xf[i].h[:], in_=xin.h[t * 128:(t + 1) * 128, :]))
            emit_ln_tile(p, ctx, xf[i], stats, xnf)
            g1(p, "V", [xnf], [xn], lambda v: v.tensor_copy(xn.h[:], xnf.h[:]))
            transposes(p, ctx, xn, ptr, xnT, 8)
            if first:
                for e in range(8):
                    g1(p, "V", [xnf, gwr], [gt], lambda v, e=e: v.tensor_tensor(
                        gt.h[:], xnf.h[:], gwr.h[:, e * D:(e + 1) * D], ALU.mult))
                    g1(p, "V", [gt], [gf], lambda v, e=e: v.reduce_sum(
                        gf.h[:, 24 + e:25 + e], gt.h[:], axis=AX.X))
                g1(p, "V", [gf, gbr], [gf], lambda v: v.tensor_tensor(
                    gf.h[:, 0:8], gf.h[:, 24:32], gbr.h[:], ALU.add))
                g1(p, "V", [gf], [gf], lambda v: v.reduce_max(gf.h[:, 16:17], gf.h[:, 0:8], axis=AX.X))
                g1(p, "V", [gf], [gf], lambda v: v.tensor_single_scalar(
                    gf.h[:, 17:18], gf.h[:, 16:17], -1.0, ALU.mult))
                g1(p, "S", [gf], [gf], lambda s: s.activation(
                    gf.h[:, 0:8], gf.h[:, 0:8], AF.Exp, bias=gf.h[:, 17:18],
                    accum_out=gf.h[:, 18:19]))
                g1(p, "V", [gf], [gf], lambda v: v.reciprocal(gf.h[:, 19:20], gf.h[:, 18:19]))
                g1(p, "V", [gf], [gf], lambda v: v.tensor_scalar(
                    gf.h[:, 0:8], gf.h[:, 0:8], gf.h[:, 19:20], None, ALU.mult))
                g1(p, "V", [gf], [gf], lambda v: v.tensor_single_scalar(
                    gf.h[:, 8:16], gf.h[:, 0:8], 0.1, ALU.is_ge))
                g1(p, "V", [gf], [gf], lambda v: v.tensor_tensor(
                    gf.h[:, 8:16], gf.h[:, 8:16], gf.h[:, 0:8], ALU.mult))
                g1(p, "V", [gf], [gf], lambda v: v.reduce_sum(gf.h[:, 20:21], gf.h[:, 8:16], axis=AX.X))
                g1(p, "V", [gf], [gf], lambda v: v.tensor_single_scalar(
                    gf.h[:, 21:22], gf.h[:, 20:21], 0.0, ALU.is_equal))
                g1(p, "V", [gf], [gf], lambda v: v.tensor_tensor(
                    gf.h[:, 20:21], gf.h[:, 20:21], gf.h[:, 21:22], ALU.add))
                g1(p, "V", [gf], [gf], lambda v: v.reciprocal(gf.h[:, 22:23], gf.h[:, 20:21]))
                g1(p, "V", [gf], [gf], lambda v: v.tensor_scalar(
                    gf.h[:, 8:16], gf.h[:, 8:16], gf.h[:, 22:23], None, ALU.mult))
                p.dma("G", gsave, [gf], lambda e, t=t: e.dma_start(
                    out=gsave.h[t * 128:(t + 1) * 128, :], in_=gf.h[:, 8:16]))
            else:
                p.dma("Y", gf, [gsave], lambda e, t=t: e.dma_start(
                    out=gf.h[:, 8:16], in_=gsave.h[t * 128:(t + 1) * 128, :]))
                p.dma("Y", pf, [partial], lambda e, t=t: e.dma_start(
                    out=pf.h[:], in_=partial.h[t * 128:(t + 1) * 128, :]))
            for j, e in enumerate(erange):
                jj = j % 2
                mm_proj(p, xnT, ew[j], pe[jj], D)
                g1(p, "V", [pe[jj], ebr], [tmp], lambda v, jj=jj, j=j: v.tensor_tensor(
                    tmp.h[:], pe[jj].h[:], ebr.h[:, j * D:(j + 1) * D], ALU.add))
                if j == 0:
                    base = xf[i] if first else pf
                    g1(p, "V", [tmp, gf, base], [acc[0]],
                       lambda v, base=base, e=e: v.scalar_tensor_tensor(
                           acc[0].h[:], tmp.h[:], gf.h[:, 8 + e:9 + e], base.h[:],
                           ALU.mult, ALU.add))
                else:
                    g1(p, "V", [tmp, gf, acc[(j - 1) % 2]], [acc[j % 2]],
                       lambda v, j=j, e=e: v.scalar_tensor_tensor(
                           acc[j % 2].h[:], tmp.h[:], gf.h[:, 8 + e:9 + e],
                           acc[(j - 1) % 2].h[:], ALU.mult, ALU.add))
            p.dma("G", xout, [acc[1]], lambda e_, t=t: e_.dma_start(
                out=xout.h[t * 128:(t + 1) * 128, :], in_=acc[1].h[:]))
        p.barrier()


def finalize_copy(nc, p, st, src, dst):
    with contextlib.ExitStack() as st2:
        tb = [Buf(st2.enter_context(SB(nc, f"fc{i}", [128, D], F32))) for i in range(2)]
        for t in range(NT):
            b = tb[t % 2]
            p.dma("Y", b, [src], lambda e, t=t, b=b: e.dma_start(
                out=b.h[:], in_=src.h[t * 128:(t + 1) * 128, :]))
            p.dma("G", dst, [b], lambda e, t=t, b=b: e.dma_start(
                out=dst.h[t * 128:(t + 1) * 128, :], in_=b.h[:]))
        p.barrier()


def build_nc():
    nc = bass.Bass("TRN2", target_bir_lowering=False)
    ap = {}
    def inp(name, shape, dt=F32):
        ap[name] = nc.dram_tensor(name, list(shape), dt, kind="ExternalInput").ap()
    def scr(name, shape, dt=F32):
        ap[name] = nc.dram_tensor(name, list(shape), dt).ap()

    inp("x", (T, D))
    inp("l0a_wq", (D, D)); inp("l0a_wk", (D, D)); inp("l0a_wv", (D, D))
    for li in range(2):
        inp(f"l0f_wqkv{li}", (D, 1536)); inp(f"l0f_wo{li}", (512, D))
        inp(f"l0f_w1{li}", (D, 5460)); inp(f"l0f_w2{li}", (2730, D))
        inp(f"l0f_qs{li}", (128, 512)); inp(f"l0f_ks{li}", (128, 512))
        inp(f"l0f_lg{li}", (128, D)); inp(f"l0f_lb{li}", (128, D))
        inp(f"l0f_b1{li}", (128, 5460)); inp(f"l0f_b2{li}", (128, D))
    for nm in ("wk", "wq", "wv", "wout"):
        inp("l0b_" + nm, (D, D))
    inp("l0b_bout", (128, D)); inp("l0b_wthr", (128, 256)); inp("l0b_masks", (128, 784))
    for nm in ("wk", "wq", "wv", "wout"):
        inp("l1b_" + nm, (D, D))
    for nm in ("bk", "bq", "bv", "bout"):
        inp("l1b_" + nm, (128, D))
    inp("c_utc", (128, 128))
    inp("c_zrow", (1, D))
    inp("c_u32", (NT, NT))
    for nm in ("wqT", "wkT", "wvT", "woT"):
        inp("l1a_" + nm, (D, D))
    for nm in ("bq", "bk", "bv", "bo"):
        inp("l1a_" + nm, (128, D))
    inp("l1f_gwr", (128, 8192)); inp("l1f_gb", (128, 8)); inp("l1f_eb", (128, 8192))
    for e in range(8):
        inp(f"l1f_ew{e}", (D, D))
    inp("c_ident", (128, 128))
    inp("c_misc", (128, 16))
    inp("c_maskF", (128, 4 * 512))
    inp("c_maskW", (128, 8 * 512))
    for nm in ("c_csq", "c_snq", "c_csk", "c_snk"):
        inp(nm, (T, 512))
    out_h = nc.dram_tensor("out", [T, D], F32, kind="ExternalOutput").ap()
    scr("sA", (T, D)); scr("sB", (T, D))
    scr("ybA", (T, D)); scr("ybB", (T, D))
    scr("qTd", (D, T), BF16)
    scr("l1f_gsave", (T, 8))
    scr("obuf", (T, 512), BF16)
    scr("obuf2", (T, D), BF16)
    scr("vbos_d", (1, D), BF16)
    scr("hmT", (2816, T), BF16)

    with contextlib.ExitStack() as st:
        p = Prog(nc, st)
        E = st.enter_context
        ident = Buf(E(SB(nc, "ident", [128, 128], BF16)))
        cmisc = Buf(E(SB(nc, "cmisc", [128, 16], F32)))
        maskF = Buf(E(SB(nc, "maskF", [128, 2048], BF16)))
        maskW = Buf(E(SB(nc, "maskW", [128, 4096], F32)))
        identf = Buf(E(SB(nc, "identf", [128, 128], F32)))
        utc = Buf(E(SB(nc, "utc", [128, 128], BF16)))
        u32 = Buf(E(SB(nc, "u32", [NT, NT], BF16)))
        p.dma("G", cmisc, [], lambda e: e.dma_start(out=cmisc.h[:], in_=ap["c_misc"]))
        p.dma("G", identf, [], lambda e: e.dma_start(out=identf.h[:], in_=ap["c_ident"]))
        g1(p, "V", [identf], [ident], lambda v: v.tensor_copy(ident.h[:], identf.h[:]))
        with contextlib.ExitStack() as s2:
            mtmp = Buf(s2.enter_context(SB(nc, "mtmp", [128, 4096], F32)))
            p.dma("G", mtmp, [], lambda e: e.dma_start(out=mtmp.h[:, 0:2048], in_=ap["c_maskF"]))
            g1(p, "V", [mtmp], [maskF], lambda v: v.tensor_copy(maskF.h[:], mtmp.h[:, 0:2048]))
            p.dma("G", mtmp, [maskF], lambda e: e.dma_start(out=mtmp.h[:, 0:128], in_=ap["c_utc"]))
            g1(p, "V", [mtmp], [utc], lambda v: v.tensor_copy(utc.h[:], mtmp.h[:, 0:128]))
            p.dma("G", mtmp, [utc], lambda e: e.dma_start(out=mtmp.h[0:NT, 0:NT], in_=ap["c_u32"]))
            g1(p, "V", [mtmp], [u32], lambda v: v.tensor_copy(u32.h[:], mtmp.h[0:NT, 0:NT]))
            p.barrier()
        p.dma("G", maskW, [], lambda e: e.dma_start(out=maskW.h[:], in_=ap["c_maskW"]))

        ctx = dict(ident=ident, cmisc=cmisc, maskF=maskF, maskW=maskW, utc=utc, u32=u32)

        bufs = {"x": Buf(ap["x"]), "sA": Buf(ap["sA"]), "sB": Buf(ap["sB"]),
                "out": Buf(out_h)}
        chain = [ph for ph in PHASES if ph in ENABLED]
        cur = "x"
        nxt_cycle = ["sA", "sB"]
        for i, ph in enumerate(chain):
            tgt = "out" if i == len(chain) - 1 else nxt_cycle[i % 2]
            if ph == "l0a":
                phase_l0a(nc, p, st, ap, ctx, bufs[cur], bufs[tgt])
            elif ph in ("l0f0", "l0f1"):
                phase_l0f(nc, p, st, ap, ctx, bufs[cur], bufs[tgt])
            elif ph == "l0b":
                phase_l0b(nc, p, st, ap, ctx, bufs[cur], bufs[tgt])
            elif ph == "l1b":
                phase_l1b(nc, p, st, ap, ctx, bufs[cur], bufs[tgt])
            elif ph == "l1a":
                phase_l1a(nc, p, st, ap, ctx, bufs[cur], bufs[tgt])
            elif ph in ("l1f0", "l1f1"):
                phase_l1f(nc, p, st, ap, ctx, bufs[cur], bufs[tgt])
            else:
                raise NotImplementedError(ph)
            cur = tgt
        if cur != "out":
            finalize_copy(nc, p, st, bufs[cur], bufs["out"])
        p.barrier()
        blk = st.enter_context(nc.Block())
        p.flush(blk)
    return nc


def host_consts():
    c = {}
    c["c_ident"] = np.eye(128, dtype=np.float32)
    mc = np.zeros((128, 16), dtype=np.float32)
    mc[:, 0] = 1e-5; mc[:, 1] = -12.0; mc[:, 2] = 0.0; mc[:, 3] = 1.0
    mc[:, 4] = -8.0; mc[:, 5] = 1e-24
    c["c_misc"] = mc
    c["c_zrow"] = np.zeros((1, D), dtype=np.float32)
    mf = np.zeros((128, 4 * 512), dtype=np.float32)
    kl = np.arange(128)[:, None]
    qf = np.arange(512)[None, :]
    for j in range(4):
        mf[:, j * 512:(j + 1) * 512] = (kl + 128 * j <= qf).astype(np.float32)
    c["c_maskF"] = mf
    mw = np.zeros((128, 8 * 512), dtype=np.float32)
    il = np.arange(512)[None, :]
    for kt in range(8):
        jl = kt * 128 + np.arange(128)[:, None]
        mw[:, kt * 512:(kt + 1) * 512] = np.where((jl - 512 <= il) & (il <= jl), 0.0, -1e30).astype(np.float32)
    c["c_maskW"] = mw
    mk = np.full((128, 784), -1e30, dtype=np.float32)
    ql = np.arange(128)[:, None]
    j = np.arange(384)[None, :]
    a0 = (j[:, :256] <= ql) & (np.abs(j[:, :256] // 16 - ql // 16) <= 2)
    mk[:, 0:1] = 0.0
    mk[:, 1:257] = np.where(a0, 0.0, -1e30)
    a1 = (j <= 128 + ql) & (6 <= j // 16 - ql // 16) & (j // 16 - ql // 16 <= 10)
    mk[:, 392:393] = 0.0
    mk[:, 393:777] = np.where(a1, 0.0, -1e30)
    c["l0b_masks_const"] = mk
    sgrid = np.arange(128)
    c["c_utc"] = (sgrid[:, None] > sgrid[None, :]).astype(np.float32)
    tg = np.arange(NT)
    c["c_u32"] = (tg[:, None] > tg[None, :]).astype(np.float32)
    half = np.arange(0, 64, 2, dtype=np.float64)
    inv_freq = 1.0 / 10000.0 ** (half / 64.0)
    tt = np.arange(T, dtype=np.float64)
    freqs = tt[:, None] * inv_freq[None, :]
    freqs = np.concatenate([freqs, freqs], axis=-1)
    base = (half + 0.4 * 64.0) / (1.4 * 64.0)
    power = (tt - T // 2) / 256.0
    scale = base[None, :] ** power[:, None]
    scale = np.concatenate([scale, scale], axis=-1)
    cos, sin = np.cos(freqs), np.sin(freqs)
    c["c_csq"] = np.tile((cos * scale).astype(np.float32), (1, 8))
    c["c_snq"] = np.tile((sin * scale).astype(np.float32), (1, 8))
    c["c_csk"] = np.tile((cos / scale).astype(np.float32), (1, 8))
    c["c_snk"] = np.tile((sin / scale).astype(np.float32), (1, 8))
    return c


def stage_l0f(m, inputs):
    for li in range(2):
        m[f"l0f_wqkv{li}"] = np.ascontiguousarray(inputs["L0_ffn_wqkv"][li], np.float32)
        m[f"l0f_wo{li}"] = np.ascontiguousarray(inputs["L0_ffn_wo"][li], np.float32)
        m[f"l0f_w1{li}"] = np.ascontiguousarray(inputs["L0_ffn_w1"][li], np.float32)
        m[f"l0f_w2{li}"] = np.ascontiguousarray(inputs["L0_ffn_w2"][li], np.float32)
        m[f"l0f_qs{li}"] = np.ascontiguousarray(np.broadcast_to(
            np.tile(np.asarray(inputs["L0_ffn_q_scale"][li], np.float32), 8)[None, :], (128, 512)))
        m[f"l0f_ks{li}"] = np.ascontiguousarray(np.broadcast_to(
            np.tile(np.asarray(inputs["L0_ffn_k_scale"][li], np.float32), 8)[None, :], (128, 512)))
        m[f"l0f_lg{li}"] = np.ascontiguousarray(np.broadcast_to(
            np.asarray(inputs["L0_ffn_ln_g"][li], np.float32)[None, :], (128, D)))
        m[f"l0f_lb{li}"] = np.ascontiguousarray(np.broadcast_to(
            np.asarray(inputs["L0_ffn_ln_b"][li], np.float32)[None, :], (128, D)))
        m[f"l0f_b1{li}"] = np.ascontiguousarray(np.broadcast_to(
            np.asarray(inputs["L0_ffn_b1"][li], np.float32)[None, :], (128, 5460)))
        m[f"l0f_b2{li}"] = np.ascontiguousarray(np.broadcast_to(
            np.asarray(inputs["L0_ffn_b2"][li], np.float32)[None, :], (128, D)))


def kernel(**inputs):
    nc = build_nc()
    consts = host_consts()
    masks_const = consts.pop("l0b_masks_const")
    x_full = np.ascontiguousarray(inputs["L0_attn_a_x"], dtype=np.float32)
    in_maps = []
    for core in range(8):
        m = dict(consts)
        m["x"] = np.ascontiguousarray(x_full[core])
        m["l0a_wq"] = np.ascontiguousarray(inputs["L0_attn_a_w_q"], np.float32)
        m["l0a_wk"] = np.ascontiguousarray(inputs["L0_attn_a_w_k"], np.float32)
        m["l0a_wv"] = np.ascontiguousarray(inputs["L0_attn_a_w_v"], np.float32)
        stage_l0f(m, inputs)
        wqkv_b = np.asarray(inputs["L0_attn_b_w_qkv"], np.float32)
        m["l0b_wq"] = np.ascontiguousarray(wqkv_b[:, 0:1024])
        m["l0b_wk"] = np.ascontiguousarray(wqkv_b[:, 1024:2048])
        m["l0b_wv"] = np.ascontiguousarray(wqkv_b[:, 2048:3072])
        m["l0b_wout"] = np.ascontiguousarray(inputs["L0_attn_b_w_out"], np.float32)
        m["l0b_bout"] = np.ascontiguousarray(np.broadcast_to(
            np.asarray(inputs["L0_attn_b_b_out"], np.float32)[None, :], (128, D)))
        m["l0b_wthr"] = np.ascontiguousarray(np.broadcast_to(
            np.asarray(inputs["L0_attn_b_w_th"], np.float32).reshape(1, 256), (128, 256)))
        m["l0b_masks"] = masks_const
        m["l1b_wk"] = np.ascontiguousarray(np.asarray(
            inputs["L1_attn_b_W_key"], np.float32).transpose(1, 0, 2).reshape(D, D))
        m["l1b_wq"] = np.ascontiguousarray(np.asarray(
            inputs["L1_attn_b_W_query"], np.float32).transpose(1, 0, 2).reshape(D, D))
        m["l1b_wv"] = np.ascontiguousarray(np.asarray(
            inputs["L1_attn_b_W_values"], np.float32).transpose(1, 0, 2).reshape(D, D))
        m["l1b_wout"] = np.ascontiguousarray(np.asarray(
            inputs["L1_attn_b_W_output"], np.float32).reshape(D, D))
        for nm, key in (("bk", "L1_attn_b_Bias_key"), ("bq", "L1_attn_b_Bias_query"),
                        ("bv", "L1_attn_b_Bias_values")):
            m["l1b_" + nm] = np.ascontiguousarray(np.broadcast_to(
                np.asarray(inputs[key], np.float32).reshape(1, D), (128, D)))
        m["l1b_bout"] = np.ascontiguousarray(np.broadcast_to(
            np.asarray(inputs["L1_attn_b_Bias_output"], np.float32)[None, :], (128, D)))
        for nm, key in (("wqT", "L1_attn_a_Wq"), ("wkT", "L1_attn_a_Wk"),
                        ("wvT", "L1_attn_a_Wv"), ("woT", "L1_attn_a_Wo")):
            m["l1a_" + nm] = np.ascontiguousarray(np.asarray(inputs[key], np.float32).T)
        for nm, key in (("bq", "L1_attn_a_bq"), ("bk", "L1_attn_a_bk"),
                        ("bv", "L1_attn_a_bv"), ("bo", "L1_attn_a_bo")):
            m["l1a_" + nm] = np.ascontiguousarray(np.broadcast_to(
                np.asarray(inputs[key], np.float32)[None, :], (128, D)))
        m["l1f_gwr"] = np.ascontiguousarray(np.broadcast_to(np.asarray(
            inputs["L1_ffn_gate_w"], np.float32).T.reshape(1, 8192), (128, 8192)))
        m["l1f_gb"] = np.ascontiguousarray(np.broadcast_to(
            np.asarray(inputs["L1_ffn_gate_b"], np.float32)[None, :], (128, 8)))
        m["l1f_eb"] = np.ascontiguousarray(np.broadcast_to(
            np.asarray(inputs["L1_ffn_expert_b"], np.float32).reshape(1, 8192), (128, 8192)))
        for e in range(8):
            m[f"l1f_ew{e}"] = np.ascontiguousarray(inputs["L1_ffn_expert_w"][e], np.float32)
        in_maps.append(m)
    res = run_bass_kernel_spmd(nc, in_maps, core_ids=list(range(8)))
    out = np.stack([r["out"] for r in res.results], axis=0)
    return out.astype(np.float32)
